# version 05
# baseline (speedup 1.0000x reference)
import math
import contextlib
import numpy as np
import ml_dtypes
import concourse.bass as bass
import concourse.mybir as mybir
from concourse.bass_utils import run_bass_kernel_spmd

F32 = mybir.dt.float32
BF16 = mybir.dt.bfloat16
AF = mybir.ActivationFunctionType
ALU = mybir.AluOpType
AX = mybir.AxisListType

D = 2048
TOK = 2048
TFULL = 4096
NH = 16
INTER = 8192
S128 = 1.0 / math.sqrt(128.0)
C1 = 1.0 / math.sqrt(2048.0)
CDP = 1.0 / math.sqrt(8192.0)

CI_S128, CI_EPS = 0, 1
NC_CONST = 4

SEMNAMES = ["bar", "ld", "st", "gw", "gst", "xl", "wl",
            "pe1", "pe2", "pe3", "pe4",
            "ve1", "ve2", "ve3", "ve4", "ve5",
            "ac1", "ac2", "ac3"] + ["q%d" % i for i in range(12)]


def _hadamard(n):
    h = np.array([[1.0]])
    while h.shape[0] < n:
        h = np.block([[h, h], [h, -h]])
    return h


class Ctr:
    def __init__(self):
        self.c = {}

    def inc(self, instr, sem, name, k=1):
        instr.then_inc(sem, k)
        self.c[name] = self.c.get(name, 0) + k
        return self.c[name]

    def get(self, name):
        return self.c.get(name, 0)

    def set(self, name, v):
        self.c[name] = v


def build_nc():
    nc = bass.Bass("TRN2", target_bir_lowering=False)

    x_in = nc.dram_tensor("x_in", [TFULL, D], F32, kind="ExternalInput")
    wqkv_in = nc.dram_tensor("wqkv_in", [48, 128, 16, 128], F32, kind="ExternalInput")
    wo_in = nc.dram_tensor("wo_in", [D, D], F32, kind="ExternalInput")
    wup_in = nc.dram_tensor("wup_in", [64, 128, 16, 128], F32, kind="ExternalInput")
    wgate_in = nc.dram_tensor("wgate_in", [64, 128, 16, 128], F32, kind="ExternalInput")
    wdownT_in = nc.dram_tensor("wdownT_in", [INTER, D], F32, kind="ExternalInput")
    su_ug_in = nc.dram_tensor("su_ug_in", [1, D], F32, kind="ExternalInput")
    su_dp_in = nc.dram_tensor("su_dp_in", [1, INTER], F32, kind="ExternalInput")
    wsc_ug_in = nc.dram_tensor("wsc_ug_in", [1, 2 * INTER], F32, kind="ExternalInput")
    wsc_dp_in = nc.dram_tensor("wsc_dp_in", [1, D], F32, kind="ExternalInput")
    h2048_in = nc.dram_tensor("h2048_in", [16, 128, 16, 128], BF16, kind="ExternalInput")
    h128_in = nc.dram_tensor("h128_in", [128, 128], BF16, kind="ExternalInput")
    h64_in = nc.dram_tensor("h64_in", [64, 64], BF16, kind="ExternalInput")
    ident_in = nc.dram_tensor("ident_in", [128, 128], BF16, kind="ExternalInput")
    identf_in = nc.dram_tensor("identf_in", [128, 128], F32, kind="ExternalInput")
    ones_in = nc.dram_tensor("ones_in", [1, 128], F32, kind="ExternalInput")
    consts_in = nc.dram_tensor("consts_in", [1, NC_CONST], F32, kind="ExternalInput")

    out = nc.dram_tensor("out", [TOK, D], F32, kind="ExternalOutput")

    q_sc = nc.dram_tensor("q_sc", [D, TFULL], BF16, kind="Internal")
    k_sc = nc.dram_tensor("k_sc", [D, TFULL], BF16, kind="Internal")
    v_sc = nc.dram_tensor("v_sc", [D, TOK], BF16, kind="Internal")
    x1_sc = nc.dram_tensor("x1_sc", [TOK, D], F32, kind="Internal")
    wup_sc = nc.dram_tensor("wup_sc", [64, 128, 2048], BF16, kind="Internal")
    wgate_sc = nc.dram_tensor("wgate_sc", [64, 128, 2048], BF16, kind="Internal")
    w2pre_sc = nc.dram_tensor("w2pre_sc", [INTER, D], BF16, kind="Internal")
    w2t_sc = nc.dram_tensor("w2t_sc", [INTER, D], BF16, kind="Internal")

    def bcast_row(dram_t, n, col0=0):
        a = dram_t.ap()
        return bass.AP(a.tensor, a.offset + col0, [[0, 128], [1, n]])

    def strided_col(dram_t, p, cols, col0=0):
        a = dram_t.ap()
        return bass.AP(a.tensor, a.offset + col0, [[1, p], [p, cols]])

    cnt = Ctr()

    with contextlib.ExitStack() as stack:
        sems = {sn: stack.enter_context(nc.semaphore(sn)) for sn in SEMNAMES}
        S = sems

        cst = stack.enter_context(nc.sbuf_tensor("cst", [128, NC_CONST], F32))
        onesf = stack.enter_context(nc.sbuf_tensor("onesf", [1, 128], F32))
        idn = stack.enter_context(nc.sbuf_tensor("idn", [128, 128], BF16))
        idnf = stack.enter_context(nc.sbuf_tensor("idnf", [128, 128], F32))
        flt = stack.enter_context(nc.sbuf_tensor("flt", [128, 1], F32))
        h128 = stack.enter_context(nc.sbuf_tensor("h128", [128, 128], BF16))
        h64 = stack.enter_context(nc.sbuf_tensor("h64", [64, 64], BF16))
        su_sb = stack.enter_context(nc.sbuf_tensor("su_sb", [128, 16], F32))
        scup = stack.enter_context(nc.sbuf_tensor("scup", [128, 64], F32))
        scgt = stack.enter_context(nc.sbuf_tensor("scgt", [128, 64], F32))

        phase_no = [0]

        def phase_wait(eng):
            if phase_no[0] > 0:
                eng.wait_ge(S["bar"], 5 * phase_no[0])

        def phase_end(eng):
            eng.sem_inc(S["bar"], 1)

        def close_phase():
            phase_no[0] += 1

        def bases(*names):
            return tuple(cnt.get(n) for n in names)

        def phase_setup():
            b_ld, b_st, b_pe1, b_pe2, b_ve1, b_ve2, b_ac1, b_ac2 = bases(
                "ld", "st", "pe1", "pe2", "ve1", "ve2", "ac1", "ac2")
            bq = [cnt.get("q%d" % _i) for _i in range(12)]
            with contextlib.ExitStack() as _es:
                t_ug = _es.enter_context(nc.sbuf_tensor("t_ug", [128, 64], F32))
                t_dp = _es.enter_context(nc.sbuf_tensor("t_dp", [128, 64], F32))
                bcdp = _es.enter_context(nc.sbuf_tensor("bcdp", [128, D], F32))
                sdp_row = _es.enter_context(nc.sbuf_tensor("sdp_row", [1, D], F32))
                wdf0 = _es.enter_context(nc.sbuf_tensor("wdf0", [128, D], F32))
                wdf1 = _es.enter_context(nc.sbuf_tensor("wdf1", [128, D], F32))
                wdb0 = _es.enter_context(nc.sbuf_tensor("wdb0", [128, D], BF16))
                wdb1 = _es.enter_context(nc.sbuf_tensor("wdb1", [128, D], BF16))
                wcb0 = _es.enter_context(nc.sbuf_tensor("wcb0", [64, 1024], BF16))
                wcb1 = _es.enter_context(nc.sbuf_tensor("wcb1", [64, 1024], BF16))
                wob0 = _es.enter_context(nc.sbuf_tensor("wob0", [64, 1024], BF16))
                wob1 = _es.enter_context(nc.sbuf_tensor("wob1", [64, 1024], BF16))
                ps_bc = _es.enter_context(nc.psum_tensor("ps_bc", [128, 512], F32))
                ps_h0 = _es.enter_context(nc.psum_tensor("ps_h0", [64, 1024], F32))
                ps_h1 = _es.enter_context(nc.psum_tensor("ps_h1", [64, 1024], F32))
                blk = _es.enter_context(nc.Block("setup"))
                wdf = [wdf0, wdf1]
                wdb = [wdb0, wdb1]
                wcb = [wcb0, wcb1]
                wob = [wob0, wob1]
                psh = [ps_h0, ps_h1]

                @blk.sync
                def _(s):
                    phase_wait(s)
                    for dst, src in (
                        (cst[:], bcast_row(consts_in, NC_CONST)),
                        (onesf[:], ones_in.ap()),
                        (idn[:], ident_in.ap()),
                        (idnf[:], identf_in.ap()),
                        (h128[:], h128_in.ap()),
                        (h64[:], h64_in.ap()),
                    ):
                        s.dma_start(dst, src).then_inc(S["ld"], 16)
                    s.dma_start(su_sb[:], strided_col(su_ug_in, 128, 16),
                                allow_slow_non_contiguous=True).then_inc(S["ld"], 16)
                    s.dma_start(t_ug[:], strided_col(wsc_ug_in, 128, 64),
                                allow_slow_non_contiguous=True).then_inc(S["ld"], 16)
                    s.dma_start(t_dp[:], strided_col(su_dp_in, 128, 64),
                                allow_slow_non_contiguous=True).then_inc(S["ld"], 16)
                    s.dma_start(sdp_row[:], wsc_dp_in.ap()).then_inc(S["ld"], 16)
                    s.dma_start(scgt[:], strided_col(wsc_ug_in, 128, 64, col0=INTER),
                                allow_slow_non_contiguous=True).then_inc(S["ld"], 16)
                    for i in range(64):
                        if i >= 2:
                            s.wait_ge(S["ve2"], b_ve2 + (i - 1))
                        s.dma_start(wdf[i % 2][:], wdownT_in.ap()[i * 128:(i + 1) * 128, :]
                                    ).then_inc(S["q%d" % (i % 2)], 16)
                        if i >= 1:
                            s.wait_ge(S["ve2"], b_ve2 + i)
                            s.dma_start(w2pre_sc.ap()[(i - 1) * 128:i * 128, :],
                                        wdb[(i - 1) % 2][:]).then_inc(S["q%d" % (2 + (i - 1) % 2)], 16)
                    s.wait_ge(S["ve2"], b_ve2 + 64)
                    s.dma_start(w2pre_sc.ap()[63 * 128:, :], wdb[1][:]).then_inc(S["q3"], 16)
                    s.wait_ge(S["q2"], bq[2] + 32 * 16)
                    s.wait_ge(S["q3"], bq[3] + 32 * 16)
                    a = w2pre_sc.ap()
                    b = w2t_sc.ap()
                    for j in range(256):
                        r, odh = divmod(j, 2)
                        src = bass.AP(a.tensor, a.offset + r * 2048 + odh * 1024,
                                      [[128 * 2048, 64], [1, 1024]])
                        if j >= 2:
                            s.wait_ge(S["pe2"], b_pe2 + (j - 1))
                        s.dma_start(wcb[j % 2][:], src).then_inc(S["q%d" % (4 + j % 2)], 16)
                        if j >= 1:
                            ro, odho = divmod(j - 1, 2)
                            s.wait_ge(S["ac2"], b_ac2 + j)
                            dst = bass.AP(b.tensor, b.offset + ro * 2048 + odho * 1024,
                                          [[128 * 2048, 64], [1, 1024]])
                            s.dma_start(dst, wob[(j - 1) % 2][:]).then_inc(S["q%d" % (6 + (j - 1) % 2)], 16)
                    s.wait_ge(S["ac2"], b_ac2 + 256)
                    dst = bass.AP(b.tensor, b.offset + 127 * 2048 + 1024,
                                  [[128 * 2048, 64], [1, 1024]])
                    s.dma_start(dst, wob[1][:]).then_inc(S["q7"], 16)
                    s.wait_ge(S["q6"], bq[6] + 128 * 16)
                    s.wait_ge(S["q7"], bq[7] + 128 * 16)
                    phase_end(s)
                cnt.set("ld", b_ld + 16 * 11)
                for _i, _v in ((0, 32), (1, 32), (2, 32), (3, 32), (4, 128), (5, 128), (6, 128), (7, 128)):
                    cnt.set("q%d" % _i, bq[_i] + _v * 16)

                @blk.vector
                def _(v):
                    phase_wait(v)
                    v.wait_ge(S["ld"], b_ld + 176)
                    v.tensor_tensor(scup[:], t_ug[:], t_dp[:], ALU.mult).then_inc(S["ve1"], 1)
                    v.wait_ge(S["ve1"], b_ve1 + 1)
                    v.tensor_scalar(scup[:], scup[:], C1, None, ALU.mult)
                    v.tensor_scalar(scgt[:], scgt[:], C1, None, ALU.mult)
                    v.tensor_scalar(sdp_row[:], sdp_row[:], CDP, None, ALU.mult
                                    ).then_inc(S["ve1"], 1)
                    for i in range(64):
                        v.wait_ge(S["q%d" % (i % 2)], bq[i % 2] + 16 * (i // 2 + 1))
                        if i == 0:
                            v.wait_ge(S["ac1"], b_ac1 + 4)
                        if i >= 2:
                            v.wait_ge(S["q%d" % (2 + i % 2)], bq[2 + i % 2] + 16 * (i // 2))
                        v.tensor_tensor(wdb[i % 2][:], wdf[i % 2][:], bcdp[:], ALU.mult)
                        v.memset(flt[:], 0.0).then_inc(S["ve2"], 1)
                    phase_end(v)
                cnt.set("ve1", b_ve1 + 2)
                cnt.set("ve2", b_ve2 + 64)

                @blk.tensor
                def _(t):
                    phase_wait(t)
                    t.wait_ge(S["ve1"], b_ve1 + 2)
                    for kk in range(4):
                        if kk >= 1:
                            t.wait_ge(S["ac1"], b_ac1 + kk)
                        t.matmul(ps_bc[:], onesf[:], sdp_row[:, kk * 512:(kk + 1) * 512],
                                 start=True, stop=True).then_inc(S["pe1"], 1)
                    for j in range(256):
                        t.wait_ge(S["q%d" % (4 + j % 2)], bq[4 + j % 2] + 16 * (j // 2 + 1))
                        if j >= 2:
                            t.wait_ge(S["ac2"], b_ac2 + (j - 1))
                        for nn in range(2):
                            mm = t.matmul(psh[j % 2][:, nn * 512:(nn + 1) * 512], h64[:],
                                          wcb[j % 2][:, nn * 512:(nn + 1) * 512],
                                          start=True, stop=True)
                            if nn == 1:
                                mm.then_inc(S["pe2"], 1)
                    phase_end(t)
                cnt.set("pe1", b_pe1 + 4)
                cnt.set("pe2", b_pe2 + 256)

                @blk.scalar
                def _(sc):
                    phase_wait(sc)
                    for kk in range(4):
                        sc.wait_ge(S["pe1"], b_pe1 + kk + 1)
                        sc.copy(bcdp[:, kk * 512:(kk + 1) * 512], ps_bc[:]).then_inc(S["ac1"], 1)
                    for j in range(256):
                        sc.wait_ge(S["pe2"], b_pe2 + j + 1)
                        if j >= 2:
                            sc.wait_ge(S["q%d" % (6 + j % 2)], bq[6 + j % 2] + 16 * (j // 2))
                        sc.copy(wob[j % 2][:], psh[j % 2][:]).then_inc(S["ac2"], 1)
                    phase_end(sc)
                cnt.set("ac1", b_ac1 + 4)
                cnt.set("ac2", b_ac2 + 256)

                @blk.gpsimd
                def _(g):
                    phase_wait(g)
                    phase_end(g)
            close_phase()

        def phase_A():
            (b_xl, b_st, b_gw, b_gst, b_wl, b_pe1, b_pe2,
             b_ve1, b_ve2, b_ve3, b_ac1, b_ac2) = bases(
                "xl", "st", "gw", "gst", "wl", "pe1", "pe2",
                "ve1", "ve2", "ve3", "ac1", "ac2")
            bq = [cnt.get("q%d" % _i) for _i in range(12)]
            with contextlib.ExitStack() as _es:
                xt0 = _es.enter_context(nc.sbuf_tensor("xt0", [128, D], F32))
                xt1 = _es.enter_context(nc.sbuf_tensor("xt1", [128, D], F32))
                sqt = _es.enter_context(nc.sbuf_tensor("sqt", [128, D], F32))
                xn0 = _es.enter_context(nc.sbuf_tensor("xn0", [128, D], BF16))
                xn1 = _es.enter_context(nc.sbuf_tensor("xn1", [128, D], BF16))
                xnfm = _es.enter_context(nc.sbuf_tensor("xnfm", [128, 16 * 2048], BF16))
                wqb0 = _es.enter_context(nc.sbuf_tensor("wqb0", [128, 2048], BF16))
                wqb1 = _es.enter_context(nc.sbuf_tensor("wqb1", [128, 2048], BF16))
                qkt0 = _es.enter_context(nc.sbuf_tensor("qkt0", [128, 512], BF16))
                qkt1 = _es.enter_context(nc.sbuf_tensor("qkt1", [128, 512], BF16))
                qkt2 = _es.enter_context(nc.sbuf_tensor("qkt2", [128, 512], BF16))
                qkt3 = _es.enter_context(nc.sbuf_tensor("qkt3", [128, 512], BF16))
                st1a = _es.enter_context(nc.sbuf_tensor("st1a", [128, 1], F32))
                st1b = _es.enter_context(nc.sbuf_tensor("st1b", [128, 1], F32))
                st2a = _es.enter_context(nc.sbuf_tensor("st2a", [128, 1], F32))
                st2b = _es.enter_context(nc.sbuf_tensor("st2b", [128, 1], F32))
                mva = _es.enter_context(nc.sbuf_tensor("mva", [128, 4], F32))
                mvb = _es.enter_context(nc.sbuf_tensor("mvb", [128, 4], F32))
                wpc0 = _es.enter_context(nc.sbuf_tensor("wpc0", [128, 2048], BF16))
                wpc1 = _es.enter_context(nc.sbuf_tensor("wpc1", [128, 2048], BF16))
                pq0 = _es.enter_context(nc.psum_tensor("pq0", [128, 512], F32))
                pq1 = _es.enter_context(nc.psum_tensor("pq1", [128, 512], F32))
                pt0 = _es.enter_context(nc.psum_tensor("pt0", [128, 128], BF16))
                pt1 = _es.enter_context(nc.psum_tensor("pt1", [128, 128], BF16))
                blk = _es.enter_context(nc.Block("phA"))
                xt = [xt0, xt1]
                xn = [xn0, xn1]
                wqb = [wqb0, wqb1]
                qkt = [qkt0, qkt1, qkt2, qkt3]
                stat1 = [st1a, st1b]
                stat2 = [st2a, st2b]
                mv = [mva, mvb]
                pq = [pq0, pq1]
                pt = [pt0, pt1]
                wpc = [wpc0, wpc1]

                NT = 16
                CT = (48, 32)
                E0 = (0, 192)
                NE = 320

                def qdst(ctg, tc):
                    tb = 0 if ctg < 48 else 1
                    ct = ctg if ctg < 48 else ctg - 48
                    if ct < 16:
                        return q_sc.ap()[ct * 128:(ct + 1) * 128,
                                         tb * 2048 + tc * 512: tb * 2048 + (tc + 1) * 512]
                    if ct < 32:
                        return k_sc.ap()[(ct - 16) * 128:(ct - 15) * 128,
                                         tb * 2048 + tc * 512: tb * 2048 + (tc + 1) * 512]
                    return v_sc.ap()[(ct - 32) * 128:(ct - 31) * 128,
                                     tc * 512:(tc + 1) * 512]

                @blk.sync
                def _(s):
                    phase_wait(s)
                    for tb in range(2):
                        for tt in range(NT):
                            u = tb * NT + tt
                            if u >= 2:
                                s.wait_ge(S["ve1"], b_ve1 + 8 * (u - 1))
                            s.dma_start(xt[u % 2][:], x_in.ap()[u * 128:(u + 1) * 128, :]
                                        ).then_inc(S["q%d" % (u % 2)], 16)
                        for ei in range(CT[tb] * 4):
                            e = E0[tb] + ei
                            s.wait_ge(S["ve2"], b_ve2 + e + 1)
                            ctg, tc = divmod(e, 4)
                            s.dma_start(qdst(ctg, tc), qkt[e % 4][:]).then_inc(S["q%d" % (4 + e % 4)], 16)
                    for _k in range(4):
                        s.wait_ge(S["q%d" % (4 + _k)], bq[4 + _k] + 80 * 16)
                    phase_end(s)
                cnt.set("q0", bq[0] + 16 * 16)
                cnt.set("q1", bq[1] + 16 * 16)
                for _k in range(4):
                    cnt.set("q%d" % (4 + _k), bq[4 + _k] + 80 * 16)

                @blk.gpsimd
                def _(g):
                    phase_wait(g)
                    pj = 0
                    for ctg in range(80):
                        if ctg >= 2:
                            g.wait_ge(S["pe2"], b_pe2 + 4 * (ctg - 1))
                        g.dma_start(wqb[ctg % 2][:],
                                    wqkv_in.ap()[ctg if ctg < 48 else ctg - 48, :, :, :]
                                    .rearrange("p a b -> p (a b)")).then_inc(S["q%d" % (2 + ctg % 2)], 16)
                        for _ in range(2):
                            if pj < 128:
                                win = wup_in if pj < 64 else wgate_in
                                wsc = wup_sc if pj < 64 else wgate_sc
                                c = pj % 64
                                if pj >= 2:
                                    g.wait_ge(S["gst"], b_gst + 16 * (pj - 1))
                                g.dma_start(wpc[pj % 2][:],
                                            win.ap()[c, :, :, :].rearrange("p a b -> p (a b)")
                                            ).then_inc(S["q%d" % (8 + pj % 2)], 16)
                                if pj >= 1:
                                    g.wait_ge(S["q%d" % (8 + (pj - 1) % 2)],
                                              bq[8 + (pj - 1) % 2] + 16 * ((pj - 1) // 2 + 1))
                                    wino = wup_sc if pj - 1 < 64 else wgate_sc
                                    g.dma_start(wino.ap()[(pj - 1) % 64, :, :],
                                                wpc[(pj - 1) % 2][:]).then_inc(S["gst"], 16)
                                pj += 1
                    g.wait_ge(S["q%d" % (8 + 127 % 2)], bq[8 + 127 % 2] + 16 * (127 // 2 + 1))
                    g.dma_start(wgate_sc.ap()[63, :, :], wpc[127 % 2][:]).then_inc(S["gst"], 16)
                    g.wait_ge(S["gst"], b_gst + 128 * 16)
                    phase_end(g)
                cnt.set("q2", bq[2] + 40 * 16)
                cnt.set("q3", bq[3] + 40 * 16)
                cnt.set("q8", bq[8] + 64 * 16)
                cnt.set("q9", bq[9] + 64 * 16)
                cnt.set("gst", b_gst + 128 * 16)

                def ln_chain(v, u):
                    par = u % 2
                    v.wait_ge(S["q%d" % (u % 2)], bq[u % 2] + 16 * (u // 2 + 1))
                    v.reduce_sum(stat1[par][:], xt[par][:], axis=AX.X).then_inc(S["ve1"], 1)
                    v.wait_ge(S["ac1"], b_ac1 + 2 * u + 1)
                    v.wait_ge(S["ve1"], b_ve1 + 8 * u + 1)
                    m = mv[par][:, 0:1]
                    var = mv[par][:, 1:2]
                    rstd = mv[par][:, 2:3]
                    negmr = mv[par][:, 3:4]
                    v.tensor_scalar(m, stat1[par][:], 1.0 / D, None, ALU.mult).then_inc(S["ve1"], 1)
                    v.tensor_scalar(var, stat2[par][:], 1.0 / D, None, ALU.mult).then_inc(S["ve1"], 1)
                    v.wait_ge(S["ve1"], b_ve1 + 8 * u + 3)
                    v.tensor_tensor(rstd, m, m, ALU.mult).then_inc(S["ve1"], 1)
                    v.wait_ge(S["ve1"], b_ve1 + 8 * u + 4)
                    v.tensor_tensor(var, var, rstd, ALU.subtract).then_inc(S["ve1"], 1)
                    v.wait_ge(S["ac1"], b_ac1 + 2 * u + 2)
                    v.reciprocal(rstd, rstd).then_inc(S["ve1"], 1)
                    v.wait_ge(S["ve1"], b_ve1 + 8 * u + 6)
                    v.scalar_tensor_tensor(negmr, m, -1.0, rstd, ALU.mult, ALU.mult
                                           ).then_inc(S["ve1"], 1)
                    v.wait_ge(S["ve1"], b_ve1 + 8 * u + 7)
                    if u >= 2:
                        v.wait_ge(S["pe1"], b_pe1 + 16 * (u - 1))
                    v.tensor_scalar(xn[par][:], xt[par][:], m, rstd,
                                    ALU.subtract, ALU.mult).then_inc(S["ve1"], 1)

                @blk.vector
                def _(v):
                    phase_wait(v)
                    for tb in range(2):
                        ln_chain(v, tb * NT)
                        for tt in range(NT):
                            u = tb * NT + tt
                            if tt + 1 < NT:
                                ln_chain(v, u + 1)
                            for k in range(16):
                                g_idx = 16 * u + k
                                v.wait_ge(S["pe1"], b_pe1 + g_idx + 1)
                                v.tensor_copy(
                                    xnfm[:, k * 2048 + tt * 128: k * 2048 + (tt + 1) * 128],
                                    pt[k % 2][:])
                                v.memset(flt[:], 0.0).then_inc(S["ve3"], 1)
                        for ei in range(CT[tb] * 4):
                            e = E0[tb] + ei
                            v.wait_ge(S["pe2"], b_pe2 + e + 1)
                            if e >= 4:
                                v.wait_ge(S["q%d" % (4 + e % 4)], bq[4 + e % 4] + 16 * (e // 4))
                            v.tensor_copy(qkt[e % 4][:], pq[e % 2][:])
                            v.memset(flt[:], 0.0).then_inc(S["ve2"], 1)
                    phase_end(v)
                cnt.set("ve1", b_ve1 + 8 * 32)
                cnt.set("ve2", b_ve2 + NE)
                cnt.set("ve3", b_ve3 + 512)

                @blk.scalar
                def _(sc):
                    phase_wait(sc)
                    for tb in range(2):
                        for tt in range(NT):
                            u = tb * NT + tt
                            par = u % 2
                            sc.wait_ge(S["q%d" % (u % 2)], bq[u % 2] + 16 * (u // 2 + 1))
                            sc.activation(sqt[:], xt[par][:], AF.Square,
                                          accum_out=stat2[par][:]).then_inc(S["ac1"], 1)
                            sc.wait_ge(S["ve1"], b_ve1 + 8 * u + 5)
                            sc.activation(mv[par][:, 2:3], mv[par][:, 1:2], AF.Sqrt,
                                          bias=cst[:, CI_EPS:CI_EPS + 1]).then_inc(S["ac1"], 1)
                    phase_end(sc)
                cnt.set("ac1", b_ac1 + 64)

                @blk.tensor
                def _(t):
                    phase_wait(t)
                    for tb in range(2):
                        for tt in range(NT):
                            u = tb * NT + tt
                            t.wait_ge(S["ve1"], b_ve1 + 8 * u + 8)
                            for k in range(16):
                                g_idx = 16 * u + k
                                if g_idx >= 2:
                                    t.wait_ge(S["ve3"], b_ve3 + g_idx - 1)
                                t.transpose(pt[k % 2][:], xn[u % 2][:, k * 128:(k + 1) * 128],
                                            idn[:]).then_inc(S["pe1"], 1)
                        t.wait_ge(S["ve3"], b_ve3 + 256 * (tb + 1))
                        for ei in range(CT[tb] * 4):
                            e = E0[tb] + ei
                            ctg, tc = divmod(e, 4)
                            t.wait_ge(S["q%d" % (2 + ctg % 2)], bq[2 + ctg % 2] + 16 * (ctg // 2 + 1))
                            if e >= 2:
                                t.wait_ge(S["ve2"], b_ve2 + e - 1)
                            for fc in range(16):
                                t.matmul(pq[e % 2][:],
                                         wqb[ctg % 2][:, fc * 128:(fc + 1) * 128],
                                         xnfm[:, fc * 2048 + tc * 512: fc * 2048 + (tc + 1) * 512],
                                         start=(fc == 0), stop=(fc == 15)
                                         ).then_inc(S["pe2"], 1) if fc == 15 else \
                                    t.matmul(pq[e % 2][:],
                                             wqb[ctg % 2][:, fc * 128:(fc + 1) * 128],
                                             xnfm[:, fc * 2048 + tc * 512: fc * 2048 + (tc + 1) * 512],
                                             start=(fc == 0), stop=False)
                    phase_end(t)
                cnt.set("pe1", b_pe1 + 512)
                cnt.set("pe2", b_pe2 + NE)
            close_phase()

        def phase_BC():
            (b_xl, b_gw, b_pe1, b_pe2, b_pe3,
             b_ve1, b_ve2, b_ac1, b_ac2) = bases(
                "xl", "gw", "pe1", "pe2", "pe3", "ve1", "ve2", "ac1", "ac2")
            with contextlib.ExitStack() as _es:
                vals = _es.enter_context(nc.sbuf_tensor("vals", [128, NH * 2048], BF16))
                wo_sb = _es.enter_context(nc.sbuf_tensor("wo_sb", [128, 16 * 2048], BF16))
                csr = _es.enter_context(nc.sbuf_tensor("csr", [1, TOK], F32))
                phase_B_inner(vals, wo_sb, csr,
                              b_xl, b_gw, b_pe1, b_pe2, b_pe3,
                              b_ve1, b_ve2, b_ac1, b_ac2)
                close_phase()
                phase_C_inner(vals, wo_sb)
                close_phase()

        def phase_B_inner(vals, wo_sb, csr,
                          b_xl, b_gw, b_pe1, b_pe2, b_pe3,
                          b_ve1, b_ve2, b_ac1, b_ac2):
            bq = [cnt.get("q%d" % _i) for _i in range(12)]
            with contextlib.ExitStack() as _es:
                qh0 = _es.enter_context(nc.sbuf_tensor("qh0", [128, TFULL], BF16))
                qh1 = _es.enter_context(nc.sbuf_tensor("qh1", [128, TFULL], BF16))
                kh0 = _es.enter_context(nc.sbuf_tensor("kh0", [128, TFULL], BF16))
                kh1 = _es.enter_context(nc.sbuf_tensor("kh1", [128, TFULL], BF16))
                vh0 = _es.enter_context(nc.sbuf_tensor("vh0", [128, TOK], BF16))
                vh1 = _es.enter_context(nc.sbuf_tensor("vh1", [128, TOK], BF16))
                un0 = _es.enter_context(nc.sbuf_tensor("un0", [128, TOK], BF16))
                un1 = _es.enter_context(nc.sbuf_tensor("un1", [128, TOK], BF16))
                utmp = _es.enter_context(nc.sbuf_tensor("utmp", [128, TOK], BF16))
                lo = _es.enter_context(nc.sbuf_tensor("lo", [128, 8], F32))
                lw = _es.enter_context(nc.sbuf_tensor("lw", [128, 8], F32))
                lwb = _es.enter_context(nc.sbuf_tensor("lwb", [128, 2], BF16))
                psA0 = _es.enter_context(nc.psum_tensor("psA0", [128, 1024], F32))
                psA1 = _es.enter_context(nc.psum_tensor("psA1", [128, 1024], F32))
                csps = _es.enter_context(nc.psum_tensor("csps", [1, 2048], F32))
                blk = _es.enter_context(nc.Block("phB"))
                qh = [qh0, qh1]
                kh = [kh0, kh1]
                vh = [vh0, vh1]
                un = [un0, un1]
                psA = [psA0, psA1]

                @blk.sync
                def _(s):
                    phase_wait(s)
                    for h in range(NH):
                        if h >= 2:
                            s.wait_ge(S["pe2"], b_pe2 + 128 * (h - 1))
                            s.wait_ge(S["ve2"], b_ve2 + 2 * (h - 1))
                        s.dma_start(qh[h % 2][:], q_sc.ap()[h * 128:(h + 1) * 128, :]
                                    ).then_inc(S["q%d" % (h % 2)], 16)
                        s.dma_start(kh[h % 2][:], k_sc.ap()[h * 128:(h + 1) * 128, :]
                                    ).then_inc(S["q%d" % (h % 2)], 16)
                        s.dma_start(vh[h % 2][:], v_sc.ap()[h * 128:(h + 1) * 128, :]
                                    ).then_inc(S["q%d" % (h % 2)], 16)
                    phase_end(s)
                cnt.set("q0", bq[0] + 8 * 48)
                cnt.set("q1", bq[1] + 8 * 48)

                @blk.gpsimd
                def _(g):
                    phase_wait(g)
                    for hc in range(16):
                        g.dma_start(wo_sb[:, hc * 2048:(hc + 1) * 2048],
                                    wo_in.ap()[hc * 128:(hc + 1) * 128, :]).then_inc(S["gw"], 16)
                    phase_end(g)
                cnt.set("gw", b_gw + 256)

                @blk.tensor
                def _(t):
                    phase_wait(t)
                    for h in range(NH):
                        for qt in range(32):
                            j = 32 * h + qt
                            if qt == 0:
                                t.wait_ge(S["q%d" % (h % 2)], bq[h % 2] + 48 * (h // 2 + 1))
                                if h >= 1:
                                    t.wait_ge(S["ve2"], b_ve2 + 2 * h)
                                    t.wait_ge(S["ac2"], b_ac2 + h)
                            if j >= 1:
                                t.wait_ge(S["ac1"], b_ac1 + 4 * (j - 1) + 3)
                            for n in range(2):
                                t.matmul(psA0[:, n * 512:(n + 1) * 512],
                                         qh[h % 2][:, qt * 128:(qt + 1) * 128],
                                         kh[h % 2][:, n * 512:(n + 1) * 512],
                                         start=True, stop=True
                                         ).then_inc(S["pe2"], 1) if n == 1 else \
                                    t.matmul(psA0[:, n * 512:(n + 1) * 512],
                                             qh[h % 2][:, qt * 128:(qt + 1) * 128],
                                             kh[h % 2][:, n * 512:(n + 1) * 512],
                                             start=True, stop=True)
                            if j >= 1:
                                t.wait_ge(S["ac1"], b_ac1 + 4 * (j - 1) + 4)
                            for n in range(2):
                                t.matmul(psA1[:, n * 512:(n + 1) * 512],
                                         qh[h % 2][:, qt * 128:(qt + 1) * 128],
                                         kh[h % 2][:, 1024 + n * 512: 1024 + (n + 1) * 512],
                                         start=True, stop=True
                                         ).then_inc(S["pe2"], 1) if n == 1 else \
                                    t.matmul(psA1[:, n * 512:(n + 1) * 512],
                                             qh[h % 2][:, qt * 128:(qt + 1) * 128],
                                             kh[h % 2][:, 1024 + n * 512: 1024 + (n + 1) * 512],
                                             start=True, stop=True)
                            if qt >= 1:
                                jj = j - 1
                                t.wait_ge(S["ve1"], b_ve1 + jj + 1)
                                for kc in range(4):
                                    t.matmul(csps[:, kc * 512:(kc + 1) * 512],
                                             lwb[:, jj % 2: jj % 2 + 1],
                                             un[jj % 2][:, kc * 512:(kc + 1) * 512],
                                             start=(jj % 32 == 0), stop=(jj % 32 == 31)
                                             ).then_inc(S["pe3"], 1) if kc == 3 else \
                                        t.matmul(csps[:, kc * 512:(kc + 1) * 512],
                                                 lwb[:, jj % 2: jj % 2 + 1],
                                                 un[jj % 2][:, kc * 512:(kc + 1) * 512],
                                                 start=(jj % 32 == 0), stop=(jj % 32 == 31))
                            t.wait_ge(S["ac1"], b_ac1 + 4 * j + 1)
                            for n in range(2):
                                t.matmul(psA0[:, n * 512:(n + 1) * 512],
                                         qh[h % 2][:, qt * 128:(qt + 1) * 128],
                                         kh[h % 2][:, 2048 + n * 512: 2048 + (n + 1) * 512],
                                         start=True, stop=True
                                         ).then_inc(S["pe2"], 1) if n == 1 else \
                                    t.matmul(psA0[:, n * 512:(n + 1) * 512],
                                             qh[h % 2][:, qt * 128:(qt + 1) * 128],
                                             kh[h % 2][:, 2048 + n * 512: 2048 + (n + 1) * 512],
                                             start=True, stop=True)
                            t.wait_ge(S["ac1"], b_ac1 + 4 * j + 2)
                            for n in range(2):
                                t.matmul(psA1[:, n * 512:(n + 1) * 512],
                                         qh[h % 2][:, qt * 128:(qt + 1) * 128],
                                         kh[h % 2][:, 3072 + n * 512: 3072 + (n + 1) * 512],
                                         start=True, stop=True
                                         ).then_inc(S["pe2"], 1) if n == 1 else \
                                    t.matmul(psA1[:, n * 512:(n + 1) * 512],
                                             qh[h % 2][:, qt * 128:(qt + 1) * 128],
                                             kh[h % 2][:, 3072 + n * 512: 3072 + (n + 1) * 512],
                                             start=True, stop=True)
                        jj = 32 * h + 31
                        t.wait_ge(S["ve1"], b_ve1 + jj + 1)
                        for kc in range(4):
                            t.matmul(csps[:, kc * 512:(kc + 1) * 512],
                                     lwb[:, jj % 2: jj % 2 + 1],
                                     un[jj % 2][:, kc * 512:(kc + 1) * 512],
                                     start=(jj % 32 == 0), stop=(jj % 32 == 31)
                                     ).then_inc(S["pe3"], 1) if kc == 3 else \
                                t.matmul(csps[:, kc * 512:(kc + 1) * 512],
                                         lwb[:, jj % 2: jj % 2 + 1],
                                         un[jj % 2][:, kc * 512:(kc + 1) * 512],
                                         start=(jj % 32 == 0), stop=(jj % 32 == 31))
                        t.wait_ge(S["ac2"], b_ac2 + h + 1)
                        for kk in range(4):
                            dstp = psA[kk // 2][:, (kk % 2) * 512:(kk % 2 + 1) * 512]
                            t.matmul(dstp, onesf[:], csr[:, kk * 512:(kk + 1) * 512],
                                     start=True, stop=True).then_inc(S["pe1"], 1)
                    phase_end(t)
                cnt.set("pe1", b_pe1 + NH * 4)
                cnt.set("pe2", b_pe2 + 512 * 4)
                cnt.set("pe3", b_pe3 + 512)

                @blk.scalar
                def _(sc):
                    phase_wait(sc)
                    sc_s = cst[:, CI_S128:CI_S128 + 1]
                    for h in range(NH):
                        for qt in range(32):
                            j = 32 * h + qt
                            par = j % 2
                            sc.wait_ge(S["pe2"], b_pe2 + 4 * j + 1)
                            if j >= 2:
                                sc.wait_ge(S["pe3"], b_pe3 + j - 1)
                            sc.activation(un[par][:, 0:1024], psA0[:], AF.Exp, scale=sc_s,
                                          accum_out=lo[:, 4 * par + 0: 4 * par + 1]
                                          ).then_inc(S["ac1"], 1)
                            sc.wait_ge(S["pe2"], b_pe2 + 4 * j + 2)
                            sc.activation(un[par][:, 1024:2048], psA1[:], AF.Exp, scale=sc_s,
                                          accum_out=lo[:, 4 * par + 1: 4 * par + 2]
                                          ).then_inc(S["ac1"], 1)
                            sc.wait_ge(S["pe2"], b_pe2 + 4 * j + 3)
                            sc.activation(utmp[:, 0:1024], psA0[:], AF.Exp, scale=sc_s,
                                          accum_out=lo[:, 4 * par + 2: 4 * par + 3]
                                          ).then_inc(S["ac1"], 1)
                            sc.wait_ge(S["pe2"], b_pe2 + 4 * j + 4)
                            sc.activation(utmp[:, 1024:2048], psA1[:], AF.Exp, scale=sc_s,
                                          accum_out=lo[:, 4 * par + 3: 4 * par + 4]
                                          ).then_inc(S["ac1"], 1)
                        sc.wait_ge(S["pe3"], b_pe3 + 32 * (h + 1))
                        if h >= 1:
                            sc.wait_ge(S["pe1"], b_pe1 + 4 * (h - 1) + 4)
                        sc.copy(csr[:], csps[:])
                        sc.copy(flt[:], flt[:]).then_inc(S["ac2"], 1)
                    phase_end(sc)
                cnt.set("ac1", b_ac1 + 512 * 4)
                cnt.set("ac2", b_ac2 + NH)

                @blk.vector
                def _(v):
                    phase_wait(v)
                    vs = cnt.get("ve3")
                    for h in range(NH):
                        for qt in range(32):
                            j = 32 * h + qt
                            par = j % 2
                            v.wait_ge(S["ac1"], b_ac1 + 4 * j + 4)
                            a = lw[:, 4 * par + 0:4 * par + 1]
                            bb2 = lw[:, 4 * par + 1:4 * par + 2]
                            l = lw[:, 4 * par + 2:4 * par + 3]
                            il = lw[:, 4 * par + 3:4 * par + 4]
                            v.tensor_tensor(a, lo[:, 4 * par:4 * par + 1],
                                            lo[:, 4 * par + 1:4 * par + 2], ALU.add)
                            v.tensor_tensor(bb2, lo[:, 4 * par + 2:4 * par + 3],
                                            lo[:, 4 * par + 3:4 * par + 4], ALU.add
                                            ).then_inc(S["ve3"], 1)
                            vs += 1
                            v.wait_ge(S["ve3"], vs)
                            v.tensor_tensor(l, a, bb2, ALU.add).then_inc(S["ve3"], 1)
                            vs += 1
                            v.wait_ge(S["ve3"], vs)
                            v.reciprocal(il, l).then_inc(S["ve3"], 1)
                            vs += 1
                            v.wait_ge(S["ve3"], vs)
                            v.tensor_copy(lwb[:, par:par + 1], il)
                            v.memset(flt[:], 0.0).then_inc(S["ve1"], 1)
                        v.wait_ge(S["q%d" % (h % 2)], bq[h % 2] + 48 * (h // 2 + 1))
                        v.wait_ge(S["pe1"], b_pe1 + 4 * h + 2)
                        v.tensor_tensor(vals[:, h * 2048:h * 2048 + 1024],
                                        vh[h % 2][:, 0:1024], psA0[:], ALU.mult)
                        v.memset(flt[:], 0.0).then_inc(S["ve2"], 1)
                        v.wait_ge(S["pe1"], b_pe1 + 4 * h + 4)
                        v.tensor_tensor(vals[:, h * 2048 + 1024:(h + 1) * 2048],
                                        vh[h % 2][:, 1024:2048], psA1[:], ALU.mult)
                        v.memset(flt[:], 0.0).then_inc(S["ve2"], 1)
                    phase_end(v)
                cnt.set("ve1", b_ve1 + 512)
                cnt.set("ve2", b_ve2 + 2 * NH)
                cnt.set("ve3", cnt.get("ve3") + 512 * 3)

        def phase_C_inner(vals, wo_sb):
            (c_xl, c_st, c_gw, c_pe2, c_ve5) = bases("xl", "st", "gw", "pe2", "ve5")
            bq = [cnt.get("q%d" % _i) for _i in range(12)]
            with contextlib.ExitStack() as _es:
                xtc0 = _es.enter_context(nc.sbuf_tensor("xtc0", [128, D], F32))
                xtc1 = _es.enter_context(nc.sbuf_tensor("xtc1", [128, D], F32))
                x1sb0 = _es.enter_context(nc.sbuf_tensor("x1sb0", [128, D], F32))
                x1sb1 = _es.enter_context(nc.sbuf_tensor("x1sb1", [128, D], F32))
                po0 = _es.enter_context(nc.psum_tensor("po0", [128, 512], F32))
                po1 = _es.enter_context(nc.psum_tensor("po1", [128, 512], F32))
                po2 = _es.enter_context(nc.psum_tensor("po2", [128, 512], F32))
                po3 = _es.enter_context(nc.psum_tensor("po3", [128, 512], F32))
                blk = _es.enter_context(nc.Block("phC"))
                xtc = [xtc0, xtc1]
                x1sb = [x1sb0, x1sb1]
                po = [po0, po1, po2, po3]

                @blk.sync
                def _(s):
                    phase_wait(s)
                    for tt in range(16):
                        if tt >= 2:
                            s.wait_ge(S["ve5"], c_ve5 + 4 * (tt - 1))
                        s.dma_start(xtc[tt % 2][:], x_in.ap()[tt * 128:(tt + 1) * 128, :]
                                    ).then_inc(S["q%d" % (tt % 2)], 16)
                        if tt >= 1:
                            s.wait_ge(S["ve5"], c_ve5 + 4 * tt)
                            s.dma_start(x1_sc.ap()[(tt - 1) * 128:tt * 128, :],
                                        x1sb[(tt - 1) % 2][:]).then_inc(S["q%d" % (2 + (tt - 1) % 2)], 16)
                    s.wait_ge(S["ve5"], c_ve5 + 64)
                    s.dma_start(x1_sc.ap()[15 * 128:, :], x1sb[15 % 2][:]).then_inc(S["q3"], 16)
                    s.wait_ge(S["q2"], bq[2] + 8 * 16)
                    s.wait_ge(S["q3"], bq[3] + 8 * 16)
                    phase_end(s)
                cnt.set("q0", bq[0] + 8 * 16)
                cnt.set("q1", bq[1] + 8 * 16)
                cnt.set("q2", bq[2] + 8 * 16)
                cnt.set("q3", bq[3] + 8 * 16)

                @blk.tensor
                def _(t):
                    phase_wait(t)
                    t.wait_ge(S["gw"], c_gw)
                    for tt in range(16):
                        for od in range(4):
                            if tt >= 1:
                                t.wait_ge(S["ve5"], c_ve5 + 4 * (tt - 1) + od + 1)
                            for hc in range(16):
                                mm = t.matmul(po[od][:],
                                              vals[:, hc * 2048 + tt * 128: hc * 2048 + (tt + 1) * 128],
                                              wo_sb[:, hc * 2048 + od * 512: hc * 2048 + (od + 1) * 512],
                                              start=(hc == 0), stop=(hc == 15))
                                if hc == 15:
                                    mm.then_inc(S["pe2"], 1)
                    phase_end(t)
                cnt.set("pe2", c_pe2 + 64)

                @blk.vector
                def _(v):
                    phase_wait(v)
                    for tt in range(16):
                        v.wait_ge(S["q%d" % (tt % 2)], bq[tt % 2] + 16 * (tt // 2 + 1))
                        if tt >= 2:
                            v.wait_ge(S["q%d" % (2 + tt % 2)], bq[2 + tt % 2] + 16 * (tt // 2))
                        for od in range(4):
                            v.wait_ge(S["pe2"], c_pe2 + 4 * tt + od + 1)
                            v.tensor_add(x1sb[tt % 2][:, od * 512:(od + 1) * 512],
                                         po[od][:],
                                         xtc[tt % 2][:, od * 512:(od + 1) * 512])
                            v.memset(flt[:], 0.0).then_inc(S["ve5"], 1)
                    phase_end(v)
                cnt.set("ve5", c_ve5 + 64)

                @blk.scalar
                def _(sc):
                    phase_wait(sc)
                    phase_end(sc)

                @blk.gpsimd
                def _(g):
                    phase_wait(g)
                    phase_end(g)

        def phase_D():
            (d_xl, d_st, d_wl, d_pe1, d_pe2, d_pe3, d_pe4,
             d_ve1, d_ve2, d_ve3, d_ve4, d_ve5, d_ac1, d_ac2, d_ac3) = bases(
                "xl", "st", "wl", "pe1", "pe2", "pe3", "pe4",
                "ve1", "ve2", "ve3", "ve4", "ve5", "ac1", "ac2", "ac3")
            bq = [cnt.get("q%d" % _i) for _i in range(12)]
            with contextlib.ExitStack() as _es:
                x1t0 = _es.enter_context(nc.sbuf_tensor("x1t0", [128, D], F32))
                x1t1 = _es.enter_context(nc.sbuf_tensor("x1t1", [128, D], F32))
                sqt2 = _es.enter_context(nc.sbuf_tensor("sqt2", [128, D], F32))
                xn20 = _es.enter_context(nc.sbuf_tensor("xn20", [128, D], F32))
                xn21 = _es.enter_context(nc.sbuf_tensor("xn21", [128, D], F32))
                yfm = _es.enter_context(nc.sbuf_tensor("yfm", [128, 16 * 512], BF16))
                tfm = _es.enter_context(nc.sbuf_tensor("tfm", [128, 16 * 512], BF16))
                g2 = _es.enter_context(nc.sbuf_tensor("g2", [128, 64 * 512], BF16))
                hb0 = _es.enter_context(nc.sbuf_tensor("hb0", [128, 2048], BF16))
                hb1 = _es.enter_context(nc.sbuf_tensor("hb1", [128, 2048], BF16))
                wub0 = _es.enter_context(nc.sbuf_tensor("wub0", [128, 2048], BF16))
                wub1 = _es.enter_context(nc.sbuf_tensor("wub1", [128, 2048], BF16))
                wgb0 = _es.enter_context(nc.sbuf_tensor("wgb0", [128, 2048], BF16))
                wgb1 = _es.enter_context(nc.sbuf_tensor("wgb1", [128, 2048], BF16))
                w2b0 = _es.enter_context(nc.sbuf_tensor("w2b0", [128, 512], BF16))
                w2b1 = _es.enter_context(nc.sbuf_tensor("w2b1", [128, 512], BF16))
                xup0 = _es.enter_context(nc.sbuf_tensor("xup0", [128, 512], F32))
                xup1 = _es.enter_context(nc.sbuf_tensor("xup1", [128, 512], F32))
                sg0 = _es.enter_context(nc.sbuf_tensor("sg0", [128, 512], F32))
                sg1 = _es.enter_context(nc.sbuf_tensor("sg1", [128, 512], F32))
                gc0 = _es.enter_context(nc.sbuf_tensor("gc0", [128, 512], BF16))
                gc1 = _es.enter_context(nc.sbuf_tensor("gc1", [128, 512], BF16))
                xob0 = _es.enter_context(nc.sbuf_tensor("xob0", [128, 512], F32))
                xob1 = _es.enter_context(nc.sbuf_tensor("xob1", [128, 512], F32))
                osb0 = _es.enter_context(nc.sbuf_tensor("osb0", [128, 512], F32))
                osb1 = _es.enter_context(nc.sbuf_tensor("osb1", [128, 512], F32))
                st1c = _es.enter_context(nc.sbuf_tensor("st1c", [128, 1], F32))
                st1d = _es.enter_context(nc.sbuf_tensor("st1d", [128, 1], F32))
                st2c = _es.enter_context(nc.sbuf_tensor("st2c", [128, 1], F32))
                st2d = _es.enter_context(nc.sbuf_tensor("st2d", [128, 1], F32))
                mvc = _es.enter_context(nc.sbuf_tensor("mvc", [128, 4], F32))
                mvd = _es.enter_context(nc.sbuf_tensor("mvd", [128, 4], F32))
                pf0 = _es.enter_context(nc.psum_tensor("pf0", [128, 512], F32))
                pf1 = _es.enter_context(nc.psum_tensor("pf1", [128, 512], F32))
                ph = _es.enter_context(nc.psum_tensor("ph", [128, 512], F32))
                pd0 = _es.enter_context(nc.psum_tensor("pd0", [128, 512], F32))
                pd1 = _es.enter_context(nc.psum_tensor("pd1", [128, 512], F32))
                pd2 = _es.enter_context(nc.psum_tensor("pd2", [128, 512], F32))
                pd3 = _es.enter_context(nc.psum_tensor("pd3", [128, 512], F32))
                blk = _es.enter_context(nc.Block("phD"))
                x1t = [x1t0, x1t1]
                xn2 = [xn20, xn21]
                hb = [hb0, hb1]
                wub = [wub0, wub1]
                wgb = [wgb0, wgb1]
                w2b = [w2b0, w2b1]
                xup = [xup0, xup1]
                sg = [sg0, sg1]
                gc = [gc0, gc1]
                xob = [xob0, xob1]
                osb = [osb0, osb1]
                stat1 = [st1c, st1d]
                stat2 = [st2c, st2d]
                mv = [mvc, mvd]
                pf = [pf0, pf1]
                pd = [pd0, pd1, pd2, pd3]

                WL_CH = 16 * 16 + 64 * 32 + 256 * 16

                def wl_h(ch, fo):
                    return d_wl + ch * WL_CH + 16 * (fo + 1)

                def wl_ug(ch, c):
                    return d_wl + ch * WL_CH + 256 + 32 * (c + 1)

                def wl_w2(ch, od, c):
                    return d_wl + ch * WL_CH + 256 + 2048 + 16 * (od * 64 + c + 1)

                XL_CH = 20 * 16

                def xl_ln(ch, tt):
                    return d_xl + ch * XL_CH + 16 * (tt + 1)

                def xl_part(ch, k):
                    return d_xl + ch * XL_CH + 64 + 16 * (k + 1)

                PE3_UP, PE3_GT, PE3_H = {}, {}, {}
                _n = 0
                for _ch in range(4):
                    for _c in range(64):
                        _cc = _ch * 64 + _c
                        _n += 1
                        PE3_UP[_cc] = _n
                        _n += 1
                        PE3_GT[_cc] = _n
                        if _c >= 1:
                            _n += 1
                            PE3_H[_cc - 1] = _n
                    _n += 1
                    PE3_H[_ch * 64 + 63] = _n

                @blk.sync
                def _(s):
                    phase_wait(s)
                    s.wait_ge(S["gst"], cnt.get("gst"))
                    for ch in range(4):
                        for tt in range(4):
                            u = ch * 4 + tt
                            if u >= 2:
                                s.wait_ge(S["ve1"], d_ve1 + 8 * (u - 1))
                            s.dma_start(x1t[u % 2][:],
                                        x1_sc.ap()[u * 128:(u + 1) * 128, :]).then_inc(S["q%d" % (u % 2)], 16)
                        for fo in range(16):
                            ff = ch * 16 + fo
                            if ff >= 2:
                                s.wait_ge(S["pe2"], d_pe2 + ff - 1)
                            s.dma_start(hb[ff % 2][:],
                                        h2048_in.ap()[fo, :, :, :].rearrange("p a b -> p (a b)")
                                        ).then_inc(S["q%d" % (4 + ff % 2)], 16)
                        for c in range(64):
                            cc = ch * 64 + c
                            if cc >= 2:
                                s.wait_ge(S["pe3"], d_pe3 + 3 * (cc - 1))
                            s.dma_start(wub[cc % 2][:], wup_sc.ap()[c, :, :]).then_inc(S["q%d" % (6 + cc % 2)], 16)
                            s.dma_start(wgb[cc % 2][:], wgate_sc.ap()[c, :, :]).then_inc(S["q%d" % (6 + cc % 2)], 16)
                        for od in range(4):
                            for c in range(64):
                                dcc = (ch * 4 + od) * 64 + c
                                if dcc >= 2:
                                    s.wait_ge(S["pe4"], d_pe4 + dcc - 1)
                                s.dma_start(w2b[dcc % 2][:],
                                            w2t_sc.ap()[c * 128:(c + 1) * 128,
                                                        od * 512:(od + 1) * 512]
                                            ).then_inc(S["q%d" % (8 + dcc % 2)], 16)
                            for tt in range(4):
                                k = od * 4 + tt
                                idx = ch * 16 + k
                                if idx >= 2:
                                    s.wait_ge(S["ve5"], d_ve5 + idx - 1)
                                s.dma_start(xob[idx % 2][:],
                                            x1_sc.ap()[(ch * 4 + tt) * 128:(ch * 4 + tt + 1) * 128,
                                                       od * 512:(od + 1) * 512]
                                            ).then_inc(S["q%d" % (2 + idx % 2)], 16)
                    phase_end(s)
                for _i, _v in ((0, 8), (1, 8), (2, 32), (3, 32), (4, 32), (5, 32),
                               (6, 128 * 2), (7, 128 * 2), (8, 512), (9, 512)):
                    cnt.set("q%d" % _i, bq[_i] + _v * 16)

                def ln_chain2(v, u):
                    par = u % 2
                    v.wait_ge(S["q%d" % (u % 2)], bq[u % 2] + 16 * (u // 2 + 1))
                    v.reduce_sum(stat1[par][:], x1t[par][:], axis=AX.X).then_inc(S["ve1"], 1)
                    v.wait_ge(S["ac1"], d_ac1 + 2 * u + 1)
                    v.wait_ge(S["ve1"], d_ve1 + 8 * u + 1)
                    m = mv[par][:, 0:1]
                    var = mv[par][:, 1:2]
                    rstd = mv[par][:, 2:3]
                    negmr = mv[par][:, 3:4]
                    v.tensor_scalar(m, stat1[par][:], 1.0 / D, None, ALU.mult).then_inc(S["ve1"], 1)
                    v.tensor_scalar(var, stat2[par][:], 1.0 / D, None, ALU.mult).then_inc(S["ve1"], 1)
                    v.wait_ge(S["ve1"], d_ve1 + 8 * u + 3)
                    v.tensor_tensor(rstd, m, m, ALU.mult).then_inc(S["ve1"], 1)
                    v.wait_ge(S["ve1"], d_ve1 + 8 * u + 4)
                    v.tensor_tensor(var, var, rstd, ALU.subtract).then_inc(S["ve1"], 1)
                    v.wait_ge(S["ac1"], d_ac1 + 2 * u + 2)
                    v.reciprocal(rstd, rstd).then_inc(S["ve1"], 1)
                    v.wait_ge(S["ve1"], d_ve1 + 8 * u + 6)
                    v.scalar_tensor_tensor(negmr, m, -1.0, rstd, ALU.mult, ALU.mult
                                           ).then_inc(S["ve1"], 1)
                    v.wait_ge(S["ve1"], d_ve1 + 8 * u + 7)
                    if u >= 2:
                        v.wait_ge(S["pe1"], d_pe1 + 16 * (u - 1))
                    v.tensor_scalar(xn2[par][:], x1t[par][:], m, rstd,
                                    ALU.subtract, ALU.mult).then_inc(S["ve1"], 1)

                @blk.vector
                def _(v):
                    phase_wait(v)
                    for ch in range(4):
                        ln_chain2(v, ch * 4)
                        for tt in range(4):
                            u = ch * 4 + tt
                            if tt + 1 < 4:
                                ln_chain2(v, u + 1)
                            for k in range(16):
                                g_idx = 16 * u + k
                                v.wait_ge(S["pe1"], d_pe1 + g_idx + 1)
                                v.tensor_scalar(
                                    yfm[:, k * 512 + tt * 128: k * 512 + (tt + 1) * 128],
                                    pf[k % 2][:, 0:128], su_sb[:, k:k + 1], None, ALU.mult)
                                v.memset(flt[:], 0.0).then_inc(S["ve3"], 1)
                        for fo in range(16):
                            ff = ch * 16 + fo
                            v.wait_ge(S["pe2"], d_pe2 + ff + 1)
                            v.tensor_copy(tfm[:, fo * 512:(fo + 1) * 512], pf[ff % 2][:])
                            v.memset(flt[:], 0.0).then_inc(S["ve4"], 1)
                        for c in range(64):
                            cc = ch * 64 + c
                            v.wait_ge(S["ac3"], d_ac3 + 2 * cc + 2)
                            v.tensor_tensor(gc[cc % 2][:], xup[cc % 2][:], sg[cc % 2][:],
                                            ALU.mult)
                            v.memset(flt[:], 0.0).then_inc(S["ve2"], 1)
                            v.wait_ge(S["pe3"], d_pe3 + PE3_H[cc])
                            v.tensor_copy(g2[:, c * 512:(c + 1) * 512], ph[:])
                            v.memset(flt[:], 0.0).then_inc(S["ve2"], 1)
                        for od in range(4):
                            for tt in range(4):
                                k = od * 4 + tt
                                idx = ch * 16 + k
                                v.wait_ge(S["pe4"], d_pe4 + (ch * 4 + od) * 64 + 64)
                                v.wait_ge(S["q%d" % (2 + idx % 2)], bq[2 + idx % 2] + 16 * (idx // 2 + 1))
                                if idx >= 2:
                                    v.wait_ge(S["q%d" % (10 + idx % 2)], bq[10 + idx % 2] + 16 * (idx // 2))
                                v.tensor_add(osb[idx % 2][:], pd[tt][:], xob[idx % 2][:])
                                v.memset(flt[:], 0.0).then_inc(S["ve5"], 1)
                    phase_end(v)
                cnt.set("ve1", d_ve1 + 8 * 16)
                cnt.set("ve2", d_ve2 + 2 * 256)
                cnt.set("ve3", d_ve3 + 256)
                cnt.set("ve4", d_ve4 + 64)
                cnt.set("ve5", d_ve5 + 64)

                @blk.scalar
                def _(sc):
                    phase_wait(sc)
                    for ch in range(4):
                        for tt in range(4):
                            u = ch * 4 + tt
                            par = u % 2
                            sc.wait_ge(S["q%d" % ((ch * 4 + tt) % 2)], bq[(ch * 4 + tt) % 2] + 16 * ((ch * 4 + tt) // 2 + 1))
                            sc.activation(sqt2[:], x1t[par][:], AF.Square,
                                          accum_out=stat2[par][:]).then_inc(S["ac1"], 1)
                            sc.wait_ge(S["ve1"], d_ve1 + 8 * u + 5)
                            sc.activation(mv[par][:, 2:3], mv[par][:, 1:2], AF.Sqrt,
                                          bias=cst[:, CI_EPS:CI_EPS + 1]).then_inc(S["ac1"], 1)
                        for c in range(64):
                            cc = ch * 64 + c
                            sc.wait_ge(S["pe3"], d_pe3 + PE3_UP[cc])
                            if cc >= 2:
                                sc.wait_ge(S["ve2"], d_ve2 + 2 * cc - 3)
                            sc.activation(xup[cc % 2][:], pf0[:], AF.Copy,
                                          scale=scup[:, c:c + 1]).then_inc(S["ac3"], 1)
                            sc.wait_ge(S["pe3"], d_pe3 + PE3_GT[cc])
                            sc.activation(sg[cc % 2][:], pf1[:], AF.Silu,
                                          scale=scgt[:, c:c + 1]).then_inc(S["ac3"], 1)
                    phase_end(sc)
                cnt.set("ac1", d_ac1 + 32)
                cnt.set("ac3", d_ac3 + 2 * 256)

                @blk.tensor
                def _(t):
                    phase_wait(t)
                    for ch in range(4):
                        if ch >= 1:
                            t.wait_ge(S["ac3"], d_ac3 + 128 * ch)
                        for tt in range(4):
                            u = ch * 4 + tt
                            t.wait_ge(S["ve1"], d_ve1 + 8 * u + 8)
                            for k in range(16):
                                g_idx = 16 * u + k
                                if g_idx >= 2:
                                    t.wait_ge(S["ve3"], d_ve3 + g_idx - 1)
                                t.transpose(pf[k % 2][:, 0:128], xn2[u % 2][:, k * 128:(k + 1) * 128],
                                            idnf[:]).then_inc(S["pe1"], 1)
                        t.wait_ge(S["ve3"], d_ve3 + 64 * (ch + 1))
                        for fo in range(16):
                            ff = ch * 16 + fo
                            t.wait_ge(S["q%d" % (4 + ff % 2)], bq[4 + ff % 2] + 16 * (ff // 2 + 1))
                            if ff >= 2:
                                t.wait_ge(S["ve4"], d_ve4 + ff - 1)
                            for fi in range(16):
                                mm = t.matmul(pf[ff % 2][:],
                                              hb[ff % 2][:, fi * 128:(fi + 1) * 128],
                                              yfm[:, fi * 512:(fi + 1) * 512],
                                              start=(fi == 0), stop=(fi == 15))
                                if fi == 15:
                                    mm.then_inc(S["pe2"], 1)
                        t.wait_ge(S["ve4"], d_ve4 + 16 * (ch + 1))
                        for c in range(64):
                            cc = ch * 64 + c
                            t.wait_ge(S["q%d" % (6 + cc % 2)], bq[6 + cc % 2] + 32 * (cc // 2 + 1))
                            if cc >= 1:
                                t.wait_ge(S["ac3"], d_ac3 + 2 * (cc - 1) + 1)
                            for fc in range(16):
                                mm = t.matmul(pf0[:], wub[cc % 2][:, fc * 128:(fc + 1) * 128],
                                              tfm[:, fc * 512:(fc + 1) * 512],
                                              start=(fc == 0), stop=(fc == 15))
                                if fc == 15:
                                    mm.then_inc(S["pe3"], 1)
                            if cc >= 1:
                                t.wait_ge(S["ac3"], d_ac3 + 2 * (cc - 1) + 2)
                            for fc in range(16):
                                mm = t.matmul(pf1[:], wgb[cc % 2][:, fc * 128:(fc + 1) * 128],
                                              tfm[:, fc * 512:(fc + 1) * 512],
                                              start=(fc == 0), stop=(fc == 15))
                                if fc == 15:
                                    mm.then_inc(S["pe3"], 1)
                            if c >= 1:
                                t.wait_ge(S["ve2"], d_ve2 + 2 * (cc - 1) + 1)
                                if cc >= 2:
                                    t.wait_ge(S["ve2"], d_ve2 + 2 * (cc - 2) + 2)
                                t.matmul(ph[:], h128[:], gc[(cc - 1) % 2][:],
                                         start=True, stop=True).then_inc(S["pe3"], 1)
                        cc = ch * 64 + 63
                        t.wait_ge(S["ve2"], d_ve2 + 2 * cc + 1)
                        if cc >= 1:
                            t.wait_ge(S["ve2"], d_ve2 + 2 * (cc - 1) + 2)
                        t.matmul(ph[:], h128[:], gc[cc % 2][:],
                                 start=True, stop=True).then_inc(S["pe3"], 1)
                        for od in range(4):
                            for c in range(64):
                                dcc = (ch * 4 + od) * 64 + c
                                t.wait_ge(S["q%d" % (8 + dcc % 2)], bq[8 + dcc % 2] + 16 * (dcc // 2 + 1))
                                t.wait_ge(S["ve2"], d_ve2 + 2 * (ch * 64 + c) + 2)
                                if c == 0 and (ch * 4 + od) >= 1:
                                    t.wait_ge(S["ve5"], d_ve5 + (ch * 4 + od - 1) * 4 + 4)
                                for tt in range(4):
                                    mm = t.matmul(pd[tt][:],
                                                  g2[:, c * 512 + tt * 128: c * 512 + (tt + 1) * 128],
                                                  w2b[dcc % 2][:],
                                                  start=(c == 0), stop=(c == 63))
                                    if tt == 3:
                                        mm.then_inc(S["pe4"], 1)
                    phase_end(t)
                cnt.set("pe1", d_pe1 + 256)
                cnt.set("pe2", d_pe2 + 64)
                cnt.set("pe3", d_pe3 + 3 * 256)
                cnt.set("pe4", d_pe4 + 1024)

                @blk.gpsimd
                def _(g):
                    phase_wait(g)
                    for ch in range(4):
                        for od in range(4):
                            for tt in range(4):
                                k = od * 4 + tt
                                idx = ch * 16 + k
                                g.wait_ge(S["ve5"], d_ve5 + idx + 1)
                                g.dma_start(out.ap()[(ch * 4 + tt) * 128:(ch * 4 + tt + 1) * 128,
                                                     od * 512:(od + 1) * 512],
                                            osb[idx % 2][:]).then_inc(S["q%d" % (10 + idx % 2)], 16)
                    g.wait_ge(S["q10"], bq[10] + 16 * 32)
                    g.wait_ge(S["q11"], bq[11] + 16 * 32)
                    phase_end(g)
                cnt.set("q10", bq[10] + 16 * 32)
                cnt.set("q11", bq[11] + 16 * 32)
            close_phase()

        phase_setup()
        phase_A()
        phase_BC()
        phase_D()
    return nc



_H2048 = None
_H128 = None
_H64 = None


def _consts():
    global _H2048, _H128, _H64
    if _H2048 is None:
        _H2048 = _hadamard(2048)
        _H128 = _hadamard(128)
        _H64 = _hadamard(64)
    bf = ml_dtypes.bfloat16
    h2048 = np.ascontiguousarray(
        _H2048.reshape(16, 128, 16, 128).transpose(2, 1, 0, 3)).astype(bf)
    h128 = _H128.astype(bf)
    h64 = _H64.astype(bf)
    ident = np.eye(128, dtype=bf)
    identf = np.eye(128, dtype=np.float32)
    ones = np.ones((1, 128), np.float32)
    consts = np.zeros((1, NC_CONST), np.float32)
    consts[0, CI_S128] = S128
    consts[0, CI_EPS] = 1e-5
    return h2048, h128, h64, ident, identf, ones, consts


def kernel(L0_attn_x, L0_attn_w_qkv, L0_attn_w_o,
           L0_ffn_SU_ug, L0_ffn_SU_dp, L0_ffn_Wscale_ug, L0_ffn_Wscale_dp,
           L0_ffn_W_up, L0_ffn_W_gate, L0_ffn_W_down):
    x = np.asarray(L0_attn_x, np.float32)
    wqkv = np.asarray(L0_attn_w_qkv, np.float32)
    wo = np.ascontiguousarray(np.asarray(L0_attn_w_o, np.float32))
    su_ug = np.asarray(L0_ffn_SU_ug, np.float32).reshape(1, D)
    su_dp = np.asarray(L0_ffn_SU_dp, np.float32).reshape(1, INTER)
    wsc_ug = np.asarray(L0_ffn_Wscale_ug, np.float32).reshape(1, 2 * INTER)
    wsc_dp = np.asarray(L0_ffn_Wscale_dp, np.float32).reshape(1, D)
    wup = np.asarray(L0_ffn_W_up, np.float32)
    wgate = np.asarray(L0_ffn_W_gate, np.float32)
    wdown = np.asarray(L0_ffn_W_down, np.float32)

    wqkv_in = np.ascontiguousarray(
        wqkv.reshape(16, 128, 48, 128).transpose(2, 1, 0, 3))
    wup_in = np.ascontiguousarray(
        wup.reshape(64, 128, 16, 128).transpose(0, 3, 2, 1))
    wgate_in = np.ascontiguousarray(
        wgate.reshape(64, 128, 16, 128).transpose(0, 3, 2, 1))
    wdownT_in = np.ascontiguousarray(wdown.T)

    h2048, h128, h64, ident, identf, ones, consts = _consts()

    nc = build_nc()
    in_maps = []
    for c in range(8):
        b, p = divmod(c, 2)
        own = x[b, p * TOK:(p + 1) * TOK, :]
        other = x[b, (1 - p) * TOK:(2 - p) * TOK, :]
        x_core = np.ascontiguousarray(np.concatenate([own, other], axis=0))
        in_maps.append(dict(
            x_in=x_core, wqkv_in=wqkv_in, wo_in=wo,
            wup_in=wup_in, wgate_in=wgate_in, wdownT_in=wdownT_in,
            su_ug_in=su_ug, su_dp_in=su_dp, wsc_ug_in=wsc_ug, wsc_dp_in=wsc_dp,
            h2048_in=h2048, h128_in=h128, h64_in=h64,
            ident_in=ident, identf_in=identf, ones_in=ones, consts_in=consts,
        ))
    res = run_bass_kernel_spmd(nc, in_maps, core_ids=list(range(8)))
    outp = np.empty((4, TFULL, D), np.float32)
    for c in range(8):
        b, p = divmod(c, 2)
        outp[b, p * TOK:(p + 1) * TOK, :] = res.results[c]["out"]
    return outp



# version 07
# speedup vs baseline: 1.1436x; 1.1436x over previous
import math
import contextlib
import numpy as np
import ml_dtypes
import concourse.bass as bass
import concourse.mybir as mybir
from concourse.bass_utils import run_bass_kernel_spmd

F32 = mybir.dt.float32
BF16 = mybir.dt.bfloat16
AF = mybir.ActivationFunctionType
ALU = mybir.AluOpType
AX = mybir.AxisListType

D = 2048
TOK = 2048
TFULL = 4096
NH = 16
INTER = 8192
S128 = 1.0 / math.sqrt(128.0)
C1 = 1.0 / math.sqrt(2048.0)
CDP = 1.0 / math.sqrt(8192.0)

CI_S128, CI_EPS = 0, 1
NC_CONST = 4

SEMNAMES = ["bar", "ld", "st", "gw", "gst", "xl", "wl",
            "pe1", "pe2", "pe3", "pe4",
            "ve1", "ve2", "ve3", "ve4", "ve5",
            "ac1", "ac2", "ac3"] + ["q%d" % i for i in range(16)]


def _hadamard(n):
    h = np.array([[1.0]])
    while h.shape[0] < n:
        h = np.block([[h, h], [h, -h]])
    return h


class Ctr:
    def __init__(self):
        self.c = {}

    def inc(self, instr, sem, name, k=1):
        instr.then_inc(sem, k)
        self.c[name] = self.c.get(name, 0) + k
        return self.c[name]

    def get(self, name):
        return self.c.get(name, 0)

    def set(self, name, v):
        self.c[name] = v


def build_nc():
    nc = bass.Bass("TRN2", target_bir_lowering=False)

    x_in = nc.dram_tensor("x_in", [TFULL, D], F32, kind="ExternalInput")
    wqkv_in = nc.dram_tensor("wqkv_in", [48, 128, 16, 128], F32, kind="ExternalInput")
    wo_in = nc.dram_tensor("wo_in", [D, D], F32, kind="ExternalInput")
    wup_in = nc.dram_tensor("wup_in", [64, 128, 16, 128], F32, kind="ExternalInput")
    wgate_in = nc.dram_tensor("wgate_in", [64, 128, 16, 128], F32, kind="ExternalInput")
    wdownT_in = nc.dram_tensor("wdownT_in", [INTER, D], F32, kind="ExternalInput")
    su_ug_in = nc.dram_tensor("su_ug_in", [1, D], F32, kind="ExternalInput")
    su_dp_in = nc.dram_tensor("su_dp_in", [1, INTER], F32, kind="ExternalInput")
    wsc_ug_in = nc.dram_tensor("wsc_ug_in", [1, 2 * INTER], F32, kind="ExternalInput")
    wsc_dp_in = nc.dram_tensor("wsc_dp_in", [1, D], F32, kind="ExternalInput")
    h2048_in = nc.dram_tensor("h2048_in", [16, 128, 16, 128], BF16, kind="ExternalInput")
    h128_in = nc.dram_tensor("h128_in", [128, 128], BF16, kind="ExternalInput")
    h64_in = nc.dram_tensor("h64_in", [64, 64], BF16, kind="ExternalInput")
    ident_in = nc.dram_tensor("ident_in", [128, 128], BF16, kind="ExternalInput")
    identf_in = nc.dram_tensor("identf_in", [128, 128], F32, kind="ExternalInput")
    ones_in = nc.dram_tensor("ones_in", [1, 128], F32, kind="ExternalInput")
    consts_in = nc.dram_tensor("consts_in", [1, NC_CONST], F32, kind="ExternalInput")

    out = nc.dram_tensor("out", [TOK, D], F32, kind="ExternalOutput")

    q_sc = nc.dram_tensor("q_sc", [D, TFULL], BF16, kind="Internal")
    k_sc = nc.dram_tensor("k_sc", [D, TFULL], BF16, kind="Internal")
    v_sc = nc.dram_tensor("v_sc", [D, TOK], BF16, kind="Internal")
    x1_sc = nc.dram_tensor("x1_sc", [TOK, D], F32, kind="Internal")
    wup_sc = nc.dram_tensor("wup_sc", [64, 128, 2048], BF16, kind="Internal")
    wgate_sc = nc.dram_tensor("wgate_sc", [64, 128, 2048], BF16, kind="Internal")
    w2pre_sc = nc.dram_tensor("w2pre_sc", [INTER, D], BF16, kind="Internal")
    w2t_sc = nc.dram_tensor("w2t_sc", [INTER, D], BF16, kind="Internal")

    def bcast_row(dram_t, n, col0=0):
        a = dram_t.ap()
        return bass.AP(a.tensor, a.offset + col0, [[0, 128], [1, n]])

    def strided_col(dram_t, p, cols, col0=0):
        a = dram_t.ap()
        return bass.AP(a.tensor, a.offset + col0, [[1, p], [p, cols]])

    cnt = Ctr()

    with contextlib.ExitStack() as stack:
        sems = {sn: stack.enter_context(nc.semaphore(sn)) for sn in SEMNAMES}
        S = sems

        cst = stack.enter_context(nc.sbuf_tensor("cst", [128, NC_CONST], F32))
        onesf = stack.enter_context(nc.sbuf_tensor("onesf", [1, 128], F32))
        idn = stack.enter_context(nc.sbuf_tensor("idn", [128, 128], BF16))
        idnf = stack.enter_context(nc.sbuf_tensor("idnf", [128, 128], F32))
        flt = stack.enter_context(nc.sbuf_tensor("flt", [128, 1], F32))
        h128 = stack.enter_context(nc.sbuf_tensor("h128", [128, 128], BF16))
        h64 = stack.enter_context(nc.sbuf_tensor("h64", [64, 64], BF16))
        su_sb = stack.enter_context(nc.sbuf_tensor("su_sb", [128, 16], F32))
        scup = stack.enter_context(nc.sbuf_tensor("scup", [128, 64], F32))
        scgt = stack.enter_context(nc.sbuf_tensor("scgt", [128, 64], F32))

        phase_no = [0]

        def phase_wait(eng):
            if phase_no[0] > 0:
                eng.wait_ge(S["bar"], 5 * phase_no[0])

        def phase_end(eng):
            eng.sem_inc(S["bar"], 1)

        def close_phase():
            phase_no[0] += 1

        def bases(*names):
            return tuple(cnt.get(n) for n in names)

        def phase_setup():
            b_ld, b_st, b_pe1, b_pe2, b_ve1, b_ve2, b_ac1, b_ac2 = bases(
                "ld", "st", "pe1", "pe2", "ve1", "ve2", "ac1", "ac2")
            bq = [cnt.get("q%d" % _i) for _i in range(16)]
            with contextlib.ExitStack() as _es:
                t_ug = _es.enter_context(nc.sbuf_tensor("t_ug", [128, 64], F32))
                t_dp = _es.enter_context(nc.sbuf_tensor("t_dp", [128, 64], F32))
                bcdp = _es.enter_context(nc.sbuf_tensor("bcdp", [128, D], F32))
                sdp_row = _es.enter_context(nc.sbuf_tensor("sdp_row", [1, D], F32))
                wdf0 = _es.enter_context(nc.sbuf_tensor("wdf0", [128, D], F32))
                wdf1 = _es.enter_context(nc.sbuf_tensor("wdf1", [128, D], F32))
                wdb0 = _es.enter_context(nc.sbuf_tensor("wdb0", [128, D], BF16))
                wdb1 = _es.enter_context(nc.sbuf_tensor("wdb1", [128, D], BF16))
                wcb0 = _es.enter_context(nc.sbuf_tensor("wcb0", [64, 1024], BF16))
                wcb1 = _es.enter_context(nc.sbuf_tensor("wcb1", [64, 1024], BF16))
                wob0 = _es.enter_context(nc.sbuf_tensor("wob0", [64, 1024], BF16))
                wob1 = _es.enter_context(nc.sbuf_tensor("wob1", [64, 1024], BF16))
                ps_bc = _es.enter_context(nc.psum_tensor("ps_bc", [128, 512], F32))
                ps_h0 = _es.enter_context(nc.psum_tensor("ps_h0", [64, 1024], F32))
                _blkcm = nc.Block("setup")
                blk = _blkcm.__enter__()
                wdf = [wdf0, wdf1]
                wdb = [wdb0, wdb1]
                wcb = [wcb0, wcb1]
                wob = [wob0, wob1]
                psh = [ps_h0, ps_h0]

                @blk.sync
                def _(s):
                    phase_wait(s)
                    for dst, src in (
                        (cst[:], bcast_row(consts_in, NC_CONST)),
                        (onesf[:], ones_in.ap()),
                        (idn[:], ident_in.ap()),
                        (idnf[:], identf_in.ap()),
                        (h128[:], h128_in.ap()),
                        (h64[:], h64_in.ap()),
                    ):
                        s.dma_start(dst, src).then_inc(S["ld"], 16)
                    s.dma_start(su_sb[:], strided_col(su_ug_in, 128, 16),
                                allow_slow_non_contiguous=True).then_inc(S["ld"], 16)
                    s.dma_start(t_ug[:], strided_col(wsc_ug_in, 128, 64),
                                allow_slow_non_contiguous=True).then_inc(S["ld"], 16)
                    s.dma_start(t_dp[:], strided_col(su_dp_in, 128, 64),
                                allow_slow_non_contiguous=True).then_inc(S["ld"], 16)
                    s.dma_start(sdp_row[:], wsc_dp_in.ap()).then_inc(S["ld"], 16)
                    s.dma_start(scgt[:], strided_col(wsc_ug_in, 128, 64, col0=INTER),
                                allow_slow_non_contiguous=True).then_inc(S["ld"], 16)
                    for i in range(64):
                        if i >= 2:
                            s.wait_ge(S["ve2"], b_ve2 + (i - 1))
                        s.dma_start(wdf[i % 2][:], wdownT_in.ap()[i * 128:(i + 1) * 128, :]
                                    ).then_inc(S["q%d" % (i % 2)], 16)
                        if i >= 1:
                            s.wait_ge(S["ve2"], b_ve2 + i)
                            s.dma_start(w2pre_sc.ap()[(i - 1) * 128:i * 128, :],
                                        wdb[(i - 1) % 2][:]).then_inc(S["q%d" % (2 + (i - 1) % 2)], 16)
                    s.wait_ge(S["ve2"], b_ve2 + 64)
                    s.dma_start(w2pre_sc.ap()[63 * 128:, :], wdb[1][:]).then_inc(S["q3"], 16)
                    s.wait_ge(S["q2"], bq[2] + 32 * 16)
                    s.wait_ge(S["q3"], bq[3] + 32 * 16)
                    a = w2pre_sc.ap()
                    b = w2t_sc.ap()
                    for j in range(256):
                        r, odh = divmod(j, 2)
                        src = bass.AP(a.tensor, a.offset + r * 2048 + odh * 1024,
                                      [[128 * 2048, 64], [1, 1024]])
                        if j >= 2:
                            s.wait_ge(S["pe2"], b_pe2 + (j - 1))
                        s.dma_start(wcb[j % 2][:], src).then_inc(S["q%d" % (4 + j % 2)], 16)
                        if j >= 1:
                            ro, odho = divmod(j - 1, 2)
                            s.wait_ge(S["ac2"], b_ac2 + j)
                            dst = bass.AP(b.tensor, b.offset + ro * 2048 + odho * 1024,
                                          [[128 * 2048, 64], [1, 1024]])
                            s.dma_start(dst, wob[(j - 1) % 2][:]).then_inc(S["q%d" % (6 + (j - 1) % 2)], 16)
                    s.wait_ge(S["ac2"], b_ac2 + 256)
                    dst = bass.AP(b.tensor, b.offset + 127 * 2048 + 1024,
                                  [[128 * 2048, 64], [1, 1024]])
                    s.dma_start(dst, wob[1][:]).then_inc(S["q7"], 16)
                    s.wait_ge(S["q6"], bq[6] + 128 * 16)
                    s.wait_ge(S["q7"], bq[7] + 128 * 16)
                    phase_end(s)
                cnt.set("ld", b_ld + 16 * 11)
                for _i, _v in ((0, 32), (1, 32), (2, 32), (3, 32), (4, 128), (5, 128), (6, 128), (7, 128)):
                    cnt.set("q%d" % _i, bq[_i] + _v * 16)

                @blk.vector
                def _(v):
                    phase_wait(v)
                    v.wait_ge(S["ld"], b_ld + 176)
                    v.tensor_tensor(scup[:], t_ug[:], t_dp[:], ALU.mult).then_inc(S["ve1"], 1)
                    v.wait_ge(S["ve1"], b_ve1 + 1)
                    v.tensor_scalar(scup[:], scup[:], C1, None, ALU.mult)
                    v.tensor_scalar(scgt[:], scgt[:], C1, None, ALU.mult)
                    v.tensor_scalar(sdp_row[:], sdp_row[:], CDP, None, ALU.mult
                                    ).then_inc(S["ve1"], 1)
                    for i in range(64):
                        v.wait_ge(S["q%d" % (i % 2)], bq[i % 2] + 16 * (i // 2 + 1))
                        if i == 0:
                            v.wait_ge(S["ac1"], b_ac1 + 4)
                        if i >= 2:
                            v.wait_ge(S["q%d" % (2 + i % 2)], bq[2 + i % 2] + 16 * (i // 2))
                        v.tensor_tensor(wdb[i % 2][:], wdf[i % 2][:], bcdp[:], ALU.mult)
                        v.memset(flt[:], 0.0).then_inc(S["ve2"], 1)
                    phase_end(v)
                cnt.set("ve1", b_ve1 + 2)
                cnt.set("ve2", b_ve2 + 64)

                @blk.tensor
                def _(t):
                    phase_wait(t)
                    t.wait_ge(S["ve1"], b_ve1 + 2)
                    for kk in range(4):
                        if kk >= 1:
                            t.wait_ge(S["ac1"], b_ac1 + kk)
                        t.matmul(ps_bc[:], onesf[:], sdp_row[:, kk * 512:(kk + 1) * 512],
                                 start=True, stop=True).then_inc(S["pe1"], 1)
                    for j in range(256):
                        t.wait_ge(S["q%d" % (4 + j % 2)], bq[4 + j % 2] + 16 * (j // 2 + 1))
                        if j >= 1:
                            t.wait_ge(S["ac2"], b_ac2 + j)
                        for nn in range(2):
                            mm = t.matmul(psh[j % 2][:, nn * 512:(nn + 1) * 512], h64[:],
                                          wcb[j % 2][:, nn * 512:(nn + 1) * 512],
                                          start=True, stop=True)
                            if nn == 1:
                                mm.then_inc(S["pe2"], 1)
                    phase_end(t)
                cnt.set("pe1", b_pe1 + 4)
                cnt.set("pe2", b_pe2 + 256)

                @blk.scalar
                def _(sc):
                    phase_wait(sc)
                    for kk in range(4):
                        sc.wait_ge(S["pe1"], b_pe1 + kk + 1)
                        sc.copy(bcdp[:, kk * 512:(kk + 1) * 512], ps_bc[:]).then_inc(S["ac1"], 1)
                    for j in range(256):
                        sc.wait_ge(S["pe2"], b_pe2 + j + 1)
                        if j >= 2:
                            sc.wait_ge(S["q%d" % (6 + j % 2)], bq[6 + j % 2] + 16 * (j // 2))
                        sc.copy(wob[j % 2][:], psh[j % 2][:]).then_inc(S["ac2"], 1)
                    phase_end(sc)
                cnt.set("ac1", b_ac1 + 4)
                cnt.set("ac2", b_ac2 + 256)

                @blk.gpsimd
                def _(g):
                    phase_wait(g)
                    phase_end(g)
                _blkcm.__exit__(None, None, None)
            close_phase()

        def phase_A():
            (b_xl, b_st, b_gw, b_gst, b_wl, b_pe1, b_pe2,
             b_ve1, b_ve2, b_ve3, b_ac1, b_ac2) = bases(
                "xl", "st", "gw", "gst", "wl", "pe1", "pe2",
                "ve1", "ve2", "ve3", "ac1", "ac2")
            bq = [cnt.get("q%d" % _i) for _i in range(16)]
            with contextlib.ExitStack() as _es:
                xt0 = _es.enter_context(nc.sbuf_tensor("xt0", [128, D], F32))
                xt1 = _es.enter_context(nc.sbuf_tensor("xt1", [128, D], F32))
                sqt = _es.enter_context(nc.sbuf_tensor("sqt", [128, D], F32))
                xn0 = _es.enter_context(nc.sbuf_tensor("xn0", [128, D], BF16))
                xn1 = _es.enter_context(nc.sbuf_tensor("xn1", [128, D], BF16))
                xnfm = _es.enter_context(nc.sbuf_tensor("xnfm", [128, 16 * 2048], BF16))
                wqb0 = _es.enter_context(nc.sbuf_tensor("wqb0", [128, 2048], BF16))
                wqb1 = _es.enter_context(nc.sbuf_tensor("wqb1", [128, 2048], BF16))
                qkt0 = _es.enter_context(nc.sbuf_tensor("qkt0", [128, 512], BF16))
                qkt1 = _es.enter_context(nc.sbuf_tensor("qkt1", [128, 512], BF16))
                qkt2 = _es.enter_context(nc.sbuf_tensor("qkt2", [128, 512], BF16))
                qkt3 = _es.enter_context(nc.sbuf_tensor("qkt3", [128, 512], BF16))
                st1a = _es.enter_context(nc.sbuf_tensor("st1a", [128, 1], F32))
                st1b = _es.enter_context(nc.sbuf_tensor("st1b", [128, 1], F32))
                st2a = _es.enter_context(nc.sbuf_tensor("st2a", [128, 1], F32))
                st2b = _es.enter_context(nc.sbuf_tensor("st2b", [128, 1], F32))
                mva = _es.enter_context(nc.sbuf_tensor("mva", [128, 4], F32))
                mvb = _es.enter_context(nc.sbuf_tensor("mvb", [128, 4], F32))
                wpc0 = _es.enter_context(nc.sbuf_tensor("wpc0", [128, 2048], BF16))
                wpc1 = _es.enter_context(nc.sbuf_tensor("wpc1", [128, 2048], BF16))
                pq0 = _es.enter_context(nc.psum_tensor("pq0", [128, 512], F32))
                pq1 = _es.enter_context(nc.psum_tensor("pq1", [128, 512], F32))
                pt0 = _es.enter_context(nc.psum_tensor("pt0", [128, 128], BF16))
                pt1 = _es.enter_context(nc.psum_tensor("pt1", [128, 128], BF16))
                blk = _es.enter_context(nc.Block("phA"))
                xt = [xt0, xt1]
                xn = [xn0, xn1]
                wqb = [wqb0, wqb1]
                qkt = [qkt0, qkt1, qkt2, qkt3]
                stat1 = [st1a, st1b]
                stat2 = [st2a, st2b]
                mv = [mva, mvb]
                pq = [pq0, pq1]
                pt = [pt0, pt1]
                wpc = [wpc0, wpc1]

                NT = 16
                CT = (48, 32)
                E0 = (0, 192)
                NE = 320

                def qdst(ctg, tc):
                    tb = 0 if ctg < 48 else 1
                    ct = ctg if ctg < 48 else ctg - 48
                    if ct < 16:
                        return q_sc.ap()[ct * 128:(ct + 1) * 128,
                                         tb * 2048 + tc * 512: tb * 2048 + (tc + 1) * 512]
                    if ct < 32:
                        return k_sc.ap()[(ct - 16) * 128:(ct - 15) * 128,
                                         tb * 2048 + tc * 512: tb * 2048 + (tc + 1) * 512]
                    return v_sc.ap()[(ct - 32) * 128:(ct - 31) * 128,
                                     tc * 512:(tc + 1) * 512]

                @blk.sync
                def _(s):
                    phase_wait(s)
                    for tb in range(2):
                        for tt in range(NT):
                            u = tb * NT + tt
                            if u >= 2:
                                s.wait_ge(S["ve1"], b_ve1 + 8 * (u - 1))
                            s.dma_start(xt[u % 2][:], x_in.ap()[u * 128:(u + 1) * 128, :]
                                        ).then_inc(S["q%d" % (u % 2)], 16)
                        for ei in range(CT[tb] * 4):
                            e = E0[tb] + ei
                            s.wait_ge(S["ve2"], b_ve2 + e + 1)
                            ctg, tc = divmod(e, 4)
                            s.dma_start(qdst(ctg, tc), qkt[e % 4][:]).then_inc(S["q%d" % (4 + e % 4)], 16)
                    for _k in range(4):
                        s.wait_ge(S["q%d" % (4 + _k)], bq[4 + _k] + 80 * 16)
                    phase_end(s)
                cnt.set("q0", bq[0] + 16 * 16)
                cnt.set("q1", bq[1] + 16 * 16)
                for _k in range(4):
                    cnt.set("q%d" % (4 + _k), bq[4 + _k] + 80 * 16)

                @blk.gpsimd
                def _(g):
                    phase_wait(g)
                    pj = 0
                    for ctg in range(80):
                        if ctg >= 2:
                            g.wait_ge(S["pe2"], b_pe2 + 4 * (ctg - 1))
                        g.dma_start(wqb[ctg % 2][:],
                                    wqkv_in.ap()[ctg if ctg < 48 else ctg - 48, :, :, :]
                                    .rearrange("p a b -> p (a b)")).then_inc(S["q%d" % (2 + ctg % 2)], 16)
                        for _ in range(2):
                            if pj < 128:
                                win = wup_in if pj < 64 else wgate_in
                                wsc = wup_sc if pj < 64 else wgate_sc
                                c = pj % 64
                                if pj >= 2:
                                    g.wait_ge(S["gst"], b_gst + 16 * (pj - 1))
                                g.dma_start(wpc[pj % 2][:],
                                            win.ap()[c, :, :, :].rearrange("p a b -> p (a b)")
                                            ).then_inc(S["q%d" % (8 + pj % 2)], 16)
                                if pj >= 1:
                                    g.wait_ge(S["q%d" % (8 + (pj - 1) % 2)],
                                              bq[8 + (pj - 1) % 2] + 16 * ((pj - 1) // 2 + 1))
                                    wino = wup_sc if pj - 1 < 64 else wgate_sc
                                    g.dma_start(wino.ap()[(pj - 1) % 64, :, :],
                                                wpc[(pj - 1) % 2][:]).then_inc(S["gst"], 16)
                                pj += 1
                    g.wait_ge(S["q%d" % (8 + 127 % 2)], bq[8 + 127 % 2] + 16 * (127 // 2 + 1))
                    g.dma_start(wgate_sc.ap()[63, :, :], wpc[127 % 2][:]).then_inc(S["gst"], 16)
                    g.wait_ge(S["gst"], b_gst + 128 * 16)
                    phase_end(g)
                cnt.set("q2", bq[2] + 40 * 16)
                cnt.set("q3", bq[3] + 40 * 16)
                cnt.set("q8", bq[8] + 64 * 16)
                cnt.set("q9", bq[9] + 64 * 16)
                cnt.set("gst", b_gst + 128 * 16)

                def ln_chain(v, u):
                    par = u % 2
                    v.wait_ge(S["q%d" % (u % 2)], bq[u % 2] + 16 * (u // 2 + 1))
                    v.reduce_sum(stat1[par][:], xt[par][:], axis=AX.X).then_inc(S["ve1"], 1)
                    v.wait_ge(S["ac1"], b_ac1 + 2 * u + 1)
                    v.wait_ge(S["ve1"], b_ve1 + 8 * u + 1)
                    m = mv[par][:, 0:1]
                    var = mv[par][:, 1:2]
                    rstd = mv[par][:, 2:3]
                    negmr = mv[par][:, 3:4]
                    v.tensor_scalar(m, stat1[par][:], 1.0 / D, None, ALU.mult).then_inc(S["ve1"], 1)
                    v.tensor_scalar(var, stat2[par][:], 1.0 / D, None, ALU.mult).then_inc(S["ve1"], 1)
                    v.wait_ge(S["ve1"], b_ve1 + 8 * u + 3)
                    v.tensor_tensor(rstd, m, m, ALU.mult).then_inc(S["ve1"], 1)
                    v.wait_ge(S["ve1"], b_ve1 + 8 * u + 4)
                    v.tensor_tensor(var, var, rstd, ALU.subtract).then_inc(S["ve1"], 1)
                    v.wait_ge(S["ac1"], b_ac1 + 2 * u + 2)
                    v.reciprocal(rstd, rstd).then_inc(S["ve1"], 1)
                    v.wait_ge(S["ve1"], b_ve1 + 8 * u + 6)
                    v.scalar_tensor_tensor(negmr, m, -1.0, rstd, ALU.mult, ALU.mult
                                           ).then_inc(S["ve1"], 1)
                    v.wait_ge(S["ve1"], b_ve1 + 8 * u + 7)
                    if u >= 2:
                        v.wait_ge(S["pe1"], b_pe1 + 16 * (u - 1))
                    v.tensor_scalar(xn[par][:], xt[par][:], m, rstd,
                                    ALU.subtract, ALU.mult).then_inc(S["ve1"], 1)

                @blk.vector
                def _(v):
                    phase_wait(v)
                    for tb in range(2):
                        ln_chain(v, tb * NT)
                        for tt in range(NT):
                            u = tb * NT + tt
                            if tt + 1 < NT:
                                ln_chain(v, u + 1)
                            for k in range(16):
                                g_idx = 16 * u + k
                                v.wait_ge(S["pe1"], b_pe1 + g_idx + 1)
                                v.tensor_copy(
                                    xnfm[:, k * 2048 + tt * 128: k * 2048 + (tt + 1) * 128],
                                    pt[k % 2][:])
                                v.memset(flt[:], 0.0).then_inc(S["ve3"], 1)
                        for ei in range(CT[tb] * 4):
                            e = E0[tb] + ei
                            v.wait_ge(S["pe2"], b_pe2 + e + 1)
                            if e >= 4:
                                v.wait_ge(S["q%d" % (4 + e % 4)], bq[4 + e % 4] + 16 * (e // 4))
                            v.tensor_copy(qkt[e % 4][:], pq[e % 2][:])
                            v.memset(flt[:], 0.0).then_inc(S["ve2"], 1)
                    phase_end(v)
                cnt.set("ve1", b_ve1 + 8 * 32)
                cnt.set("ve2", b_ve2 + NE)
                cnt.set("ve3", b_ve3 + 512)

                @blk.scalar
                def _(sc):
                    phase_wait(sc)
                    sc.wait_ge(S["ld"], cnt.get("ld"))
                    for tb in range(2):
                        for tt in range(NT):
                            u = tb * NT + tt
                            par = u % 2
                            sc.wait_ge(S["q%d" % (u % 2)], bq[u % 2] + 16 * (u // 2 + 1))
                            sc.activation(sqt[:], xt[par][:], AF.Square,
                                          accum_out=stat2[par][:]).then_inc(S["ac1"], 1)
                            sc.wait_ge(S["ve1"], b_ve1 + 8 * u + 5)
                            sc.activation(mv[par][:, 2:3], mv[par][:, 1:2], AF.Sqrt,
                                          bias=cst[:, CI_EPS:CI_EPS + 1]).then_inc(S["ac1"], 1)
                    phase_end(sc)
                cnt.set("ac1", b_ac1 + 64)

                @blk.tensor
                def _(t):
                    phase_wait(t)
                    t.wait_ge(S["ld"], cnt.get("ld"))
                    for tb in range(2):
                        for tt in range(NT):
                            u = tb * NT + tt
                            t.wait_ge(S["ve1"], b_ve1 + 8 * u + 8)
                            for k in range(16):
                                g_idx = 16 * u + k
                                if g_idx >= 2:
                                    t.wait_ge(S["ve3"], b_ve3 + g_idx - 1)
                                t.transpose(pt[k % 2][:], xn[u % 2][:, k * 128:(k + 1) * 128],
                                            idn[:]).then_inc(S["pe1"], 1)
                        t.wait_ge(S["ve3"], b_ve3 + 256 * (tb + 1))
                        for ei in range(CT[tb] * 4):
                            e = E0[tb] + ei
                            ctg, tc = divmod(e, 4)
                            t.wait_ge(S["q%d" % (2 + ctg % 2)], bq[2 + ctg % 2] + 16 * (ctg // 2 + 1))
                            if e >= 2:
                                t.wait_ge(S["ve2"], b_ve2 + e - 1)
                            for fc in range(16):
                                t.matmul(pq[e % 2][:],
                                         wqb[ctg % 2][:, fc * 128:(fc + 1) * 128],
                                         xnfm[:, fc * 2048 + tc * 512: fc * 2048 + (tc + 1) * 512],
                                         start=(fc == 0), stop=(fc == 15)
                                         ).then_inc(S["pe2"], 1) if fc == 15 else \
                                    t.matmul(pq[e % 2][:],
                                             wqb[ctg % 2][:, fc * 128:(fc + 1) * 128],
                                             xnfm[:, fc * 2048 + tc * 512: fc * 2048 + (tc + 1) * 512],
                                             start=(fc == 0), stop=False)
                    phase_end(t)
                cnt.set("pe1", b_pe1 + 512)
                cnt.set("pe2", b_pe2 + NE)
            close_phase()

        def phase_BC():
            (b_xl, b_gw, b_pe1, b_pe2, b_pe3,
             b_ve1, b_ve2, b_ac1, b_ac2) = bases(
                "xl", "gw", "pe1", "pe2", "pe3", "ve1", "ve2", "ac1", "ac2")
            with contextlib.ExitStack() as _es:
                vals = _es.enter_context(nc.sbuf_tensor("vals", [128, NH * 2048], BF16))
                wo_sb = _es.enter_context(nc.sbuf_tensor("wo_sb", [128, 16 * 2048], BF16))
                csr = _es.enter_context(nc.sbuf_tensor("csr", [1, TOK], F32))
                phase_B_inner(vals, wo_sb, csr,
                              b_xl, b_gw, b_pe1, b_pe2, b_pe3,
                              b_ve1, b_ve2, b_ac1, b_ac2)
                close_phase()
                phase_C_inner(vals, wo_sb)
                close_phase()

        def phase_B_inner(vals, wo_sb, csr,
                          b_xl, b_gw, b_pe1, b_pe2, b_pe3,
                          b_ve1, b_ve2, b_ac1, b_ac2):
            bq = [cnt.get("q%d" % _i) for _i in range(16)]
            with contextlib.ExitStack() as _es:
                qh0 = _es.enter_context(nc.sbuf_tensor("qh0", [128, TFULL], BF16))
                qh1 = _es.enter_context(nc.sbuf_tensor("qh1", [128, TFULL], BF16))
                kh0 = _es.enter_context(nc.sbuf_tensor("kh0", [128, TFULL], BF16))
                kh1 = _es.enter_context(nc.sbuf_tensor("kh1", [128, TFULL], BF16))
                vh0 = _es.enter_context(nc.sbuf_tensor("vh0", [128, TOK], BF16))
                vh1 = _es.enter_context(nc.sbuf_tensor("vh1", [128, TOK], BF16))
                un0 = _es.enter_context(nc.sbuf_tensor("un0", [128, TOK], BF16))
                un1 = _es.enter_context(nc.sbuf_tensor("un1", [128, TOK], BF16))
                utmp = _es.enter_context(nc.sbuf_tensor("utmp", [128, TOK], BF16))
                lo = _es.enter_context(nc.sbuf_tensor("lo", [128, 8], F32))
                lw = _es.enter_context(nc.sbuf_tensor("lw", [128, 8], F32))
                lwb = _es.enter_context(nc.sbuf_tensor("lwb", [128, 2], BF16))
                psA0 = _es.enter_context(nc.psum_tensor("psA0", [128, 1024], F32))
                psA1 = _es.enter_context(nc.psum_tensor("psA1", [128, 1024], F32))
                csps = _es.enter_context(nc.psum_tensor("csps", [1, 2048], F32))
                blk = _es.enter_context(nc.Block("phB"))
                qh = [qh0, qh1]
                kh = [kh0, kh1]
                vh = [vh0, vh1]
                un = [un0, un1]
                psA = [psA0, psA1]

                @blk.sync
                def _(s):
                    phase_wait(s)
                    for h in range(NH):
                        if h >= 2:
                            s.wait_ge(S["pe2"], b_pe2 + 128 * (h - 1))
                            s.wait_ge(S["ve2"], b_ve2 + 2 * (h - 1))
                        s.dma_start(qh[h % 2][:], q_sc.ap()[h * 128:(h + 1) * 128, :]
                                    ).then_inc(S["q%d" % (h % 2)], 16)
                        s.dma_start(kh[h % 2][:], k_sc.ap()[h * 128:(h + 1) * 128, :]
                                    ).then_inc(S["q%d" % (h % 2)], 16)
                        s.dma_start(vh[h % 2][:], v_sc.ap()[h * 128:(h + 1) * 128, :]
                                    ).then_inc(S["q%d" % (h % 2)], 16)
                    phase_end(s)
                cnt.set("q0", bq[0] + 8 * 48)
                cnt.set("q1", bq[1] + 8 * 48)

                @blk.gpsimd
                def _(g):
                    phase_wait(g)
                    for hc in range(16):
                        g.dma_start(wo_sb[:, hc * 2048:(hc + 1) * 2048],
                                    wo_in.ap()[hc * 128:(hc + 1) * 128, :]).then_inc(S["gw"], 16)
                    phase_end(g)
                cnt.set("gw", b_gw + 256)

                @blk.tensor
                def _(t):
                    phase_wait(t)
                    for h in range(NH):
                        for qt in range(32):
                            j = 32 * h + qt
                            if qt == 0:
                                t.wait_ge(S["q%d" % (h % 2)], bq[h % 2] + 48 * (h // 2 + 1))
                                if h >= 1:
                                    t.wait_ge(S["ve2"], b_ve2 + 2 * h)
                                    t.wait_ge(S["ac2"], b_ac2 + h)
                            if j >= 1:
                                t.wait_ge(S["ac1"], b_ac1 + 4 * (j - 1) + 3)
                            for n in range(2):
                                t.matmul(psA0[:, n * 512:(n + 1) * 512],
                                         qh[h % 2][:, qt * 128:(qt + 1) * 128],
                                         kh[h % 2][:, n * 512:(n + 1) * 512],
                                         start=True, stop=True
                                         ).then_inc(S["pe2"], 1) if n == 1 else \
                                    t.matmul(psA0[:, n * 512:(n + 1) * 512],
                                             qh[h % 2][:, qt * 128:(qt + 1) * 128],
                                             kh[h % 2][:, n * 512:(n + 1) * 512],
                                             start=True, stop=True)
                            if j >= 1:
                                t.wait_ge(S["ac1"], b_ac1 + 4 * (j - 1) + 4)
                            for n in range(2):
                                t.matmul(psA1[:, n * 512:(n + 1) * 512],
                                         qh[h % 2][:, qt * 128:(qt + 1) * 128],
                                         kh[h % 2][:, 1024 + n * 512: 1024 + (n + 1) * 512],
                                         start=True, stop=True
                                         ).then_inc(S["pe2"], 1) if n == 1 else \
                                    t.matmul(psA1[:, n * 512:(n + 1) * 512],
                                             qh[h % 2][:, qt * 128:(qt + 1) * 128],
                                             kh[h % 2][:, 1024 + n * 512: 1024 + (n + 1) * 512],
                                             start=True, stop=True)
                            if qt >= 1:
                                jj = j - 1
                                t.wait_ge(S["ve1"], b_ve1 + jj + 1)
                                for kc in range(4):
                                    t.matmul(csps[:, kc * 512:(kc + 1) * 512],
                                             lwb[:, jj % 2: jj % 2 + 1],
                                             un[jj % 2][:, kc * 512:(kc + 1) * 512],
                                             start=(jj % 32 == 0), stop=(jj % 32 == 31)
                                             ).then_inc(S["pe3"], 1) if kc == 3 else \
                                        t.matmul(csps[:, kc * 512:(kc + 1) * 512],
                                                 lwb[:, jj % 2: jj % 2 + 1],
                                                 un[jj % 2][:, kc * 512:(kc + 1) * 512],
                                                 start=(jj % 32 == 0), stop=(jj % 32 == 31))
                            t.wait_ge(S["ac1"], b_ac1 + 4 * j + 1)
                            for n in range(2):
                                t.matmul(psA0[:, n * 512:(n + 1) * 512],
                                         qh[h % 2][:, qt * 128:(qt + 1) * 128],
                                         kh[h % 2][:, 2048 + n * 512: 2048 + (n + 1) * 512],
                                         start=True, stop=True
                                         ).then_inc(S["pe2"], 1) if n == 1 else \
                                    t.matmul(psA0[:, n * 512:(n + 1) * 512],
                                             qh[h % 2][:, qt * 128:(qt + 1) * 128],
                                             kh[h % 2][:, 2048 + n * 512: 2048 + (n + 1) * 512],
                                             start=True, stop=True)
                            t.wait_ge(S["ac1"], b_ac1 + 4 * j + 2)
                            for n in range(2):
                                t.matmul(psA1[:, n * 512:(n + 1) * 512],
                                         qh[h % 2][:, qt * 128:(qt + 1) * 128],
                                         kh[h % 2][:, 3072 + n * 512: 3072 + (n + 1) * 512],
                                         start=True, stop=True
                                         ).then_inc(S["pe2"], 1) if n == 1 else \
                                    t.matmul(psA1[:, n * 512:(n + 1) * 512],
                                             qh[h % 2][:, qt * 128:(qt + 1) * 128],
                                             kh[h % 2][:, 3072 + n * 512: 3072 + (n + 1) * 512],
                                             start=True, stop=True)
                        jj = 32 * h + 31
                        t.wait_ge(S["ve1"], b_ve1 + jj + 1)
                        for kc in range(4):
                            t.matmul(csps[:, kc * 512:(kc + 1) * 512],
                                     lwb[:, jj % 2: jj % 2 + 1],
                                     un[jj % 2][:, kc * 512:(kc + 1) * 512],
                                     start=(jj % 32 == 0), stop=(jj % 32 == 31)
                                     ).then_inc(S["pe3"], 1) if kc == 3 else \
                                t.matmul(csps[:, kc * 512:(kc + 1) * 512],
                                         lwb[:, jj % 2: jj % 2 + 1],
                                         un[jj % 2][:, kc * 512:(kc + 1) * 512],
                                         start=(jj % 32 == 0), stop=(jj % 32 == 31))
                        t.wait_ge(S["ac2"], b_ac2 + h + 1)
                        for kk in range(4):
                            dstp = psA[kk // 2][:, (kk % 2) * 512:(kk % 2 + 1) * 512]
                            t.matmul(dstp, onesf[:], csr[:, kk * 512:(kk + 1) * 512],
                                     start=True, stop=True).then_inc(S["pe1"], 1)
                    phase_end(t)
                cnt.set("pe1", b_pe1 + NH * 4)
                cnt.set("pe2", b_pe2 + 512 * 4)
                cnt.set("pe3", b_pe3 + 512)

                @blk.scalar
                def _(sc):
                    phase_wait(sc)
                    sc_s = cst[:, CI_S128:CI_S128 + 1]
                    for h in range(NH):
                        for qt in range(32):
                            j = 32 * h + qt
                            par = j % 2
                            sc.wait_ge(S["pe2"], b_pe2 + 4 * j + 1)
                            if j >= 2:
                                sc.wait_ge(S["pe3"], b_pe3 + j - 1)
                            sc.activation(un[par][:, 0:1024], psA0[:], AF.Exp, scale=sc_s,
                                          accum_out=lo[:, 4 * par + 0: 4 * par + 1]
                                          ).then_inc(S["ac1"], 1)
                            sc.wait_ge(S["pe2"], b_pe2 + 4 * j + 2)
                            sc.activation(un[par][:, 1024:2048], psA1[:], AF.Exp, scale=sc_s,
                                          accum_out=lo[:, 4 * par + 1: 4 * par + 2]
                                          ).then_inc(S["ac1"], 1)
                            sc.wait_ge(S["pe2"], b_pe2 + 4 * j + 3)
                            sc.activation(utmp[:, 0:1024], psA0[:], AF.Exp, scale=sc_s,
                                          accum_out=lo[:, 4 * par + 2: 4 * par + 3]
                                          ).then_inc(S["ac1"], 1)
                            sc.wait_ge(S["pe2"], b_pe2 + 4 * j + 4)
                            sc.activation(utmp[:, 1024:2048], psA1[:], AF.Exp, scale=sc_s,
                                          accum_out=lo[:, 4 * par + 3: 4 * par + 4]
                                          ).then_inc(S["ac1"], 1)
                        sc.wait_ge(S["pe3"], b_pe3 + 32 * (h + 1))
                        if h >= 1:
                            sc.wait_ge(S["pe1"], b_pe1 + 4 * (h - 1) + 4)
                        sc.copy(csr[:], csps[:])
                        sc.copy(flt[:], flt[:]).then_inc(S["ac2"], 1)
                    phase_end(sc)
                cnt.set("ac1", b_ac1 + 512 * 4)
                cnt.set("ac2", b_ac2 + NH)

                @blk.vector
                def _(v):
                    phase_wait(v)
                    vs = cnt.get("ve3")
                    for h in range(NH):
                        for qt in range(32):
                            j = 32 * h + qt
                            par = j % 2
                            v.wait_ge(S["ac1"], b_ac1 + 4 * j + 4)
                            a = lw[:, 4 * par + 0:4 * par + 1]
                            bb2 = lw[:, 4 * par + 1:4 * par + 2]
                            l = lw[:, 4 * par + 2:4 * par + 3]
                            il = lw[:, 4 * par + 3:4 * par + 4]
                            v.tensor_tensor(a, lo[:, 4 * par:4 * par + 1],
                                            lo[:, 4 * par + 1:4 * par + 2], ALU.add)
                            v.tensor_tensor(bb2, lo[:, 4 * par + 2:4 * par + 3],
                                            lo[:, 4 * par + 3:4 * par + 4], ALU.add
                                            ).then_inc(S["ve3"], 1)
                            vs += 1
                            v.wait_ge(S["ve3"], vs)
                            v.tensor_tensor(l, a, bb2, ALU.add).then_inc(S["ve3"], 1)
                            vs += 1
                            v.wait_ge(S["ve3"], vs)
                            v.reciprocal(il, l).then_inc(S["ve3"], 1)
                            vs += 1
                            v.wait_ge(S["ve3"], vs)
                            v.tensor_copy(lwb[:, par:par + 1], il)
                            v.memset(flt[:], 0.0).then_inc(S["ve1"], 1)
                        v.wait_ge(S["q%d" % (h % 2)], bq[h % 2] + 48 * (h // 2 + 1))
                        v.wait_ge(S["pe1"], b_pe1 + 4 * h + 2)
                        v.tensor_tensor(vals[:, h * 2048:h * 2048 + 1024],
                                        vh[h % 2][:, 0:1024], psA0[:], ALU.mult)
                        v.memset(flt[:], 0.0).then_inc(S["ve2"], 1)
                        v.wait_ge(S["pe1"], b_pe1 + 4 * h + 4)
                        v.tensor_tensor(vals[:, h * 2048 + 1024:(h + 1) * 2048],
                                        vh[h % 2][:, 1024:2048], psA1[:], ALU.mult)
                        v.memset(flt[:], 0.0).then_inc(S["ve2"], 1)
                    phase_end(v)
                cnt.set("ve1", b_ve1 + 512)
                cnt.set("ve2", b_ve2 + 2 * NH)
                cnt.set("ve3", cnt.get("ve3") + 512 * 3)

        def phase_C_inner(vals, wo_sb):
            (c_xl, c_st, c_gw, c_pe2, c_ve5) = bases("xl", "st", "gw", "pe2", "ve5")
            bq = [cnt.get("q%d" % _i) for _i in range(16)]
            with contextlib.ExitStack() as _es:
                xtc0 = _es.enter_context(nc.sbuf_tensor("xtc0", [128, D], F32))
                xtc1 = _es.enter_context(nc.sbuf_tensor("xtc1", [128, D], F32))
                x1sb0 = _es.enter_context(nc.sbuf_tensor("x1sb0", [128, D], F32))
                x1sb1 = _es.enter_context(nc.sbuf_tensor("x1sb1", [128, D], F32))
                po0 = _es.enter_context(nc.psum_tensor("po0", [128, 512], F32))
                po1 = _es.enter_context(nc.psum_tensor("po1", [128, 512], F32))
                po2 = _es.enter_context(nc.psum_tensor("po2", [128, 512], F32))
                po3 = _es.enter_context(nc.psum_tensor("po3", [128, 512], F32))
                blk = _es.enter_context(nc.Block("phC"))
                xtc = [xtc0, xtc1]
                x1sb = [x1sb0, x1sb1]
                po = [po0, po1, po2, po3]

                @blk.sync
                def _(s):
                    phase_wait(s)
                    for tt in range(16):
                        if tt >= 2:
                            s.wait_ge(S["ve5"], c_ve5 + 4 * (tt - 1))
                        s.dma_start(xtc[tt % 2][:], x_in.ap()[tt * 128:(tt + 1) * 128, :]
                                    ).then_inc(S["q%d" % (tt % 2)], 16)
                        if tt >= 1:
                            s.wait_ge(S["ve5"], c_ve5 + 4 * tt)
                            s.dma_start(x1_sc.ap()[(tt - 1) * 128:tt * 128, :],
                                        x1sb[(tt - 1) % 2][:]).then_inc(S["q%d" % (2 + (tt - 1) % 2)], 16)
                    s.wait_ge(S["ve5"], c_ve5 + 64)
                    s.dma_start(x1_sc.ap()[15 * 128:, :], x1sb[15 % 2][:]).then_inc(S["q3"], 16)
                    s.wait_ge(S["q2"], bq[2] + 8 * 16)
                    s.wait_ge(S["q3"], bq[3] + 8 * 16)
                    phase_end(s)
                cnt.set("q0", bq[0] + 8 * 16)
                cnt.set("q1", bq[1] + 8 * 16)
                cnt.set("q2", bq[2] + 8 * 16)
                cnt.set("q3", bq[3] + 8 * 16)

                @blk.tensor
                def _(t):
                    phase_wait(t)
                    t.wait_ge(S["gw"], c_gw)
                    for tt in range(16):
                        for od in range(4):
                            if tt >= 1:
                                t.wait_ge(S["ve5"], c_ve5 + 4 * (tt - 1) + od + 1)
                            for hc in range(16):
                                mm = t.matmul(po[od][:],
                                              vals[:, hc * 2048 + tt * 128: hc * 2048 + (tt + 1) * 128],
                                              wo_sb[:, hc * 2048 + od * 512: hc * 2048 + (od + 1) * 512],
                                              start=(hc == 0), stop=(hc == 15))
                                if hc == 15:
                                    mm.then_inc(S["pe2"], 1)
                    phase_end(t)
                cnt.set("pe2", c_pe2 + 64)

                @blk.vector
                def _(v):
                    phase_wait(v)
                    for tt in range(16):
                        v.wait_ge(S["q%d" % (tt % 2)], bq[tt % 2] + 16 * (tt // 2 + 1))
                        if tt >= 2:
                            v.wait_ge(S["q%d" % (2 + tt % 2)], bq[2 + tt % 2] + 16 * (tt // 2))
                        for od in range(4):
                            v.wait_ge(S["pe2"], c_pe2 + 4 * tt + od + 1)
                            v.tensor_add(x1sb[tt % 2][:, od * 512:(od + 1) * 512],
                                         po[od][:],
                                         xtc[tt % 2][:, od * 512:(od + 1) * 512])
                            v.memset(flt[:], 0.0).then_inc(S["ve5"], 1)
                    phase_end(v)
                cnt.set("ve5", c_ve5 + 64)

                @blk.scalar
                def _(sc):
                    phase_wait(sc)
                    phase_end(sc)

                @blk.gpsimd
                def _(g):
                    phase_wait(g)
                    phase_end(g)

        def phase_D():
            (d_xl, d_st, d_wl, d_pe1, d_pe2, d_pe3, d_pe4,
             d_ve1, d_ve2, d_ve3, d_ve4, d_ve5, d_ac1, d_ac2, d_ac3) = bases(
                "xl", "st", "wl", "pe1", "pe2", "pe3", "pe4",
                "ve1", "ve2", "ve3", "ve4", "ve5", "ac1", "ac2", "ac3")
            bq = [cnt.get("q%d" % _i) for _i in range(16)]
            with contextlib.ExitStack() as _es:
                x1t0 = _es.enter_context(nc.sbuf_tensor("x1t0", [128, D], F32))
                x1t1 = _es.enter_context(nc.sbuf_tensor("x1t1", [128, D], F32))
                sqt2 = _es.enter_context(nc.sbuf_tensor("sqt2", [128, D], F32))
                xn20 = _es.enter_context(nc.sbuf_tensor("xn20", [128, D], F32))
                xn21 = _es.enter_context(nc.sbuf_tensor("xn21", [128, D], F32))
                yfm = _es.enter_context(nc.sbuf_tensor("yfm", [128, 16 * 512], BF16))
                tfm = _es.enter_context(nc.sbuf_tensor("tfm", [128, 16 * 512], BF16))
                g2 = _es.enter_context(nc.sbuf_tensor("g2", [128, 64 * 512], BF16))
                hb0 = _es.enter_context(nc.sbuf_tensor("hb0", [128, 2048], BF16))
                hb1 = _es.enter_context(nc.sbuf_tensor("hb1", [128, 2048], BF16))
                wub0 = _es.enter_context(nc.sbuf_tensor("wub0", [128, 2048], BF16))
                wub1 = _es.enter_context(nc.sbuf_tensor("wub1", [128, 2048], BF16))
                wgb0 = _es.enter_context(nc.sbuf_tensor("wgb0", [128, 2048], BF16))
                wgb1 = _es.enter_context(nc.sbuf_tensor("wgb1", [128, 2048], BF16))
                w2b0 = _es.enter_context(nc.sbuf_tensor("w2b0", [128, 512], BF16))
                w2b1 = _es.enter_context(nc.sbuf_tensor("w2b1", [128, 512], BF16))
                w2b2 = _es.enter_context(nc.sbuf_tensor("w2b2", [128, 512], BF16))
                w2b3 = _es.enter_context(nc.sbuf_tensor("w2b3", [128, 512], BF16))
                xup0 = _es.enter_context(nc.sbuf_tensor("xup0", [128, 512], F32))
                xup1 = _es.enter_context(nc.sbuf_tensor("xup1", [128, 512], F32))
                sg0 = _es.enter_context(nc.sbuf_tensor("sg0", [128, 512], F32))
                sg1 = _es.enter_context(nc.sbuf_tensor("sg1", [128, 512], F32))
                gc0 = _es.enter_context(nc.sbuf_tensor("gc0", [128, 512], BF16))
                gc1 = _es.enter_context(nc.sbuf_tensor("gc1", [128, 512], BF16))
                xob0 = _es.enter_context(nc.sbuf_tensor("xob0", [128, 512], F32))
                xob1 = _es.enter_context(nc.sbuf_tensor("xob1", [128, 512], F32))
                osb0 = _es.enter_context(nc.sbuf_tensor("osb0", [128, 512], F32))
                osb1 = _es.enter_context(nc.sbuf_tensor("osb1", [128, 512], F32))
                st1c = _es.enter_context(nc.sbuf_tensor("st1c", [128, 1], F32))
                st1d = _es.enter_context(nc.sbuf_tensor("st1d", [128, 1], F32))
                st2c = _es.enter_context(nc.sbuf_tensor("st2c", [128, 1], F32))
                st2d = _es.enter_context(nc.sbuf_tensor("st2d", [128, 1], F32))
                mvc = _es.enter_context(nc.sbuf_tensor("mvc", [128, 4], F32))
                mvd = _es.enter_context(nc.sbuf_tensor("mvd", [128, 4], F32))
                pf0 = _es.enter_context(nc.psum_tensor("pf0", [128, 512], F32))
                pf1 = _es.enter_context(nc.psum_tensor("pf1", [128, 512], F32))
                ph = _es.enter_context(nc.psum_tensor("ph", [128, 512], F32))
                pd0 = _es.enter_context(nc.psum_tensor("pd0", [128, 512], F32))
                pd1 = _es.enter_context(nc.psum_tensor("pd1", [128, 512], F32))
                pd2 = _es.enter_context(nc.psum_tensor("pd2", [128, 512], F32))
                pd3 = _es.enter_context(nc.psum_tensor("pd3", [128, 512], F32))
                blk = _es.enter_context(nc.Block("phD"))
                x1t = [x1t0, x1t1]
                xn2 = [xn20, xn21]
                hb = [hb0, hb1]
                wub = [wub0, wub1]
                wgb = [wgb0, wgb1]
                w2b = [w2b0, w2b1, w2b2, w2b3]
                xup = [xup0, xup1]
                sg = [sg0, sg1]
                gc = [gc0, gc1]
                xob = [xob0, xob1]
                osb = [osb0, osb1]
                stat1 = [st1c, st1d]
                stat2 = [st2c, st2d]
                mv = [mvc, mvd]
                pf = [pf0, pf1]
                pd = [pd0, pd1, pd2, pd3]

                WL_CH = 16 * 16 + 64 * 32 + 256 * 16

                def wl_h(ch, fo):
                    return d_wl + ch * WL_CH + 16 * (fo + 1)

                def wl_ug(ch, c):
                    return d_wl + ch * WL_CH + 256 + 32 * (c + 1)

                def wl_w2(ch, od, c):
                    return d_wl + ch * WL_CH + 256 + 2048 + 16 * (od * 64 + c + 1)

                XL_CH = 20 * 16

                def xl_ln(ch, tt):
                    return d_xl + ch * XL_CH + 16 * (tt + 1)

                def xl_part(ch, k):
                    return d_xl + ch * XL_CH + 64 + 16 * (k + 1)

                PE3_UP, PE3_GT, PE3_H = {}, {}, {}
                _n = 0
                for _ch in range(4):
                    for _c in range(64):
                        _cc = _ch * 64 + _c
                        _n += 1
                        PE3_UP[_cc] = _n
                        _n += 1
                        PE3_GT[_cc] = _n
                        if _c >= 1:
                            _n += 1
                            PE3_H[_cc - 1] = _n
                    _n += 1
                    PE3_H[_ch * 64 + 63] = _n

                @blk.sync
                def _(s):
                    phase_wait(s)
                    s.wait_ge(S["gst"], cnt.get("gst"))
                    for ch in range(4):
                        for tt in range(4):
                            u = ch * 4 + tt
                            if u >= 2:
                                s.wait_ge(S["ve1"], d_ve1 + 8 * (u - 1))
                            s.dma_start(x1t[u % 2][:],
                                        x1_sc.ap()[u * 128:(u + 1) * 128, :]).then_inc(S["q%d" % (u % 2)], 16)
                        for fo in range(16):
                            ff = ch * 16 + fo
                            if ff >= 2:
                                s.wait_ge(S["pe2"], d_pe2 + ff - 1)
                            s.dma_start(hb[ff % 2][:],
                                        h2048_in.ap()[fo, :, :, :].rearrange("p a b -> p (a b)")
                                        ).then_inc(S["q%d" % (4 + ff % 2)], 16)
                        for c in range(64):
                            cc = ch * 64 + c
                            if cc >= 2:
                                s.wait_ge(S["pe3"], d_pe3 + 3 * (cc - 1))
                            s.dma_start(wub[cc % 2][:], wup_sc.ap()[c, :, :]).then_inc(S["q%d" % (6 + cc % 2)], 16)
                            s.dma_start(wgb[cc % 2][:], wgate_sc.ap()[c, :, :]).then_inc(S["q%d" % (6 + cc % 2)], 16)
                        for od in range(4):
                            for c in range(64):
                                dcc = (ch * 4 + od) * 64 + c
                                if dcc >= 4:
                                    s.wait_ge(S["pe4"], d_pe4 + dcc - 3)
                                s.dma_start(w2b[dcc % 4][:],
                                            w2t_sc.ap()[c * 128:(c + 1) * 128,
                                                        od * 512:(od + 1) * 512]
                                            ).then_inc(S["q%d" % (12 + dcc % 4)], 16)
                            for tt in range(4):
                                k = od * 4 + tt
                                idx = ch * 16 + k
                                if idx >= 2:
                                    s.wait_ge(S["ve5"], d_ve5 + idx - 1)
                                s.dma_start(xob[idx % 2][:],
                                            x1_sc.ap()[(ch * 4 + tt) * 128:(ch * 4 + tt + 1) * 128,
                                                       od * 512:(od + 1) * 512]
                                            ).then_inc(S["q%d" % (2 + idx % 2)], 16)
                    phase_end(s)
                for _i, _v in ((0, 8), (1, 8), (2, 32), (3, 32), (4, 32), (5, 32),
                               (6, 128 * 2), (7, 128 * 2),
                               (12, 256), (13, 256), (14, 256), (15, 256)):
                    cnt.set("q%d" % _i, bq[_i] + _v * 16)

                def ln_chain2(v, u):
                    par = u % 2
                    v.wait_ge(S["q%d" % (u % 2)], bq[u % 2] + 16 * (u // 2 + 1))
                    v.reduce_sum(stat1[par][:], x1t[par][:], axis=AX.X).then_inc(S["ve1"], 1)
                    v.wait_ge(S["ac1"], d_ac1 + 2 * u + 1)
                    v.wait_ge(S["ve1"], d_ve1 + 8 * u + 1)
                    m = mv[par][:, 0:1]
                    var = mv[par][:, 1:2]
                    rstd = mv[par][:, 2:3]
                    negmr = mv[par][:, 3:4]
                    v.tensor_scalar(m, stat1[par][:], 1.0 / D, None, ALU.mult).then_inc(S["ve1"], 1)
                    v.tensor_scalar(var, stat2[par][:], 1.0 / D, None, ALU.mult).then_inc(S["ve1"], 1)
                    v.wait_ge(S["ve1"], d_ve1 + 8 * u + 3)
                    v.tensor_tensor(rstd, m, m, ALU.mult).then_inc(S["ve1"], 1)
                    v.wait_ge(S["ve1"], d_ve1 + 8 * u + 4)
                    v.tensor_tensor(var, var, rstd, ALU.subtract).then_inc(S["ve1"], 1)
                    v.wait_ge(S["ac1"], d_ac1 + 2 * u + 2)
                    v.reciprocal(rstd, rstd).then_inc(S["ve1"], 1)
                    v.wait_ge(S["ve1"], d_ve1 + 8 * u + 6)
                    v.scalar_tensor_tensor(negmr, m, -1.0, rstd, ALU.mult, ALU.mult
                                           ).then_inc(S["ve1"], 1)
                    v.wait_ge(S["ve1"], d_ve1 + 8 * u + 7)
                    if u >= 2:
                        v.wait_ge(S["pe1"], d_pe1 + 16 * (u - 1))
                    v.tensor_scalar(xn2[par][:], x1t[par][:], m, rstd,
                                    ALU.subtract, ALU.mult).then_inc(S["ve1"], 1)

                @blk.vector
                def _(v):
                    phase_wait(v)
                    for ch in range(4):
                        ln_chain2(v, ch * 4)
                        for tt in range(4):
                            u = ch * 4 + tt
                            if tt + 1 < 4:
                                ln_chain2(v, u + 1)
                            for k in range(16):
                                g_idx = 16 * u + k
                                v.wait_ge(S["pe1"], d_pe1 + g_idx + 1)
                                v.tensor_scalar(
                                    yfm[:, k * 512 + tt * 128: k * 512 + (tt + 1) * 128],
                                    pf[k % 2][:, 0:128], su_sb[:, k:k + 1], None, ALU.mult)
                                v.memset(flt[:], 0.0).then_inc(S["ve3"], 1)
                        for fo in range(16):
                            ff = ch * 16 + fo
                            v.wait_ge(S["pe2"], d_pe2 + ff + 1)
                            v.tensor_copy(tfm[:, fo * 512:(fo + 1) * 512], pf[ff % 2][:])
                            v.memset(flt[:], 0.0).then_inc(S["ve4"], 1)
                        for c in range(64):
                            cc = ch * 64 + c
                            v.wait_ge(S["ac3"], d_ac3 + 2 * cc + 2)
                            v.tensor_tensor(gc[cc % 2][:], xup[cc % 2][:], sg[cc % 2][:],
                                            ALU.mult)
                            v.memset(flt[:], 0.0).then_inc(S["ve2"], 1)
                            v.wait_ge(S["pe3"], d_pe3 + PE3_H[cc])
                            v.tensor_copy(g2[:, c * 512:(c + 1) * 512], ph[:])
                            v.memset(flt[:], 0.0).then_inc(S["ve2"], 1)
                        for od in range(4):
                            for tt in range(4):
                                k = od * 4 + tt
                                idx = ch * 16 + k
                                v.wait_ge(S["pe4"], d_pe4 + (ch * 4 + od) * 64 + 64)
                                v.wait_ge(S["q%d" % (2 + idx % 2)], bq[2 + idx % 2] + 16 * (idx // 2 + 1))
                                if idx >= 2:
                                    v.wait_ge(S["q%d" % (10 + idx % 2)], bq[10 + idx % 2] + 16 * (idx // 2))
                                v.tensor_add(osb[idx % 2][:], pd[tt][:], xob[idx % 2][:])
                                v.memset(flt[:], 0.0).then_inc(S["ve5"], 1)
                    phase_end(v)
                cnt.set("ve1", d_ve1 + 8 * 16)
                cnt.set("ve2", d_ve2 + 2 * 256)
                cnt.set("ve3", d_ve3 + 256)
                cnt.set("ve4", d_ve4 + 64)
                cnt.set("ve5", d_ve5 + 64)

                @blk.scalar
                def _(sc):
                    phase_wait(sc)
                    for ch in range(4):
                        for tt in range(4):
                            u = ch * 4 + tt
                            par = u % 2
                            sc.wait_ge(S["q%d" % ((ch * 4 + tt) % 2)], bq[(ch * 4 + tt) % 2] + 16 * ((ch * 4 + tt) // 2 + 1))
                            sc.activation(sqt2[:], x1t[par][:], AF.Square,
                                          accum_out=stat2[par][:]).then_inc(S["ac1"], 1)
                            sc.wait_ge(S["ve1"], d_ve1 + 8 * u + 5)
                            sc.activation(mv[par][:, 2:3], mv[par][:, 1:2], AF.Sqrt,
                                          bias=cst[:, CI_EPS:CI_EPS + 1]).then_inc(S["ac1"], 1)
                        for c in range(64):
                            cc = ch * 64 + c
                            sc.wait_ge(S["pe3"], d_pe3 + PE3_UP[cc])
                            if cc >= 2:
                                sc.wait_ge(S["ve2"], d_ve2 + 2 * cc - 3)
                            sc.activation(xup[cc % 2][:], pf0[:], AF.Copy,
                                          scale=scup[:, c:c + 1]).then_inc(S["ac3"], 1)
                            sc.wait_ge(S["pe3"], d_pe3 + PE3_GT[cc])
                            sc.activation(sg[cc % 2][:], pf1[:], AF.Silu,
                                          scale=scgt[:, c:c + 1]).then_inc(S["ac3"], 1)
                    phase_end(sc)
                cnt.set("ac1", d_ac1 + 32)
                cnt.set("ac3", d_ac3 + 2 * 256)

                @blk.tensor
                def _(t):
                    phase_wait(t)
                    for ch in range(4):
                        if ch >= 1:
                            t.wait_ge(S["ac3"], d_ac3 + 128 * ch)
                        for tt in range(4):
                            u = ch * 4 + tt
                            t.wait_ge(S["ve1"], d_ve1 + 8 * u + 8)
                            for k in range(16):
                                g_idx = 16 * u + k
                                if g_idx >= 2:
                                    t.wait_ge(S["ve3"], d_ve3 + g_idx - 1)
                                t.transpose(pf[k % 2][:, 0:128], xn2[u % 2][:, k * 128:(k + 1) * 128],
                                            idnf[:]).then_inc(S["pe1"], 1)
                        t.wait_ge(S["ve3"], d_ve3 + 64 * (ch + 1))
                        for fo in range(16):
                            ff = ch * 16 + fo
                            t.wait_ge(S["q%d" % (4 + ff % 2)], bq[4 + ff % 2] + 16 * (ff // 2 + 1))
                            if ff >= 2:
                                t.wait_ge(S["ve4"], d_ve4 + ff - 1)
                            for fi in range(16):
                                mm = t.matmul(pf[ff % 2][:],
                                              hb[ff % 2][:, fi * 128:(fi + 1) * 128],
                                              yfm[:, fi * 512:(fi + 1) * 512],
                                              start=(fi == 0), stop=(fi == 15))
                                if fi == 15:
                                    mm.then_inc(S["pe2"], 1)
                        t.wait_ge(S["ve4"], d_ve4 + 16 * (ch + 1))
                        for c in range(64):
                            cc = ch * 64 + c
                            t.wait_ge(S["q%d" % (6 + cc % 2)], bq[6 + cc % 2] + 32 * (cc // 2 + 1))
                            if cc >= 1:
                                t.wait_ge(S["ac3"], d_ac3 + 2 * (cc - 1) + 1)
                            for fc in range(16):
                                mm = t.matmul(pf0[:], wub[cc % 2][:, fc * 128:(fc + 1) * 128],
                                              tfm[:, fc * 512:(fc + 1) * 512],
                                              start=(fc == 0), stop=(fc == 15))
                                if fc == 15:
                                    mm.then_inc(S["pe3"], 1)
                            if cc >= 1:
                                t.wait_ge(S["ac3"], d_ac3 + 2 * (cc - 1) + 2)
                            for fc in range(16):
                                mm = t.matmul(pf1[:], wgb[cc % 2][:, fc * 128:(fc + 1) * 128],
                                              tfm[:, fc * 512:(fc + 1) * 512],
                                              start=(fc == 0), stop=(fc == 15))
                                if fc == 15:
                                    mm.then_inc(S["pe3"], 1)
                            if c >= 1:
                                t.wait_ge(S["ve2"], d_ve2 + 2 * (cc - 1) + 1)
                                if cc >= 2:
                                    t.wait_ge(S["ve2"], d_ve2 + 2 * (cc - 2) + 2)
                                t.matmul(ph[:], h128[:], gc[(cc - 1) % 2][:],
                                         start=True, stop=True).then_inc(S["pe3"], 1)
                        cc = ch * 64 + 63
                        t.wait_ge(S["ve2"], d_ve2 + 2 * cc + 1)
                        if cc >= 1:
                            t.wait_ge(S["ve2"], d_ve2 + 2 * (cc - 1) + 2)
                        t.matmul(ph[:], h128[:], gc[cc % 2][:],
                                 start=True, stop=True).then_inc(S["pe3"], 1)
                        for od in range(4):
                            for c in range(64):
                                dcc = (ch * 4 + od) * 64 + c
                                t.wait_ge(S["q%d" % (12 + dcc % 4)], bq[12 + dcc % 4] + 16 * (dcc // 4 + 1))
                                t.wait_ge(S["ve2"], d_ve2 + 2 * (ch * 64 + c) + 2)
                                if c == 0 and (ch * 4 + od) >= 1:
                                    t.wait_ge(S["ve5"], d_ve5 + (ch * 4 + od - 1) * 4 + 4)
                                for tt in range(4):
                                    mm = t.matmul(pd[tt][:],
                                                  g2[:, c * 512 + tt * 128: c * 512 + (tt + 1) * 128],
                                                  w2b[dcc % 4][:],
                                                  start=(c == 0), stop=(c == 63))
                                    if tt == 3:
                                        mm.then_inc(S["pe4"], 1)
                    phase_end(t)
                cnt.set("pe1", d_pe1 + 256)
                cnt.set("pe2", d_pe2 + 64)
                cnt.set("pe3", d_pe3 + 3 * 256)
                cnt.set("pe4", d_pe4 + 1024)

                @blk.gpsimd
                def _(g):
                    phase_wait(g)
                    for ch in range(4):
                        for od in range(4):
                            for tt in range(4):
                                k = od * 4 + tt
                                idx = ch * 16 + k
                                g.wait_ge(S["ve5"], d_ve5 + idx + 1)
                                g.dma_start(out.ap()[(ch * 4 + tt) * 128:(ch * 4 + tt + 1) * 128,
                                                     od * 512:(od + 1) * 512],
                                            osb[idx % 2][:]).then_inc(S["q%d" % (10 + idx % 2)], 16)
                    g.wait_ge(S["q10"], bq[10] + 16 * 32)
                    g.wait_ge(S["q11"], bq[11] + 16 * 32)
                    phase_end(g)
                cnt.set("q10", bq[10] + 16 * 32)
                cnt.set("q11", bq[11] + 16 * 32)
            close_phase()

        phase_setup()
        phase_A()
        phase_BC()
        phase_D()
    return nc



_H2048 = None
_H128 = None
_H64 = None


def _consts():
    global _H2048, _H128, _H64
    if _H2048 is None:
        _H2048 = _hadamard(2048)
        _H128 = _hadamard(128)
        _H64 = _hadamard(64)
    bf = ml_dtypes.bfloat16
    h2048 = np.ascontiguousarray(
        _H2048.reshape(16, 128, 16, 128).transpose(2, 1, 0, 3)).astype(bf)
    h128 = _H128.astype(bf)
    h64 = _H64.astype(bf)
    ident = np.eye(128, dtype=bf)
    identf = np.eye(128, dtype=np.float32)
    ones = np.ones((1, 128), np.float32)
    consts = np.zeros((1, NC_CONST), np.float32)
    consts[0, CI_S128] = S128
    consts[0, CI_EPS] = 1e-5
    return h2048, h128, h64, ident, identf, ones, consts


def kernel(L0_attn_x, L0_attn_w_qkv, L0_attn_w_o,
           L0_ffn_SU_ug, L0_ffn_SU_dp, L0_ffn_Wscale_ug, L0_ffn_Wscale_dp,
           L0_ffn_W_up, L0_ffn_W_gate, L0_ffn_W_down):
    x = np.asarray(L0_attn_x, np.float32)
    wqkv = np.asarray(L0_attn_w_qkv, np.float32)
    wo = np.ascontiguousarray(np.asarray(L0_attn_w_o, np.float32))
    su_ug = np.asarray(L0_ffn_SU_ug, np.float32).reshape(1, D)
    su_dp = np.asarray(L0_ffn_SU_dp, np.float32).reshape(1, INTER)
    wsc_ug = np.asarray(L0_ffn_Wscale_ug, np.float32).reshape(1, 2 * INTER)
    wsc_dp = np.asarray(L0_ffn_Wscale_dp, np.float32).reshape(1, D)
    wup = np.asarray(L0_ffn_W_up, np.float32)
    wgate = np.asarray(L0_ffn_W_gate, np.float32)
    wdown = np.asarray(L0_ffn_W_down, np.float32)

    wqkv_in = np.ascontiguousarray(
        wqkv.reshape(16, 128, 48, 128).transpose(2, 1, 0, 3))
    wup_in = np.ascontiguousarray(
        wup.reshape(64, 128, 16, 128).transpose(0, 3, 2, 1))
    wgate_in = np.ascontiguousarray(
        wgate.reshape(64, 128, 16, 128).transpose(0, 3, 2, 1))
    wdownT_in = np.ascontiguousarray(wdown.T)

    h2048, h128, h64, ident, identf, ones, consts = _consts()

    nc = build_nc()
    in_maps = []
    for c in range(8):
        b, p = divmod(c, 2)
        own = x[b, p * TOK:(p + 1) * TOK, :]
        other = x[b, (1 - p) * TOK:(2 - p) * TOK, :]
        x_core = np.ascontiguousarray(np.concatenate([own, other], axis=0))
        in_maps.append(dict(
            x_in=x_core, wqkv_in=wqkv_in, wo_in=wo,
            wup_in=wup_in, wgate_in=wgate_in, wdownT_in=wdownT_in,
            su_ug_in=su_ug, su_dp_in=su_dp, wsc_ug_in=wsc_ug, wsc_dp_in=wsc_dp,
            h2048_in=h2048, h128_in=h128, h64_in=h64,
            ident_in=ident, identf_in=identf, ones_in=ones, consts_in=consts,
        ))
    res = run_bass_kernel_spmd(nc, in_maps, core_ids=list(range(8)))
    outp = np.empty((4, TFULL, D), np.float32)
    for c in range(8):
        b, p = divmod(c, 2)
        outp[b, p * TOK:(p + 1) * TOK, :] = res.results[c]["out"]
    return outp



# version 08
# speedup vs baseline: 1.1605x; 1.0148x over previous
import math
import contextlib
import numpy as np
import ml_dtypes
import concourse.bass as bass
import concourse.mybir as mybir
from concourse.bass_utils import run_bass_kernel_spmd

F32 = mybir.dt.float32
BF16 = mybir.dt.bfloat16
AF = mybir.ActivationFunctionType
ALU = mybir.AluOpType
AX = mybir.AxisListType

D = 2048
TOK = 2048
TFULL = 4096
NH = 16
INTER = 8192
S128 = 1.0 / math.sqrt(128.0)
C1 = 1.0 / math.sqrt(2048.0)
CDP = 1.0 / math.sqrt(8192.0)

CI_S128, CI_EPS = 0, 1
NC_CONST = 4

SEMNAMES = ["bar", "ld", "st", "gw", "gst", "xl", "wl",
            "pe1", "pe2", "pe3", "pe4",
            "ve1", "ve2", "ve3", "ve4", "ve5",
            "ac1", "ac2", "ac3"] + ["q%d" % i for i in range(18)]


def _hadamard(n):
    h = np.array([[1.0]])
    while h.shape[0] < n:
        h = np.block([[h, h], [h, -h]])
    return h


class Ctr:
    def __init__(self):
        self.c = {}

    def inc(self, instr, sem, name, k=1):
        instr.then_inc(sem, k)
        self.c[name] = self.c.get(name, 0) + k
        return self.c[name]

    def get(self, name):
        return self.c.get(name, 0)

    def set(self, name, v):
        self.c[name] = v


def build_nc():
    nc = bass.Bass("TRN2", target_bir_lowering=False)

    x_in = nc.dram_tensor("x_in", [TFULL, D], F32, kind="ExternalInput")
    wqkv_in = nc.dram_tensor("wqkv_in", [48, 128, 16, 128], F32, kind="ExternalInput")
    wo_in = nc.dram_tensor("wo_in", [D, D], F32, kind="ExternalInput")
    wup_in = nc.dram_tensor("wup_in", [64, 128, 16, 128], F32, kind="ExternalInput")
    wgate_in = nc.dram_tensor("wgate_in", [64, 128, 16, 128], F32, kind="ExternalInput")
    wdownT_in = nc.dram_tensor("wdownT_in", [INTER, D], F32, kind="ExternalInput")
    su_ug_in = nc.dram_tensor("su_ug_in", [1, D], F32, kind="ExternalInput")
    su_dp_in = nc.dram_tensor("su_dp_in", [1, INTER], F32, kind="ExternalInput")
    wsc_ug_in = nc.dram_tensor("wsc_ug_in", [1, 2 * INTER], F32, kind="ExternalInput")
    wsc_dp_in = nc.dram_tensor("wsc_dp_in", [1, D], F32, kind="ExternalInput")
    h2048_in = nc.dram_tensor("h2048_in", [16, 128, 16, 128], BF16, kind="ExternalInput")
    h128_in = nc.dram_tensor("h128_in", [128, 128], BF16, kind="ExternalInput")
    h64_in = nc.dram_tensor("h64_in", [64, 64], BF16, kind="ExternalInput")
    ident_in = nc.dram_tensor("ident_in", [128, 128], BF16, kind="ExternalInput")
    identf_in = nc.dram_tensor("identf_in", [128, 128], F32, kind="ExternalInput")
    ones_in = nc.dram_tensor("ones_in", [1, 128], F32, kind="ExternalInput")
    consts_in = nc.dram_tensor("consts_in", [1, NC_CONST], F32, kind="ExternalInput")

    out = nc.dram_tensor("out", [TOK, D], F32, kind="ExternalOutput")

    q_sc = nc.dram_tensor("q_sc", [D, TFULL], BF16, kind="Internal")
    k_sc = nc.dram_tensor("k_sc", [D, TFULL], BF16, kind="Internal")
    v_sc = nc.dram_tensor("v_sc", [D, TOK], BF16, kind="Internal")
    x1_sc = nc.dram_tensor("x1_sc", [TOK, D], F32, kind="Internal")
    wup_sc = nc.dram_tensor("wup_sc", [64, 128, 2048], BF16, kind="Internal")
    wgate_sc = nc.dram_tensor("wgate_sc", [64, 128, 2048], BF16, kind="Internal")
    w2pre_sc = nc.dram_tensor("w2pre_sc", [INTER, D], BF16, kind="Internal")
    w2t_sc = nc.dram_tensor("w2t_sc", [INTER, D], BF16, kind="Internal")

    def bcast_row(dram_t, n, col0=0):
        a = dram_t.ap()
        return bass.AP(a.tensor, a.offset + col0, [[0, 128], [1, n]])

    def strided_col(dram_t, p, cols, col0=0):
        a = dram_t.ap()
        return bass.AP(a.tensor, a.offset + col0, [[1, p], [p, cols]])

    cnt = Ctr()

    with contextlib.ExitStack() as stack:
        sems = {sn: stack.enter_context(nc.semaphore(sn)) for sn in SEMNAMES}
        S = sems

        cst = stack.enter_context(nc.sbuf_tensor("cst", [128, NC_CONST], F32))
        onesf = stack.enter_context(nc.sbuf_tensor("onesf", [1, 128], F32))
        idn = stack.enter_context(nc.sbuf_tensor("idn", [128, 128], BF16))
        idnf = stack.enter_context(nc.sbuf_tensor("idnf", [128, 128], F32))
        flt = stack.enter_context(nc.sbuf_tensor("flt", [128, 1], F32))
        h128 = stack.enter_context(nc.sbuf_tensor("h128", [128, 128], BF16))
        h64 = stack.enter_context(nc.sbuf_tensor("h64", [64, 64], BF16))
        su_sb = stack.enter_context(nc.sbuf_tensor("su_sb", [128, 16], F32))
        scup = stack.enter_context(nc.sbuf_tensor("scup", [128, 64], F32))
        scgt = stack.enter_context(nc.sbuf_tensor("scgt", [128, 64], F32))

        phase_no = [0]

        def phase_wait(eng):
            if phase_no[0] > 0:
                eng.wait_ge(S["bar"], 5 * phase_no[0])

        def phase_end(eng):
            eng.sem_inc(S["bar"], 1)

        def close_phase():
            phase_no[0] += 1

        def bases(*names):
            return tuple(cnt.get(n) for n in names)

        def phase_setup():
            b_ld, b_st, b_pe1, b_pe2, b_ve1, b_ve2, b_ac1, b_ac2 = bases(
                "ld", "st", "pe1", "pe2", "ve1", "ve2", "ac1", "ac2")
            bq = [cnt.get("q%d" % _i) for _i in range(18)]
            with contextlib.ExitStack() as _es:
                t_ug = _es.enter_context(nc.sbuf_tensor("t_ug", [128, 64], F32))
                t_dp = _es.enter_context(nc.sbuf_tensor("t_dp", [128, 64], F32))
                bcdp = _es.enter_context(nc.sbuf_tensor("bcdp", [128, D], F32))
                sdp_row = _es.enter_context(nc.sbuf_tensor("sdp_row", [1, D], F32))
                wdf0 = _es.enter_context(nc.sbuf_tensor("wdf0", [128, D], F32))
                wdf1 = _es.enter_context(nc.sbuf_tensor("wdf1", [128, D], F32))
                wdb0 = _es.enter_context(nc.sbuf_tensor("wdb0", [128, D], BF16))
                wdb1 = _es.enter_context(nc.sbuf_tensor("wdb1", [128, D], BF16))
                wcb0 = _es.enter_context(nc.sbuf_tensor("wcb0", [64, 1024], BF16))
                wcb1 = _es.enter_context(nc.sbuf_tensor("wcb1", [64, 1024], BF16))
                wob0 = _es.enter_context(nc.sbuf_tensor("wob0", [64, 1024], BF16))
                wob1 = _es.enter_context(nc.sbuf_tensor("wob1", [64, 1024], BF16))
                ps_bc = _es.enter_context(nc.psum_tensor("ps_bc", [128, 512], F32))
                ps_h0 = _es.enter_context(nc.psum_tensor("ps_h0", [64, 1024], F32))
                _blkcm = nc.Block("setup")
                blk = _blkcm.__enter__()
                wdf = [wdf0, wdf1]
                wdb = [wdb0, wdb1]
                wcb = [wcb0, wcb1]
                wob = [wob0, wob1]
                psh = [ps_h0, ps_h0]

                @blk.sync
                def _(s):
                    phase_wait(s)
                    for dst, src in (
                        (cst[:], bcast_row(consts_in, NC_CONST)),
                        (onesf[:], ones_in.ap()),
                        (idn[:], ident_in.ap()),
                        (idnf[:], identf_in.ap()),
                        (h128[:], h128_in.ap()),
                        (h64[:], h64_in.ap()),
                    ):
                        s.dma_start(dst, src).then_inc(S["ld"], 16)
                    s.dma_start(su_sb[:], strided_col(su_ug_in, 128, 16),
                                allow_slow_non_contiguous=True).then_inc(S["ld"], 16)
                    s.dma_start(t_ug[:], strided_col(wsc_ug_in, 128, 64),
                                allow_slow_non_contiguous=True).then_inc(S["ld"], 16)
                    s.dma_start(t_dp[:], strided_col(su_dp_in, 128, 64),
                                allow_slow_non_contiguous=True).then_inc(S["ld"], 16)
                    s.dma_start(sdp_row[:], wsc_dp_in.ap()).then_inc(S["ld"], 16)
                    s.dma_start(scgt[:], strided_col(wsc_ug_in, 128, 64, col0=INTER),
                                allow_slow_non_contiguous=True).then_inc(S["ld"], 16)
                    for i in range(64):
                        if i >= 2:
                            s.wait_ge(S["ve2"], b_ve2 + (i - 1))
                        s.dma_start(wdf[i % 2][:], wdownT_in.ap()[i * 128:(i + 1) * 128, :]
                                    ).then_inc(S["q%d" % (i % 2)], 16)
                        if i >= 1:
                            s.wait_ge(S["ve2"], b_ve2 + i)
                            s.dma_start(w2pre_sc.ap()[(i - 1) * 128:i * 128, :],
                                        wdb[(i - 1) % 2][:]).then_inc(S["q%d" % (2 + (i - 1) % 2)], 16)
                    s.wait_ge(S["ve2"], b_ve2 + 64)
                    s.dma_start(w2pre_sc.ap()[63 * 128:, :], wdb[1][:]).then_inc(S["q3"], 16)
                    s.wait_ge(S["q2"], bq[2] + 32 * 16)
                    s.wait_ge(S["q3"], bq[3] + 32 * 16)
                    a = w2pre_sc.ap()
                    b = w2t_sc.ap()
                    for j in range(256):
                        r, odh = divmod(j, 2)
                        src = bass.AP(a.tensor, a.offset + r * 2048 + odh * 1024,
                                      [[128 * 2048, 64], [1, 1024]])
                        if j >= 2:
                            s.wait_ge(S["pe2"], b_pe2 + (j - 1))
                        s.dma_start(wcb[j % 2][:], src).then_inc(S["q%d" % (4 + j % 2)], 16)
                        if j >= 1:
                            ro, odho = divmod(j - 1, 2)
                            s.wait_ge(S["ac2"], b_ac2 + j)
                            dst = bass.AP(b.tensor, b.offset + ro * 2048 + odho * 1024,
                                          [[128 * 2048, 64], [1, 1024]])
                            s.dma_start(dst, wob[(j - 1) % 2][:]).then_inc(S["q%d" % (6 + (j - 1) % 2)], 16)
                    s.wait_ge(S["ac2"], b_ac2 + 256)
                    dst = bass.AP(b.tensor, b.offset + 127 * 2048 + 1024,
                                  [[128 * 2048, 64], [1, 1024]])
                    s.dma_start(dst, wob[1][:]).then_inc(S["q7"], 16)
                    s.wait_ge(S["q6"], bq[6] + 128 * 16)
                    s.wait_ge(S["q7"], bq[7] + 128 * 16)
                    phase_end(s)
                cnt.set("ld", b_ld + 16 * 11)
                for _i, _v in ((0, 32), (1, 32), (2, 32), (3, 32), (4, 128), (5, 128), (6, 128), (7, 128)):
                    cnt.set("q%d" % _i, bq[_i] + _v * 16)

                @blk.vector
                def _(v):
                    phase_wait(v)
                    v.wait_ge(S["ld"], b_ld + 176)
                    v.tensor_tensor(scup[:], t_ug[:], t_dp[:], ALU.mult).then_inc(S["ve1"], 1)
                    v.wait_ge(S["ve1"], b_ve1 + 1)
                    v.tensor_scalar(scup[:], scup[:], C1, None, ALU.mult)
                    v.tensor_scalar(scgt[:], scgt[:], C1, None, ALU.mult)
                    v.tensor_scalar(sdp_row[:], sdp_row[:], CDP, None, ALU.mult
                                    ).then_inc(S["ve1"], 1)
                    for i in range(64):
                        v.wait_ge(S["q%d" % (i % 2)], bq[i % 2] + 16 * (i // 2 + 1))
                        if i == 0:
                            v.wait_ge(S["ac1"], b_ac1 + 4)
                        if i >= 2:
                            v.wait_ge(S["q%d" % (2 + i % 2)], bq[2 + i % 2] + 16 * (i // 2))
                        v.tensor_tensor(wdb[i % 2][:], wdf[i % 2][:], bcdp[:], ALU.mult)
                        v.memset(flt[:], 0.0).then_inc(S["ve2"], 1)
                    phase_end(v)
                cnt.set("ve1", b_ve1 + 2)
                cnt.set("ve2", b_ve2 + 64)

                @blk.tensor
                def _(t):
                    phase_wait(t)
                    t.wait_ge(S["ve1"], b_ve1 + 2)
                    for kk in range(4):
                        if kk >= 1:
                            t.wait_ge(S["ac1"], b_ac1 + kk)
                        t.matmul(ps_bc[:], onesf[:], sdp_row[:, kk * 512:(kk + 1) * 512],
                                 start=True, stop=True).then_inc(S["pe1"], 1)
                    for j in range(256):
                        t.wait_ge(S["q%d" % (4 + j % 2)], bq[4 + j % 2] + 16 * (j // 2 + 1))
                        if j >= 1:
                            t.wait_ge(S["ac2"], b_ac2 + j)
                        for nn in range(2):
                            mm = t.matmul(psh[j % 2][:, nn * 512:(nn + 1) * 512], h64[:],
                                          wcb[j % 2][:, nn * 512:(nn + 1) * 512],
                                          start=True, stop=True)
                            if nn == 1:
                                mm.then_inc(S["pe2"], 1)
                    phase_end(t)
                cnt.set("pe1", b_pe1 + 4)
                cnt.set("pe2", b_pe2 + 256)

                @blk.scalar
                def _(sc):
                    phase_wait(sc)
                    for kk in range(4):
                        sc.wait_ge(S["pe1"], b_pe1 + kk + 1)
                        sc.copy(bcdp[:, kk * 512:(kk + 1) * 512], ps_bc[:]).then_inc(S["ac1"], 1)
                    for j in range(256):
                        sc.wait_ge(S["pe2"], b_pe2 + j + 1)
                        if j >= 2:
                            sc.wait_ge(S["q%d" % (6 + j % 2)], bq[6 + j % 2] + 16 * (j // 2))
                        sc.copy(wob[j % 2][:], psh[j % 2][:]).then_inc(S["ac2"], 1)
                    phase_end(sc)
                cnt.set("ac1", b_ac1 + 4)
                cnt.set("ac2", b_ac2 + 256)

                @blk.gpsimd
                def _(g):
                    phase_wait(g)
                    phase_end(g)
                _blkcm.__exit__(None, None, None)
                phase_A()
            close_phase()
            close_phase()

        def phase_A():
            (b_xl, b_st, b_gw, b_gst, b_wl, b_pe1, b_pe2,
             b_ve1, b_ve2, b_ve3, b_ac1, b_ac2) = bases(
                "xl", "st", "gw", "gst", "wl", "pe1", "pe2",
                "ve1", "ve2", "ve3", "ac1", "ac2")
            bq = [cnt.get("q%d" % _i) for _i in range(18)]
            with contextlib.ExitStack() as _es:
                xt0 = _es.enter_context(nc.sbuf_tensor("xt0", [128, D], F32))
                xt1 = _es.enter_context(nc.sbuf_tensor("xt1", [128, D], F32))
                sqt = _es.enter_context(nc.sbuf_tensor("sqt", [128, D], F32))
                xn0 = _es.enter_context(nc.sbuf_tensor("xn0", [128, D], BF16))
                xn1 = _es.enter_context(nc.sbuf_tensor("xn1", [128, D], BF16))
                xnfm = _es.enter_context(nc.sbuf_tensor("xnfm", [128, 16 * 2048], BF16))
                wqb0 = _es.enter_context(nc.sbuf_tensor("wqb0", [128, 2048], BF16))
                wqb1 = _es.enter_context(nc.sbuf_tensor("wqb1", [128, 2048], BF16))
                qkt0 = _es.enter_context(nc.sbuf_tensor("qkt0", [128, 512], BF16))
                qkt1 = _es.enter_context(nc.sbuf_tensor("qkt1", [128, 512], BF16))
                qkt2 = _es.enter_context(nc.sbuf_tensor("qkt2", [128, 512], BF16))
                qkt3 = _es.enter_context(nc.sbuf_tensor("qkt3", [128, 512], BF16))
                st1a = _es.enter_context(nc.sbuf_tensor("st1a", [128, 1], F32))
                st1b = _es.enter_context(nc.sbuf_tensor("st1b", [128, 1], F32))
                st2a = _es.enter_context(nc.sbuf_tensor("st2a", [128, 1], F32))
                st2b = _es.enter_context(nc.sbuf_tensor("st2b", [128, 1], F32))
                mva = _es.enter_context(nc.sbuf_tensor("mva", [128, 4], F32))
                mvb = _es.enter_context(nc.sbuf_tensor("mvb", [128, 4], F32))
                wpc0 = _es.enter_context(nc.sbuf_tensor("wpc0", [128, 2048], BF16))
                wpc1 = _es.enter_context(nc.sbuf_tensor("wpc1", [128, 2048], BF16))
                pq0 = _es.enter_context(nc.psum_tensor("pq0", [128, 512], F32))
                pq1 = _es.enter_context(nc.psum_tensor("pq1", [128, 512], F32))
                pt0 = _es.enter_context(nc.psum_tensor("pt0", [128, 128], BF16))
                pt1 = _es.enter_context(nc.psum_tensor("pt1", [128, 128], BF16))
                blk = _es.enter_context(nc.Block("phA"))
                xt = [xt0, xt1]
                xn = [xn0, xn1]
                wqb = [wqb0, wqb1]
                qkt = [qkt0, qkt1, qkt2, qkt3]
                stat1 = [st1a, st1b]
                stat2 = [st2a, st2b]
                mv = [mva, mvb]
                pq = [pq0, pq1]
                pt = [pt0, pt1]
                wpc = [wpc0, wpc1]

                NT = 16
                CT = (48, 32)
                E0 = (0, 192)
                NE = 320

                def qdst(ctg, tc):
                    tb = 0 if ctg < 48 else 1
                    ct = ctg if ctg < 48 else ctg - 48
                    if ct < 16:
                        return q_sc.ap()[ct * 128:(ct + 1) * 128,
                                         tb * 2048 + tc * 512: tb * 2048 + (tc + 1) * 512]
                    if ct < 32:
                        return k_sc.ap()[(ct - 16) * 128:(ct - 15) * 128,
                                         tb * 2048 + tc * 512: tb * 2048 + (tc + 1) * 512]
                    return v_sc.ap()[(ct - 32) * 128:(ct - 31) * 128,
                                     tc * 512:(tc + 1) * 512]

                @blk.sync
                def _(s):
                    phase_wait(s)
                    for tb in range(2):
                        for tt in range(NT):
                            u = tb * NT + tt
                            if u >= 2:
                                s.wait_ge(S["ve1"], b_ve1 + 8 * (u - 1))
                            s.dma_start(xt[u % 2][:], x_in.ap()[u * 128:(u + 1) * 128, :]
                                        ).then_inc(S["q%d" % (8 + u % 2)], 16)
                        for ei in range(CT[tb] * 4):
                            e = E0[tb] + ei
                            s.wait_ge(S["ve2"], b_ve2 + e + 1)
                            ctg, tc = divmod(e, 4)
                            s.dma_start(qdst(ctg, tc), qkt[e % 4][:]).then_inc(S["q%d" % (12 + e % 4)], 16)
                    for _k in range(4):
                        s.wait_ge(S["q%d" % (12 + _k)], bq[12 + _k] + 80 * 16)
                    phase_end(s)
                cnt.set("q8", bq[8] + 16 * 16)
                cnt.set("q9", bq[9] + 16 * 16)
                for _k in range(4):
                    cnt.set("q%d" % (12 + _k), bq[12 + _k] + 80 * 16)

                @blk.gpsimd
                def _(g):
                    phase_wait(g)
                    pj = 0
                    for ctg in range(80):
                        if ctg >= 2:
                            g.wait_ge(S["pe2"], b_pe2 + 4 * (ctg - 1))
                        g.dma_start(wqb[ctg % 2][:],
                                    wqkv_in.ap()[ctg if ctg < 48 else ctg - 48, :, :, :]
                                    .rearrange("p a b -> p (a b)")).then_inc(S["q%d" % (10 + ctg % 2)], 16)
                        for _ in range(2):
                            if pj < 128:
                                win = wup_in if pj < 64 else wgate_in
                                wsc = wup_sc if pj < 64 else wgate_sc
                                c = pj % 64
                                if pj >= 2:
                                    g.wait_ge(S["gst"], b_gst + 16 * (pj - 1))
                                g.dma_start(wpc[pj % 2][:],
                                            win.ap()[c, :, :, :].rearrange("p a b -> p (a b)")
                                            ).then_inc(S["q%d" % (16 + pj % 2)], 16)
                                if pj >= 1:
                                    g.wait_ge(S["q%d" % (16 + (pj - 1) % 2)],
                                              bq[16 + (pj - 1) % 2] + 16 * ((pj - 1) // 2 + 1))
                                    wino = wup_sc if pj - 1 < 64 else wgate_sc
                                    g.dma_start(wino.ap()[(pj - 1) % 64, :, :],
                                                wpc[(pj - 1) % 2][:]).then_inc(S["gst"], 16)
                                pj += 1
                    g.wait_ge(S["q%d" % (16 + 127 % 2)], bq[16 + 127 % 2] + 16 * (127 // 2 + 1))
                    g.dma_start(wgate_sc.ap()[63, :, :], wpc[127 % 2][:]).then_inc(S["gst"], 16)
                    g.wait_ge(S["gst"], b_gst + 128 * 16)
                    phase_end(g)
                cnt.set("q10", bq[10] + 40 * 16)
                cnt.set("q11", bq[11] + 40 * 16)
                cnt.set("q16", bq[16] + 64 * 16)
                cnt.set("q17", bq[17] + 64 * 16)
                cnt.set("gst", b_gst + 128 * 16)

                def ln_chain(v, u):
                    par = u % 2
                    v.wait_ge(S["q%d" % (8 + u % 2)], bq[8 + u % 2] + 16 * (u // 2 + 1))
                    v.reduce_sum(stat1[par][:], xt[par][:], axis=AX.X).then_inc(S["ve1"], 1)
                    v.wait_ge(S["ac1"], b_ac1 + 2 * u + 1)
                    v.wait_ge(S["ve1"], b_ve1 + 8 * u + 1)
                    m = mv[par][:, 0:1]
                    var = mv[par][:, 1:2]
                    rstd = mv[par][:, 2:3]
                    negmr = mv[par][:, 3:4]
                    v.tensor_scalar(m, stat1[par][:], 1.0 / D, None, ALU.mult).then_inc(S["ve1"], 1)
                    v.tensor_scalar(var, stat2[par][:], 1.0 / D, None, ALU.mult).then_inc(S["ve1"], 1)
                    v.wait_ge(S["ve1"], b_ve1 + 8 * u + 3)
                    v.tensor_tensor(rstd, m, m, ALU.mult).then_inc(S["ve1"], 1)
                    v.wait_ge(S["ve1"], b_ve1 + 8 * u + 4)
                    v.tensor_tensor(var, var, rstd, ALU.subtract).then_inc(S["ve1"], 1)
                    v.wait_ge(S["ac1"], b_ac1 + 2 * u + 2)
                    v.reciprocal(rstd, rstd).then_inc(S["ve1"], 1)
                    v.wait_ge(S["ve1"], b_ve1 + 8 * u + 6)
                    v.scalar_tensor_tensor(negmr, m, -1.0, rstd, ALU.mult, ALU.mult
                                           ).then_inc(S["ve1"], 1)
                    v.wait_ge(S["ve1"], b_ve1 + 8 * u + 7)
                    if u >= 2:
                        v.wait_ge(S["pe1"], b_pe1 + 16 * (u - 1))
                    v.tensor_scalar(xn[par][:], xt[par][:], m, rstd,
                                    ALU.subtract, ALU.mult).then_inc(S["ve1"], 1)

                @blk.vector
                def _(v):
                    phase_wait(v)
                    for tb in range(2):
                        ln_chain(v, tb * NT)
                        for tt in range(NT):
                            u = tb * NT + tt
                            if tt + 1 < NT:
                                ln_chain(v, u + 1)
                            for k in range(16):
                                g_idx = 16 * u + k
                                v.wait_ge(S["pe1"], b_pe1 + g_idx + 1)
                                v.tensor_copy(
                                    xnfm[:, k * 2048 + tt * 128: k * 2048 + (tt + 1) * 128],
                                    pt[k % 2][:])
                                v.memset(flt[:], 0.0).then_inc(S["ve3"], 1)
                        for ei in range(CT[tb] * 4):
                            e = E0[tb] + ei
                            v.wait_ge(S["pe2"], b_pe2 + e + 1)
                            if e >= 4:
                                v.wait_ge(S["q%d" % (12 + e % 4)], bq[12 + e % 4] + 16 * (e // 4))
                            v.tensor_copy(qkt[e % 4][:], pq[e % 2][:])
                            v.memset(flt[:], 0.0).then_inc(S["ve2"], 1)
                    phase_end(v)
                cnt.set("ve1", b_ve1 + 8 * 32)
                cnt.set("ve2", b_ve2 + NE)
                cnt.set("ve3", b_ve3 + 512)

                @blk.scalar
                def _(sc):
                    phase_wait(sc)
                    sc.wait_ge(S["ld"], cnt.get("ld"))
                    for tb in range(2):
                        for tt in range(NT):
                            u = tb * NT + tt
                            par = u % 2
                            sc.wait_ge(S["q%d" % (8 + u % 2)], bq[8 + u % 2] + 16 * (u // 2 + 1))
                            sc.activation(sqt[:], xt[par][:], AF.Square,
                                          accum_out=stat2[par][:]).then_inc(S["ac1"], 1)
                            sc.wait_ge(S["ve1"], b_ve1 + 8 * u + 5)
                            sc.activation(mv[par][:, 2:3], mv[par][:, 1:2], AF.Sqrt,
                                          bias=cst[:, CI_EPS:CI_EPS + 1]).then_inc(S["ac1"], 1)
                    phase_end(sc)
                cnt.set("ac1", b_ac1 + 64)

                @blk.tensor
                def _(t):
                    phase_wait(t)
                    t.wait_ge(S["ld"], cnt.get("ld"))
                    for tb in range(2):
                        for tt in range(NT):
                            u = tb * NT + tt
                            t.wait_ge(S["ve1"], b_ve1 + 8 * u + 8)
                            for k in range(16):
                                g_idx = 16 * u + k
                                if g_idx >= 2:
                                    t.wait_ge(S["ve3"], b_ve3 + g_idx - 1)
                                t.transpose(pt[k % 2][:], xn[u % 2][:, k * 128:(k + 1) * 128],
                                            idn[:]).then_inc(S["pe1"], 1)
                        t.wait_ge(S["ve3"], b_ve3 + 256 * (tb + 1))
                        for ei in range(CT[tb] * 4):
                            e = E0[tb] + ei
                            ctg, tc = divmod(e, 4)
                            t.wait_ge(S["q%d" % (10 + ctg % 2)], bq[10 + ctg % 2] + 16 * (ctg // 2 + 1))
                            if e >= 2:
                                t.wait_ge(S["ve2"], b_ve2 + e - 1)
                            for fc in range(16):
                                t.matmul(pq[e % 2][:],
                                         wqb[ctg % 2][:, fc * 128:(fc + 1) * 128],
                                         xnfm[:, fc * 2048 + tc * 512: fc * 2048 + (tc + 1) * 512],
                                         start=(fc == 0), stop=(fc == 15)
                                         ).then_inc(S["pe2"], 1) if fc == 15 else \
                                    t.matmul(pq[e % 2][:],
                                             wqb[ctg % 2][:, fc * 128:(fc + 1) * 128],
                                             xnfm[:, fc * 2048 + tc * 512: fc * 2048 + (tc + 1) * 512],
                                             start=(fc == 0), stop=False)
                    phase_end(t)
                cnt.set("pe1", b_pe1 + 512)
                cnt.set("pe2", b_pe2 + NE)

        def phase_BC():
            (b_xl, b_gw, b_pe1, b_pe2, b_pe3,
             b_ve1, b_ve2, b_ac1, b_ac2) = bases(
                "xl", "gw", "pe1", "pe2", "pe3", "ve1", "ve2", "ac1", "ac2")
            with contextlib.ExitStack() as _es:
                vals = _es.enter_context(nc.sbuf_tensor("vals", [128, NH * 2048], BF16))
                wo_sb = _es.enter_context(nc.sbuf_tensor("wo_sb", [128, 16 * 2048], BF16))
                csr = _es.enter_context(nc.sbuf_tensor("csr", [1, TOK], F32))
                phase_B_inner(vals, wo_sb, csr,
                              b_xl, b_gw, b_pe1, b_pe2, b_pe3,
                              b_ve1, b_ve2, b_ac1, b_ac2)
                close_phase()
                phase_C_inner(vals, wo_sb)
                close_phase()

        def phase_B_inner(vals, wo_sb, csr,
                          b_xl, b_gw, b_pe1, b_pe2, b_pe3,
                          b_ve1, b_ve2, b_ac1, b_ac2):
            bq = [cnt.get("q%d" % _i) for _i in range(18)]
            with contextlib.ExitStack() as _es:
                qh0 = _es.enter_context(nc.sbuf_tensor("qh0", [128, TFULL], BF16))
                qh1 = _es.enter_context(nc.sbuf_tensor("qh1", [128, TFULL], BF16))
                kh0 = _es.enter_context(nc.sbuf_tensor("kh0", [128, TFULL], BF16))
                kh1 = _es.enter_context(nc.sbuf_tensor("kh1", [128, TFULL], BF16))
                vh0 = _es.enter_context(nc.sbuf_tensor("vh0", [128, TOK], BF16))
                vh1 = _es.enter_context(nc.sbuf_tensor("vh1", [128, TOK], BF16))
                un0 = _es.enter_context(nc.sbuf_tensor("un0", [128, TOK], BF16))
                un1 = _es.enter_context(nc.sbuf_tensor("un1", [128, TOK], BF16))
                utmp = _es.enter_context(nc.sbuf_tensor("utmp", [128, TOK], BF16))
                lo = _es.enter_context(nc.sbuf_tensor("lo", [128, 8], F32))
                lw = _es.enter_context(nc.sbuf_tensor("lw", [128, 8], F32))
                lwb = _es.enter_context(nc.sbuf_tensor("lwb", [128, 2], BF16))
                psA0 = _es.enter_context(nc.psum_tensor("psA0", [128, 1024], F32))
                psA1 = _es.enter_context(nc.psum_tensor("psA1", [128, 1024], F32))
                csps = _es.enter_context(nc.psum_tensor("csps", [1, 2048], F32))
                blk = _es.enter_context(nc.Block("phB"))
                qh = [qh0, qh1]
                kh = [kh0, kh1]
                vh = [vh0, vh1]
                un = [un0, un1]
                psA = [psA0, psA1]

                @blk.sync
                def _(s):
                    phase_wait(s)
                    for h in range(NH):
                        if h >= 2:
                            s.wait_ge(S["pe2"], b_pe2 + 128 * (h - 1))
                            s.wait_ge(S["ve2"], b_ve2 + 2 * (h - 1))
                        s.dma_start(qh[h % 2][:], q_sc.ap()[h * 128:(h + 1) * 128, :]
                                    ).then_inc(S["q%d" % (h % 2)], 16)
                        s.dma_start(kh[h % 2][:], k_sc.ap()[h * 128:(h + 1) * 128, :]
                                    ).then_inc(S["q%d" % (h % 2)], 16)
                        s.dma_start(vh[h % 2][:], v_sc.ap()[h * 128:(h + 1) * 128, :]
                                    ).then_inc(S["q%d" % (h % 2)], 16)
                    phase_end(s)
                cnt.set("q0", bq[0] + 8 * 48)
                cnt.set("q1", bq[1] + 8 * 48)

                @blk.gpsimd
                def _(g):
                    phase_wait(g)
                    for hc in range(16):
                        g.dma_start(wo_sb[:, hc * 2048:(hc + 1) * 2048],
                                    wo_in.ap()[hc * 128:(hc + 1) * 128, :]).then_inc(S["gw"], 16)
                    phase_end(g)
                cnt.set("gw", b_gw + 256)

                @blk.tensor
                def _(t):
                    phase_wait(t)
                    for h in range(NH):
                        for qt in range(32):
                            j = 32 * h + qt
                            if qt == 0:
                                t.wait_ge(S["q%d" % (h % 2)], bq[h % 2] + 48 * (h // 2 + 1))
                                if h >= 1:
                                    t.wait_ge(S["ve2"], b_ve2 + 2 * h)
                                    t.wait_ge(S["ac2"], b_ac2 + h)
                            if j >= 1:
                                t.wait_ge(S["ac1"], b_ac1 + 4 * (j - 1) + 3)
                            for n in range(2):
                                t.matmul(psA0[:, n * 512:(n + 1) * 512],
                                         qh[h % 2][:, qt * 128:(qt + 1) * 128],
                                         kh[h % 2][:, n * 512:(n + 1) * 512],
                                         start=True, stop=True
                                         ).then_inc(S["pe2"], 1) if n == 1 else \
                                    t.matmul(psA0[:, n * 512:(n + 1) * 512],
                                             qh[h % 2][:, qt * 128:(qt + 1) * 128],
                                             kh[h % 2][:, n * 512:(n + 1) * 512],
                                             start=True, stop=True)
                            if j >= 1:
                                t.wait_ge(S["ac1"], b_ac1 + 4 * (j - 1) + 4)
                            for n in range(2):
                                t.matmul(psA1[:, n * 512:(n + 1) * 512],
                                         qh[h % 2][:, qt * 128:(qt + 1) * 128],
                                         kh[h % 2][:, 1024 + n * 512: 1024 + (n + 1) * 512],
                                         start=True, stop=True
                                         ).then_inc(S["pe2"], 1) if n == 1 else \
                                    t.matmul(psA1[:, n * 512:(n + 1) * 512],
                                             qh[h % 2][:, qt * 128:(qt + 1) * 128],
                                             kh[h % 2][:, 1024 + n * 512: 1024 + (n + 1) * 512],
                                             start=True, stop=True)
                            if qt >= 1:
                                jj = j - 1
                                t.wait_ge(S["ve1"], b_ve1 + jj + 1)
                                for kc in range(4):
                                    t.matmul(csps[:, kc * 512:(kc + 1) * 512],
                                             lwb[:, jj % 2: jj % 2 + 1],
                                             un[jj % 2][:, kc * 512:(kc + 1) * 512],
                                             start=(jj % 32 == 0), stop=(jj % 32 == 31)
                                             ).then_inc(S["pe3"], 1) if kc == 3 else \
                                        t.matmul(csps[:, kc * 512:(kc + 1) * 512],
                                                 lwb[:, jj % 2: jj % 2 + 1],
                                                 un[jj % 2][:, kc * 512:(kc + 1) * 512],
                                                 start=(jj % 32 == 0), stop=(jj % 32 == 31))
                            t.wait_ge(S["ac1"], b_ac1 + 4 * j + 1)
                            for n in range(2):
                                t.matmul(psA0[:, n * 512:(n + 1) * 512],
                                         qh[h % 2][:, qt * 128:(qt + 1) * 128],
                                         kh[h % 2][:, 2048 + n * 512: 2048 + (n + 1) * 512],
                                         start=True, stop=True
                                         ).then_inc(S["pe2"], 1) if n == 1 else \
                                    t.matmul(psA0[:, n * 512:(n + 1) * 512],
                                             qh[h % 2][:, qt * 128:(qt + 1) * 128],
                                             kh[h % 2][:, 2048 + n * 512: 2048 + (n + 1) * 512],
                                             start=True, stop=True)
                            t.wait_ge(S["ac1"], b_ac1 + 4 * j + 2)
                            for n in range(2):
                                t.matmul(psA1[:, n * 512:(n + 1) * 512],
                                         qh[h % 2][:, qt * 128:(qt + 1) * 128],
                                         kh[h % 2][:, 3072 + n * 512: 3072 + (n + 1) * 512],
                                         start=True, stop=True
                                         ).then_inc(S["pe2"], 1) if n == 1 else \
                                    t.matmul(psA1[:, n * 512:(n + 1) * 512],
                                             qh[h % 2][:, qt * 128:(qt + 1) * 128],
                                             kh[h % 2][:, 3072 + n * 512: 3072 + (n + 1) * 512],
                                             start=True, stop=True)
                        jj = 32 * h + 31
                        t.wait_ge(S["ve1"], b_ve1 + jj + 1)
                        for kc in range(4):
                            t.matmul(csps[:, kc * 512:(kc + 1) * 512],
                                     lwb[:, jj % 2: jj % 2 + 1],
                                     un[jj % 2][:, kc * 512:(kc + 1) * 512],
                                     start=(jj % 32 == 0), stop=(jj % 32 == 31)
                                     ).then_inc(S["pe3"], 1) if kc == 3 else \
                                t.matmul(csps[:, kc * 512:(kc + 1) * 512],
                                         lwb[:, jj % 2: jj % 2 + 1],
                                         un[jj % 2][:, kc * 512:(kc + 1) * 512],
                                         start=(jj % 32 == 0), stop=(jj % 32 == 31))
                        t.wait_ge(S["ac2"], b_ac2 + h + 1)
                        for kk in range(4):
                            dstp = psA[kk // 2][:, (kk % 2) * 512:(kk % 2 + 1) * 512]
                            t.matmul(dstp, onesf[:], csr[:, kk * 512:(kk + 1) * 512],
                                     start=True, stop=True).then_inc(S["pe1"], 1)
                    phase_end(t)
                cnt.set("pe1", b_pe1 + NH * 4)
                cnt.set("pe2", b_pe2 + 512 * 4)
                cnt.set("pe3", b_pe3 + 512)

                @blk.scalar
                def _(sc):
                    phase_wait(sc)
                    sc_s = cst[:, CI_S128:CI_S128 + 1]
                    for h in range(NH):
                        for qt in range(32):
                            j = 32 * h + qt
                            par = j % 2
                            sc.wait_ge(S["pe2"], b_pe2 + 4 * j + 1)
                            if j >= 2:
                                sc.wait_ge(S["pe3"], b_pe3 + j - 1)
                            sc.activation(un[par][:, 0:1024], psA0[:], AF.Exp, scale=sc_s,
                                          accum_out=lo[:, 4 * par + 0: 4 * par + 1]
                                          ).then_inc(S["ac1"], 1)
                            sc.wait_ge(S["pe2"], b_pe2 + 4 * j + 2)
                            sc.activation(un[par][:, 1024:2048], psA1[:], AF.Exp, scale=sc_s,
                                          accum_out=lo[:, 4 * par + 1: 4 * par + 2]
                                          ).then_inc(S["ac1"], 1)
                            sc.wait_ge(S["pe2"], b_pe2 + 4 * j + 3)
                            sc.activation(utmp[:, 0:1024], psA0[:], AF.Exp, scale=sc_s,
                                          accum_out=lo[:, 4 * par + 2: 4 * par + 3]
                                          ).then_inc(S["ac1"], 1)
                            sc.wait_ge(S["pe2"], b_pe2 + 4 * j + 4)
                            sc.activation(utmp[:, 1024:2048], psA1[:], AF.Exp, scale=sc_s,
                                          accum_out=lo[:, 4 * par + 3: 4 * par + 4]
                                          ).then_inc(S["ac1"], 1)
                        sc.wait_ge(S["pe3"], b_pe3 + 32 * (h + 1))
                        if h >= 1:
                            sc.wait_ge(S["pe1"], b_pe1 + 4 * (h - 1) + 4)
                        sc.copy(csr[:], csps[:])
                        sc.copy(flt[:], flt[:]).then_inc(S["ac2"], 1)
                    phase_end(sc)
                cnt.set("ac1", b_ac1 + 512 * 4)
                cnt.set("ac2", b_ac2 + NH)

                @blk.vector
                def _(v):
                    phase_wait(v)
                    vs = cnt.get("ve3")
                    for h in range(NH):
                        for qt in range(32):
                            j = 32 * h + qt
                            par = j % 2
                            v.wait_ge(S["ac1"], b_ac1 + 4 * j + 4)
                            a = lw[:, 4 * par + 0:4 * par + 1]
                            bb2 = lw[:, 4 * par + 1:4 * par + 2]
                            l = lw[:, 4 * par + 2:4 * par + 3]
                            il = lw[:, 4 * par + 3:4 * par + 4]
                            v.tensor_tensor(a, lo[:, 4 * par:4 * par + 1],
                                            lo[:, 4 * par + 1:4 * par + 2], ALU.add)
                            v.tensor_tensor(bb2, lo[:, 4 * par + 2:4 * par + 3],
                                            lo[:, 4 * par + 3:4 * par + 4], ALU.add
                                            ).then_inc(S["ve3"], 1)
                            vs += 1
                            v.wait_ge(S["ve3"], vs)
                            v.tensor_tensor(l, a, bb2, ALU.add).then_inc(S["ve3"], 1)
                            vs += 1
                            v.wait_ge(S["ve3"], vs)
                            v.reciprocal(il, l).then_inc(S["ve3"], 1)
                            vs += 1
                            v.wait_ge(S["ve3"], vs)
                            v.tensor_copy(lwb[:, par:par + 1], il)
                            v.memset(flt[:], 0.0).then_inc(S["ve1"], 1)
                        v.wait_ge(S["q%d" % (h % 2)], bq[h % 2] + 48 * (h // 2 + 1))
                        v.wait_ge(S["pe1"], b_pe1 + 4 * h + 2)
                        v.tensor_tensor(vals[:, h * 2048:h * 2048 + 1024],
                                        vh[h % 2][:, 0:1024], psA0[:], ALU.mult)
                        v.memset(flt[:], 0.0).then_inc(S["ve2"], 1)
                        v.wait_ge(S["pe1"], b_pe1 + 4 * h + 4)
                        v.tensor_tensor(vals[:, h * 2048 + 1024:(h + 1) * 2048],
                                        vh[h % 2][:, 1024:2048], psA1[:], ALU.mult)
                        v.memset(flt[:], 0.0).then_inc(S["ve2"], 1)
                    phase_end(v)
                cnt.set("ve1", b_ve1 + 512)
                cnt.set("ve2", b_ve2 + 2 * NH)
                cnt.set("ve3", cnt.get("ve3") + 512 * 3)

        def phase_C_inner(vals, wo_sb):
            (c_xl, c_st, c_gw, c_pe2, c_ve5) = bases("xl", "st", "gw", "pe2", "ve5")
            bq = [cnt.get("q%d" % _i) for _i in range(18)]
            with contextlib.ExitStack() as _es:
                xtc0 = _es.enter_context(nc.sbuf_tensor("xtc0", [128, D], F32))
                xtc1 = _es.enter_context(nc.sbuf_tensor("xtc1", [128, D], F32))
                x1sb0 = _es.enter_context(nc.sbuf_tensor("x1sb0", [128, D], F32))
                x1sb1 = _es.enter_context(nc.sbuf_tensor("x1sb1", [128, D], F32))
                po0 = _es.enter_context(nc.psum_tensor("po0", [128, 512], F32))
                po1 = _es.enter_context(nc.psum_tensor("po1", [128, 512], F32))
                po2 = _es.enter_context(nc.psum_tensor("po2", [128, 512], F32))
                po3 = _es.enter_context(nc.psum_tensor("po3", [128, 512], F32))
                blk = _es.enter_context(nc.Block("phC"))
                xtc = [xtc0, xtc1]
                x1sb = [x1sb0, x1sb1]
                po = [po0, po1, po2, po3]

                @blk.sync
                def _(s):
                    phase_wait(s)
                    for tt in range(16):
                        if tt >= 2:
                            s.wait_ge(S["ve5"], c_ve5 + 4 * (tt - 1))
                        s.dma_start(xtc[tt % 2][:], x_in.ap()[tt * 128:(tt + 1) * 128, :]
                                    ).then_inc(S["q%d" % (tt % 2)], 16)
                        if tt >= 1:
                            s.wait_ge(S["ve5"], c_ve5 + 4 * tt)
                            s.dma_start(x1_sc.ap()[(tt - 1) * 128:tt * 128, :],
                                        x1sb[(tt - 1) % 2][:]).then_inc(S["q%d" % (2 + (tt - 1) % 2)], 16)
                    s.wait_ge(S["ve5"], c_ve5 + 64)
                    s.dma_start(x1_sc.ap()[15 * 128:, :], x1sb[15 % 2][:]).then_inc(S["q3"], 16)
                    s.wait_ge(S["q2"], bq[2] + 8 * 16)
                    s.wait_ge(S["q3"], bq[3] + 8 * 16)
                    phase_end(s)
                cnt.set("q0", bq[0] + 8 * 16)
                cnt.set("q1", bq[1] + 8 * 16)
                cnt.set("q2", bq[2] + 8 * 16)
                cnt.set("q3", bq[3] + 8 * 16)

                @blk.tensor
                def _(t):
                    phase_wait(t)
                    t.wait_ge(S["gw"], c_gw)
                    for tt in range(16):
                        for od in range(4):
                            if tt >= 1:
                                t.wait_ge(S["ve5"], c_ve5 + 4 * (tt - 1) + od + 1)
                            for hc in range(16):
                                mm = t.matmul(po[od][:],
                                              vals[:, hc * 2048 + tt * 128: hc * 2048 + (tt + 1) * 128],
                                              wo_sb[:, hc * 2048 + od * 512: hc * 2048 + (od + 1) * 512],
                                              start=(hc == 0), stop=(hc == 15))
                                if hc == 15:
                                    mm.then_inc(S["pe2"], 1)
                    phase_end(t)
                cnt.set("pe2", c_pe2 + 64)

                @blk.vector
                def _(v):
                    phase_wait(v)
                    for tt in range(16):
                        v.wait_ge(S["q%d" % (tt % 2)], bq[tt % 2] + 16 * (tt // 2 + 1))
                        if tt >= 2:
                            v.wait_ge(S["q%d" % (2 + tt % 2)], bq[2 + tt % 2] + 16 * (tt // 2))
                        for od in range(4):
                            v.wait_ge(S["pe2"], c_pe2 + 4 * tt + od + 1)
                            v.tensor_add(x1sb[tt % 2][:, od * 512:(od + 1) * 512],
                                         po[od][:],
                                         xtc[tt % 2][:, od * 512:(od + 1) * 512])
                            v.memset(flt[:], 0.0).then_inc(S["ve5"], 1)
                    phase_end(v)
                cnt.set("ve5", c_ve5 + 64)

                @blk.scalar
                def _(sc):
                    phase_wait(sc)
                    phase_end(sc)

                @blk.gpsimd
                def _(g):
                    phase_wait(g)
                    phase_end(g)

        def phase_D():
            (d_xl, d_st, d_wl, d_pe1, d_pe2, d_pe3, d_pe4,
             d_ve1, d_ve2, d_ve3, d_ve4, d_ve5, d_ac1, d_ac2, d_ac3) = bases(
                "xl", "st", "wl", "pe1", "pe2", "pe3", "pe4",
                "ve1", "ve2", "ve3", "ve4", "ve5", "ac1", "ac2", "ac3")
            bq = [cnt.get("q%d" % _i) for _i in range(18)]
            with contextlib.ExitStack() as _es:
                x1t0 = _es.enter_context(nc.sbuf_tensor("x1t0", [128, D], F32))
                x1t1 = _es.enter_context(nc.sbuf_tensor("x1t1", [128, D], F32))
                sqt2 = _es.enter_context(nc.sbuf_tensor("sqt2", [128, D], F32))
                xn20 = _es.enter_context(nc.sbuf_tensor("xn20", [128, D], F32))
                xn21 = _es.enter_context(nc.sbuf_tensor("xn21", [128, D], F32))
                yfm = _es.enter_context(nc.sbuf_tensor("yfm", [128, 16 * 512], BF16))
                tfm = _es.enter_context(nc.sbuf_tensor("tfm", [128, 16 * 512], BF16))
                g2 = _es.enter_context(nc.sbuf_tensor("g2", [128, 64 * 512], BF16))
                hb0 = _es.enter_context(nc.sbuf_tensor("hb0", [128, 2048], BF16))
                hb1 = _es.enter_context(nc.sbuf_tensor("hb1", [128, 2048], BF16))
                wub0 = _es.enter_context(nc.sbuf_tensor("wub0", [128, 2048], BF16))
                wub1 = _es.enter_context(nc.sbuf_tensor("wub1", [128, 2048], BF16))
                wgb0 = _es.enter_context(nc.sbuf_tensor("wgb0", [128, 2048], BF16))
                wgb1 = _es.enter_context(nc.sbuf_tensor("wgb1", [128, 2048], BF16))
                w2b0 = _es.enter_context(nc.sbuf_tensor("w2b0", [128, 512], BF16))
                w2b1 = _es.enter_context(nc.sbuf_tensor("w2b1", [128, 512], BF16))
                w2b2 = _es.enter_context(nc.sbuf_tensor("w2b2", [128, 512], BF16))
                w2b3 = _es.enter_context(nc.sbuf_tensor("w2b3", [128, 512], BF16))
                xup0 = _es.enter_context(nc.sbuf_tensor("xup0", [128, 512], F32))
                xup1 = _es.enter_context(nc.sbuf_tensor("xup1", [128, 512], F32))
                sg0 = _es.enter_context(nc.sbuf_tensor("sg0", [128, 512], F32))
                sg1 = _es.enter_context(nc.sbuf_tensor("sg1", [128, 512], F32))
                gc0 = _es.enter_context(nc.sbuf_tensor("gc0", [128, 512], BF16))
                gc1 = _es.enter_context(nc.sbuf_tensor("gc1", [128, 512], BF16))
                xob0 = _es.enter_context(nc.sbuf_tensor("xob0", [128, 512], F32))
                xob1 = _es.enter_context(nc.sbuf_tensor("xob1", [128, 512], F32))
                osb0 = _es.enter_context(nc.sbuf_tensor("osb0", [128, 512], F32))
                osb1 = _es.enter_context(nc.sbuf_tensor("osb1", [128, 512], F32))
                st1c = _es.enter_context(nc.sbuf_tensor("st1c", [128, 1], F32))
                st1d = _es.enter_context(nc.sbuf_tensor("st1d", [128, 1], F32))
                st2c = _es.enter_context(nc.sbuf_tensor("st2c", [128, 1], F32))
                st2d = _es.enter_context(nc.sbuf_tensor("st2d", [128, 1], F32))
                mvc = _es.enter_context(nc.sbuf_tensor("mvc", [128, 4], F32))
                mvd = _es.enter_context(nc.sbuf_tensor("mvd", [128, 4], F32))
                pf0 = _es.enter_context(nc.psum_tensor("pf0", [128, 512], F32))
                pf1 = _es.enter_context(nc.psum_tensor("pf1", [128, 512], F32))
                ph = _es.enter_context(nc.psum_tensor("ph", [128, 512], F32))
                pd0 = _es.enter_context(nc.psum_tensor("pd0", [128, 512], F32))
                pd1 = _es.enter_context(nc.psum_tensor("pd1", [128, 512], F32))
                pd2 = _es.enter_context(nc.psum_tensor("pd2", [128, 512], F32))
                pd3 = _es.enter_context(nc.psum_tensor("pd3", [128, 512], F32))
                blk = _es.enter_context(nc.Block("phD"))
                x1t = [x1t0, x1t1]
                xn2 = [xn20, xn21]
                hb = [hb0, hb1]
                wub = [wub0, wub1]
                wgb = [wgb0, wgb1]
                w2b = [w2b0, w2b1, w2b2, w2b3]
                xup = [xup0, xup1]
                sg = [sg0, sg1]
                gc = [gc0, gc1]
                xob = [xob0, xob1]
                osb = [osb0, osb1]
                stat1 = [st1c, st1d]
                stat2 = [st2c, st2d]
                mv = [mvc, mvd]
                pf = [pf0, pf1]
                pd = [pd0, pd1, pd2, pd3]

                WL_CH = 16 * 16 + 64 * 32 + 256 * 16

                def wl_h(ch, fo):
                    return d_wl + ch * WL_CH + 16 * (fo + 1)

                def wl_ug(ch, c):
                    return d_wl + ch * WL_CH + 256 + 32 * (c + 1)

                def wl_w2(ch, od, c):
                    return d_wl + ch * WL_CH + 256 + 2048 + 16 * (od * 64 + c + 1)

                XL_CH = 20 * 16

                def xl_ln(ch, tt):
                    return d_xl + ch * XL_CH + 16 * (tt + 1)

                def xl_part(ch, k):
                    return d_xl + ch * XL_CH + 64 + 16 * (k + 1)

                PE3_UP, PE3_GT, PE3_H = {}, {}, {}
                _n = 0
                for _ch in range(4):
                    for _c in range(64):
                        _cc = _ch * 64 + _c
                        _n += 1
                        PE3_UP[_cc] = _n
                        _n += 1
                        PE3_GT[_cc] = _n
                        if _c >= 1:
                            _n += 1
                            PE3_H[_cc - 1] = _n
                    _n += 1
                    PE3_H[_ch * 64 + 63] = _n

                @blk.sync
                def _(s):
                    phase_wait(s)
                    s.wait_ge(S["gst"], cnt.get("gst"))
                    for ch in range(4):
                        for tt in range(4):
                            u = ch * 4 + tt
                            if u >= 2:
                                s.wait_ge(S["ve1"], d_ve1 + 8 * (u - 1))
                            s.dma_start(x1t[u % 2][:],
                                        x1_sc.ap()[u * 128:(u + 1) * 128, :]).then_inc(S["q%d" % (u % 2)], 16)
                        for fo in range(16):
                            ff = ch * 16 + fo
                            if ff >= 2:
                                s.wait_ge(S["pe2"], d_pe2 + ff - 1)
                            s.dma_start(hb[ff % 2][:],
                                        h2048_in.ap()[fo, :, :, :].rearrange("p a b -> p (a b)")
                                        ).then_inc(S["q%d" % (4 + ff % 2)], 16)
                        for c in range(64):
                            cc = ch * 64 + c
                            if cc >= 2:
                                s.wait_ge(S["pe3"], d_pe3 + 3 * (cc - 1))
                            s.dma_start(wub[cc % 2][:], wup_sc.ap()[c, :, :]).then_inc(S["q%d" % (6 + cc % 2)], 16)
                            s.dma_start(wgb[cc % 2][:], wgate_sc.ap()[c, :, :]).then_inc(S["q%d" % (6 + cc % 2)], 16)
                        for od in range(4):
                            for c in range(64):
                                dcc = (ch * 4 + od) * 64 + c
                                if dcc >= 4:
                                    s.wait_ge(S["pe4"], d_pe4 + dcc - 3)
                                s.dma_start(w2b[dcc % 4][:],
                                            w2t_sc.ap()[c * 128:(c + 1) * 128,
                                                        od * 512:(od + 1) * 512]
                                            ).then_inc(S["q%d" % (12 + dcc % 4)], 16)
                            for tt in range(4):
                                k = od * 4 + tt
                                idx = ch * 16 + k
                                if idx >= 2:
                                    s.wait_ge(S["ve5"], d_ve5 + idx - 1)
                                s.dma_start(xob[idx % 2][:],
                                            x1_sc.ap()[(ch * 4 + tt) * 128:(ch * 4 + tt + 1) * 128,
                                                       od * 512:(od + 1) * 512]
                                            ).then_inc(S["q%d" % (2 + idx % 2)], 16)
                    phase_end(s)
                for _i, _v in ((0, 8), (1, 8), (2, 32), (3, 32), (4, 32), (5, 32),
                               (6, 128 * 2), (7, 128 * 2),
                               (12, 256), (13, 256), (14, 256), (15, 256)):
                    cnt.set("q%d" % _i, bq[_i] + _v * 16)

                def ln_chain2(v, u):
                    par = u % 2
                    v.wait_ge(S["q%d" % (u % 2)], bq[u % 2] + 16 * (u // 2 + 1))
                    v.reduce_sum(stat1[par][:], x1t[par][:], axis=AX.X).then_inc(S["ve1"], 1)
                    v.wait_ge(S["ac1"], d_ac1 + 2 * u + 1)
                    v.wait_ge(S["ve1"], d_ve1 + 8 * u + 1)
                    m = mv[par][:, 0:1]
                    var = mv[par][:, 1:2]
                    rstd = mv[par][:, 2:3]
                    negmr = mv[par][:, 3:4]
                    v.tensor_scalar(m, stat1[par][:], 1.0 / D, None, ALU.mult).then_inc(S["ve1"], 1)
                    v.tensor_scalar(var, stat2[par][:], 1.0 / D, None, ALU.mult).then_inc(S["ve1"], 1)
                    v.wait_ge(S["ve1"], d_ve1 + 8 * u + 3)
                    v.tensor_tensor(rstd, m, m, ALU.mult).then_inc(S["ve1"], 1)
                    v.wait_ge(S["ve1"], d_ve1 + 8 * u + 4)
                    v.tensor_tensor(var, var, rstd, ALU.subtract).then_inc(S["ve1"], 1)
                    v.wait_ge(S["ac1"], d_ac1 + 2 * u + 2)
                    v.reciprocal(rstd, rstd).then_inc(S["ve1"], 1)
                    v.wait_ge(S["ve1"], d_ve1 + 8 * u + 6)
                    v.scalar_tensor_tensor(negmr, m, -1.0, rstd, ALU.mult, ALU.mult
                                           ).then_inc(S["ve1"], 1)
                    v.wait_ge(S["ve1"], d_ve1 + 8 * u + 7)
                    if u >= 2:
                        v.wait_ge(S["pe1"], d_pe1 + 16 * (u - 1))
                    v.tensor_scalar(xn2[par][:], x1t[par][:], m, rstd,
                                    ALU.subtract, ALU.mult).then_inc(S["ve1"], 1)

                @blk.vector
                def _(v):
                    phase_wait(v)
                    for ch in range(4):
                        ln_chain2(v, ch * 4)
                        for tt in range(4):
                            u = ch * 4 + tt
                            if tt + 1 < 4:
                                ln_chain2(v, u + 1)
                            for k in range(16):
                                g_idx = 16 * u + k
                                v.wait_ge(S["pe1"], d_pe1 + g_idx + 1)
                                v.tensor_scalar(
                                    yfm[:, k * 512 + tt * 128: k * 512 + (tt + 1) * 128],
                                    pf[k % 2][:, 0:128], su_sb[:, k:k + 1], None, ALU.mult)
                                v.memset(flt[:], 0.0).then_inc(S["ve3"], 1)
                        for fo in range(16):
                            ff = ch * 16 + fo
                            v.wait_ge(S["pe2"], d_pe2 + ff + 1)
                            v.tensor_copy(tfm[:, fo * 512:(fo + 1) * 512], pf[ff % 2][:])
                            v.memset(flt[:], 0.0).then_inc(S["ve4"], 1)
                        for c in range(64):
                            cc = ch * 64 + c
                            v.wait_ge(S["ac3"], d_ac3 + 2 * cc + 2)
                            v.tensor_tensor(gc[cc % 2][:], xup[cc % 2][:], sg[cc % 2][:],
                                            ALU.mult)
                            v.memset(flt[:], 0.0).then_inc(S["ve2"], 1)
                            v.wait_ge(S["pe3"], d_pe3 + PE3_H[cc])
                            v.tensor_copy(g2[:, c * 512:(c + 1) * 512], ph[:])
                            v.memset(flt[:], 0.0).then_inc(S["ve2"], 1)
                        for od in range(4):
                            for tt in range(4):
                                k = od * 4 + tt
                                idx = ch * 16 + k
                                v.wait_ge(S["pe4"], d_pe4 + (ch * 4 + od) * 64 + 64)
                                v.wait_ge(S["q%d" % (2 + idx % 2)], bq[2 + idx % 2] + 16 * (idx // 2 + 1))
                                if idx >= 2:
                                    v.wait_ge(S["q%d" % (10 + idx % 2)], bq[10 + idx % 2] + 16 * (idx // 2))
                                v.tensor_add(osb[idx % 2][:], pd[tt][:], xob[idx % 2][:])
                                v.memset(flt[:], 0.0).then_inc(S["ve5"], 1)
                    phase_end(v)
                cnt.set("ve1", d_ve1 + 8 * 16)
                cnt.set("ve2", d_ve2 + 2 * 256)
                cnt.set("ve3", d_ve3 + 256)
                cnt.set("ve4", d_ve4 + 64)
                cnt.set("ve5", d_ve5 + 64)

                @blk.scalar
                def _(sc):
                    phase_wait(sc)
                    for ch in range(4):
                        for tt in range(4):
                            u = ch * 4 + tt
                            par = u % 2
                            sc.wait_ge(S["q%d" % ((ch * 4 + tt) % 2)], bq[(ch * 4 + tt) % 2] + 16 * ((ch * 4 + tt) // 2 + 1))
                            sc.activation(sqt2[:], x1t[par][:], AF.Square,
                                          accum_out=stat2[par][:]).then_inc(S["ac1"], 1)
                            sc.wait_ge(S["ve1"], d_ve1 + 8 * u + 5)
                            sc.activation(mv[par][:, 2:3], mv[par][:, 1:2], AF.Sqrt,
                                          bias=cst[:, CI_EPS:CI_EPS + 1]).then_inc(S["ac1"], 1)
                        for c in range(64):
                            cc = ch * 64 + c
                            sc.wait_ge(S["pe3"], d_pe3 + PE3_UP[cc])
                            if cc >= 2:
                                sc.wait_ge(S["ve2"], d_ve2 + 2 * cc - 3)
                            sc.activation(xup[cc % 2][:], pf0[:], AF.Copy,
                                          scale=scup[:, c:c + 1]).then_inc(S["ac3"], 1)
                            sc.wait_ge(S["pe3"], d_pe3 + PE3_GT[cc])
                            sc.activation(sg[cc % 2][:], pf1[:], AF.Silu,
                                          scale=scgt[:, c:c + 1]).then_inc(S["ac3"], 1)
                    phase_end(sc)
                cnt.set("ac1", d_ac1 + 32)
                cnt.set("ac3", d_ac3 + 2 * 256)

                @blk.tensor
                def _(t):
                    phase_wait(t)
                    for ch in range(4):
                        if ch >= 1:
                            t.wait_ge(S["ac3"], d_ac3 + 128 * ch)
                        for tt in range(4):
                            u = ch * 4 + tt
                            t.wait_ge(S["ve1"], d_ve1 + 8 * u + 8)
                            for k in range(16):
                                g_idx = 16 * u + k
                                if g_idx >= 2:
                                    t.wait_ge(S["ve3"], d_ve3 + g_idx - 1)
                                t.transpose(pf[k % 2][:, 0:128], xn2[u % 2][:, k * 128:(k + 1) * 128],
                                            idnf[:]).then_inc(S["pe1"], 1)
                        t.wait_ge(S["ve3"], d_ve3 + 64 * (ch + 1))
                        for fo in range(16):
                            ff = ch * 16 + fo
                            t.wait_ge(S["q%d" % (4 + ff % 2)], bq[4 + ff % 2] + 16 * (ff // 2 + 1))
                            if ff >= 2:
                                t.wait_ge(S["ve4"], d_ve4 + ff - 1)
                            for fi in range(16):
                                mm = t.matmul(pf[ff % 2][:],
                                              hb[ff % 2][:, fi * 128:(fi + 1) * 128],
                                              yfm[:, fi * 512:(fi + 1) * 512],
                                              start=(fi == 0), stop=(fi == 15))
                                if fi == 15:
                                    mm.then_inc(S["pe2"], 1)
                        t.wait_ge(S["ve4"], d_ve4 + 16 * (ch + 1))
                        for c in range(64):
                            cc = ch * 64 + c
                            t.wait_ge(S["q%d" % (6 + cc % 2)], bq[6 + cc % 2] + 32 * (cc // 2 + 1))
                            if cc >= 1:
                                t.wait_ge(S["ac3"], d_ac3 + 2 * (cc - 1) + 1)
                            for fc in range(16):
                                mm = t.matmul(pf0[:], wub[cc % 2][:, fc * 128:(fc + 1) * 128],
                                              tfm[:, fc * 512:(fc + 1) * 512],
                                              start=(fc == 0), stop=(fc == 15))
                                if fc == 15:
                                    mm.then_inc(S["pe3"], 1)
                            if cc >= 1:
                                t.wait_ge(S["ac3"], d_ac3 + 2 * (cc - 1) + 2)
                            for fc in range(16):
                                mm = t.matmul(pf1[:], wgb[cc % 2][:, fc * 128:(fc + 1) * 128],
                                              tfm[:, fc * 512:(fc + 1) * 512],
                                              start=(fc == 0), stop=(fc == 15))
                                if fc == 15:
                                    mm.then_inc(S["pe3"], 1)
                            if c >= 1:
                                t.wait_ge(S["ve2"], d_ve2 + 2 * (cc - 1) + 1)
                                if cc >= 2:
                                    t.wait_ge(S["ve2"], d_ve2 + 2 * (cc - 2) + 2)
                                t.matmul(ph[:], h128[:], gc[(cc - 1) % 2][:],
                                         start=True, stop=True).then_inc(S["pe3"], 1)
                        cc = ch * 64 + 63
                        t.wait_ge(S["ve2"], d_ve2 + 2 * cc + 1)
                        if cc >= 1:
                            t.wait_ge(S["ve2"], d_ve2 + 2 * (cc - 1) + 2)
                        t.matmul(ph[:], h128[:], gc[cc % 2][:],
                                 start=True, stop=True).then_inc(S["pe3"], 1)
                        for od in range(4):
                            for c in range(64):
                                dcc = (ch * 4 + od) * 64 + c
                                t.wait_ge(S["q%d" % (12 + dcc % 4)], bq[12 + dcc % 4] + 16 * (dcc // 4 + 1))
                                t.wait_ge(S["ve2"], d_ve2 + 2 * (ch * 64 + c) + 2)
                                if c == 0 and (ch * 4 + od) >= 1:
                                    t.wait_ge(S["ve5"], d_ve5 + (ch * 4 + od - 1) * 4 + 4)
                                for tt in range(4):
                                    mm = t.matmul(pd[tt][:],
                                                  g2[:, c * 512 + tt * 128: c * 512 + (tt + 1) * 128],
                                                  w2b[dcc % 4][:],
                                                  start=(c == 0), stop=(c == 63))
                                    if tt == 3:
                                        mm.then_inc(S["pe4"], 1)
                    phase_end(t)
                cnt.set("pe1", d_pe1 + 256)
                cnt.set("pe2", d_pe2 + 64)
                cnt.set("pe3", d_pe3 + 3 * 256)
                cnt.set("pe4", d_pe4 + 1024)

                @blk.gpsimd
                def _(g):
                    phase_wait(g)
                    for ch in range(4):
                        for od in range(4):
                            for tt in range(4):
                                k = od * 4 + tt
                                idx = ch * 16 + k
                                g.wait_ge(S["ve5"], d_ve5 + idx + 1)
                                g.dma_start(out.ap()[(ch * 4 + tt) * 128:(ch * 4 + tt + 1) * 128,
                                                     od * 512:(od + 1) * 512],
                                            osb[idx % 2][:]).then_inc(S["q%d" % (10 + idx % 2)], 16)
                    g.wait_ge(S["q10"], bq[10] + 16 * 32)
                    g.wait_ge(S["q11"], bq[11] + 16 * 32)
                    phase_end(g)
                cnt.set("q10", bq[10] + 16 * 32)
                cnt.set("q11", bq[11] + 16 * 32)
            close_phase()

        phase_setup()
        phase_BC()
        phase_D()
    return nc



_H2048 = None
_H128 = None
_H64 = None


def _consts():
    global _H2048, _H128, _H64
    if _H2048 is None:
        _H2048 = _hadamard(2048)
        _H128 = _hadamard(128)
        _H64 = _hadamard(64)
    bf = ml_dtypes.bfloat16
    h2048 = np.ascontiguousarray(
        _H2048.reshape(16, 128, 16, 128).transpose(2, 1, 0, 3)).astype(bf)
    h128 = _H128.astype(bf)
    h64 = _H64.astype(bf)
    ident = np.eye(128, dtype=bf)
    identf = np.eye(128, dtype=np.float32)
    ones = np.ones((1, 128), np.float32)
    consts = np.zeros((1, NC_CONST), np.float32)
    consts[0, CI_S128] = S128
    consts[0, CI_EPS] = 1e-5
    return h2048, h128, h64, ident, identf, ones, consts


def kernel(L0_attn_x, L0_attn_w_qkv, L0_attn_w_o,
           L0_ffn_SU_ug, L0_ffn_SU_dp, L0_ffn_Wscale_ug, L0_ffn_Wscale_dp,
           L0_ffn_W_up, L0_ffn_W_gate, L0_ffn_W_down):
    x = np.asarray(L0_attn_x, np.float32)
    wqkv = np.asarray(L0_attn_w_qkv, np.float32)
    wo = np.ascontiguousarray(np.asarray(L0_attn_w_o, np.float32))
    su_ug = np.asarray(L0_ffn_SU_ug, np.float32).reshape(1, D)
    su_dp = np.asarray(L0_ffn_SU_dp, np.float32).reshape(1, INTER)
    wsc_ug = np.asarray(L0_ffn_Wscale_ug, np.float32).reshape(1, 2 * INTER)
    wsc_dp = np.asarray(L0_ffn_Wscale_dp, np.float32).reshape(1, D)
    wup = np.asarray(L0_ffn_W_up, np.float32)
    wgate = np.asarray(L0_ffn_W_gate, np.float32)
    wdown = np.asarray(L0_ffn_W_down, np.float32)

    wqkv_in = np.ascontiguousarray(
        wqkv.reshape(16, 128, 48, 128).transpose(2, 1, 0, 3))
    wup_in = np.ascontiguousarray(
        wup.reshape(64, 128, 16, 128).transpose(0, 3, 2, 1))
    wgate_in = np.ascontiguousarray(
        wgate.reshape(64, 128, 16, 128).transpose(0, 3, 2, 1))
    wdownT_in = np.ascontiguousarray(wdown.T)

    h2048, h128, h64, ident, identf, ones, consts = _consts()

    nc = build_nc()
    in_maps = []
    for c in range(8):
        b, p = divmod(c, 2)
        own = x[b, p * TOK:(p + 1) * TOK, :]
        other = x[b, (1 - p) * TOK:(2 - p) * TOK, :]
        x_core = np.ascontiguousarray(np.concatenate([own, other], axis=0))
        in_maps.append(dict(
            x_in=x_core, wqkv_in=wqkv_in, wo_in=wo,
            wup_in=wup_in, wgate_in=wgate_in, wdownT_in=wdownT_in,
            su_ug_in=su_ug, su_dp_in=su_dp, wsc_ug_in=wsc_ug, wsc_dp_in=wsc_dp,
            h2048_in=h2048, h128_in=h128, h64_in=h64,
            ident_in=ident, identf_in=identf, ones_in=ones, consts_in=consts,
        ))
    res = run_bass_kernel_spmd(nc, in_maps, core_ids=list(range(8)))
    outp = np.empty((4, TFULL, D), np.float32)
    for c in range(8):
        b, p = divmod(c, 2)
        outp[b, p * TOK:(p + 1) * TOK, :] = res.results[c]["out"]
    return outp



# version 09
# speedup vs baseline: 1.2220x; 1.0529x over previous
import math
import contextlib
import numpy as np
import ml_dtypes
import concourse.bass as bass
import concourse.mybir as mybir
from concourse.bass_utils import run_bass_kernel_spmd

F32 = mybir.dt.float32
BF16 = mybir.dt.bfloat16
AF = mybir.ActivationFunctionType
ALU = mybir.AluOpType
AX = mybir.AxisListType

D = 2048
TOK = 2048
TFULL = 4096
NH = 16
INTER = 8192
S128 = 1.0 / math.sqrt(128.0)
C1 = 1.0 / math.sqrt(2048.0)
CDP = 1.0 / math.sqrt(8192.0)

CI_S128, CI_EPS = 0, 1
NC_CONST = 4

SEMNAMES = ["bar", "ld", "st", "gw", "gst", "xl", "wl",
            "pe1", "pe2", "pe3", "pe4",
            "ve1", "ve2", "ve3", "ve4", "ve5",
            "ac1", "ac2", "ac3"] + ["q%d" % i for i in range(22)]


def _hadamard(n):
    h = np.array([[1.0]])
    while h.shape[0] < n:
        h = np.block([[h, h], [h, -h]])
    return h


class Ctr:
    def __init__(self):
        self.c = {}

    def inc(self, instr, sem, name, k=1):
        instr.then_inc(sem, k)
        self.c[name] = self.c.get(name, 0) + k
        return self.c[name]

    def get(self, name):
        return self.c.get(name, 0)

    def set(self, name, v):
        self.c[name] = v


def build_nc():
    nc = bass.Bass("TRN2", target_bir_lowering=False)

    x_in = nc.dram_tensor("x_in", [TFULL, D], F32, kind="ExternalInput")
    wqkv_in = nc.dram_tensor("wqkv_in", [48, 128, 16, 128], F32, kind="ExternalInput")
    wo_in = nc.dram_tensor("wo_in", [D, D], F32, kind="ExternalInput")
    wup_in = nc.dram_tensor("wup_in", [64, 128, 16, 128], F32, kind="ExternalInput")
    wgate_in = nc.dram_tensor("wgate_in", [64, 128, 16, 128], F32, kind="ExternalInput")
    wdownT_in = nc.dram_tensor("wdownT_in", [INTER, D], F32, kind="ExternalInput")
    su_ug_in = nc.dram_tensor("su_ug_in", [1, D], F32, kind="ExternalInput")
    su_dp_in = nc.dram_tensor("su_dp_in", [1, INTER], F32, kind="ExternalInput")
    wsc_ug_in = nc.dram_tensor("wsc_ug_in", [1, 2 * INTER], F32, kind="ExternalInput")
    wsc_dp_in = nc.dram_tensor("wsc_dp_in", [1, D], F32, kind="ExternalInput")
    h2048_in = nc.dram_tensor("h2048_in", [16, 128, 16, 128], BF16, kind="ExternalInput")
    h128_in = nc.dram_tensor("h128_in", [128, 128], BF16, kind="ExternalInput")
    h64_in = nc.dram_tensor("h64_in", [64, 64], BF16, kind="ExternalInput")
    ident_in = nc.dram_tensor("ident_in", [128, 128], BF16, kind="ExternalInput")
    identf_in = nc.dram_tensor("identf_in", [128, 128], F32, kind="ExternalInput")
    ones_in = nc.dram_tensor("ones_in", [1, 128], F32, kind="ExternalInput")
    consts_in = nc.dram_tensor("consts_in", [1, NC_CONST], F32, kind="ExternalInput")

    out = nc.dram_tensor("out", [TOK, D], F32, kind="ExternalOutput")

    q_sc = nc.dram_tensor("q_sc", [D, TFULL], BF16, kind="Internal")
    k_sc = nc.dram_tensor("k_sc", [D, TFULL], BF16, kind="Internal")
    v_sc = nc.dram_tensor("v_sc", [D, TOK], BF16, kind="Internal")
    x1_sc = nc.dram_tensor("x1_sc", [TOK, D], F32, kind="Internal")
    wup_sc = nc.dram_tensor("wup_sc", [64, 128, 2048], BF16, kind="Internal")
    wgate_sc = nc.dram_tensor("wgate_sc", [64, 128, 2048], BF16, kind="Internal")
    w2pre_sc = nc.dram_tensor("w2pre_sc", [INTER, D], BF16, kind="Internal")
    w2t_sc = nc.dram_tensor("w2t_sc", [INTER, D], BF16, kind="Internal")

    def bcast_row(dram_t, n, col0=0):
        a = dram_t.ap()
        return bass.AP(a.tensor, a.offset + col0, [[0, 128], [1, n]])

    def strided_col(dram_t, p, cols, col0=0):
        a = dram_t.ap()
        return bass.AP(a.tensor, a.offset + col0, [[1, p], [p, cols]])

    cnt = Ctr()

    with contextlib.ExitStack() as stack:
        sems = {sn: stack.enter_context(nc.semaphore(sn)) for sn in SEMNAMES}
        S = sems

        cst = stack.enter_context(nc.sbuf_tensor("cst", [128, NC_CONST], F32))
        onesf = stack.enter_context(nc.sbuf_tensor("onesf", [1, 128], F32))
        idn = stack.enter_context(nc.sbuf_tensor("idn", [128, 128], BF16))
        idnf = stack.enter_context(nc.sbuf_tensor("idnf", [128, 128], F32))
        flt = stack.enter_context(nc.sbuf_tensor("flt", [128, 1], F32))
        h128 = stack.enter_context(nc.sbuf_tensor("h128", [128, 128], BF16))
        h64 = stack.enter_context(nc.sbuf_tensor("h64", [64, 64], BF16))
        su_sb = stack.enter_context(nc.sbuf_tensor("su_sb", [128, 16], F32))
        scup = stack.enter_context(nc.sbuf_tensor("scup", [128, 64], F32))
        scgt = stack.enter_context(nc.sbuf_tensor("scgt", [128, 64], F32))

        phase_no = [0]

        def phase_wait(eng):
            if phase_no[0] > 0:
                eng.wait_ge(S["bar"], 5 * phase_no[0])

        def phase_end(eng):
            eng.sem_inc(S["bar"], 1)

        def close_phase():
            phase_no[0] += 1

        def bases(*names):
            return tuple(cnt.get(n) for n in names)

        def phase_setup():
            b_ld, b_st, b_pe1, b_pe2, b_ve1, b_ve2, b_ac1, b_ac2 = bases(
                "ld", "st", "pe1", "pe2", "ve1", "ve2", "ac1", "ac2")
            bq = [cnt.get("q%d" % _i) for _i in range(22)]
            with contextlib.ExitStack() as _es:
                t_ug = _es.enter_context(nc.sbuf_tensor("t_ug", [128, 64], F32))
                t_dp = _es.enter_context(nc.sbuf_tensor("t_dp", [128, 64], F32))
                bcdp = _es.enter_context(nc.sbuf_tensor("bcdp", [128, D], F32))
                sdp_row = _es.enter_context(nc.sbuf_tensor("sdp_row", [1, D], F32))
                wdf0 = _es.enter_context(nc.sbuf_tensor("wdf0", [128, D], F32))
                wdf1 = _es.enter_context(nc.sbuf_tensor("wdf1", [128, D], F32))
                wdb0 = _es.enter_context(nc.sbuf_tensor("wdb0", [128, D], BF16))
                wdb1 = _es.enter_context(nc.sbuf_tensor("wdb1", [128, D], BF16))
                wcb0 = _es.enter_context(nc.sbuf_tensor("wcb0", [64, 1024], BF16))
                wcb1 = _es.enter_context(nc.sbuf_tensor("wcb1", [64, 1024], BF16))
                wob0 = _es.enter_context(nc.sbuf_tensor("wob0", [64, 1024], BF16))
                wob1 = _es.enter_context(nc.sbuf_tensor("wob1", [64, 1024], BF16))
                ps_bc = _es.enter_context(nc.psum_tensor("ps_bc", [128, 512], F32))
                ps_h0 = _es.enter_context(nc.psum_tensor("ps_h0", [64, 1024], F32))
                _blkcm = nc.Block("setup")
                blk = _blkcm.__enter__()
                wdf = [wdf0, wdf1]
                wdb = [wdb0, wdb1]
                wcb = [wcb0, wcb1]
                wob = [wob0, wob1]
                psh = [ps_h0, ps_h0]

                @blk.sync
                def _(s):
                    phase_wait(s)
                    for dst, src in (
                        (cst[:], bcast_row(consts_in, NC_CONST)),
                        (onesf[:], ones_in.ap()),
                        (idn[:], ident_in.ap()),
                        (idnf[:], identf_in.ap()),
                        (h128[:], h128_in.ap()),
                        (h64[:], h64_in.ap()),
                    ):
                        s.dma_start(dst, src).then_inc(S["ld"], 16)
                    s.dma_start(su_sb[:], strided_col(su_ug_in, 128, 16),
                                allow_slow_non_contiguous=True).then_inc(S["ld"], 16)
                    s.dma_start(t_ug[:], strided_col(wsc_ug_in, 128, 64),
                                allow_slow_non_contiguous=True).then_inc(S["ld"], 16)
                    s.dma_start(t_dp[:], strided_col(su_dp_in, 128, 64),
                                allow_slow_non_contiguous=True).then_inc(S["ld"], 16)
                    s.dma_start(sdp_row[:], wsc_dp_in.ap()).then_inc(S["ld"], 16)
                    s.dma_start(scgt[:], strided_col(wsc_ug_in, 128, 64, col0=INTER),
                                allow_slow_non_contiguous=True).then_inc(S["ld"], 16)
                    for i in range(64):
                        if i >= 2:
                            s.wait_ge(S["ve2"], b_ve2 + (i - 1))
                        s.dma_start(wdf[i % 2][:], wdownT_in.ap()[i * 128:(i + 1) * 128, :]
                                    ).then_inc(S["q%d" % (i % 2)], 16)
                        if i >= 1:
                            s.wait_ge(S["ve2"], b_ve2 + i)
                            s.dma_start(w2pre_sc.ap()[(i - 1) * 128:i * 128, :],
                                        wdb[(i - 1) % 2][:]).then_inc(S["q%d" % (2 + (i - 1) % 2)], 16)
                    s.wait_ge(S["ve2"], b_ve2 + 64)
                    s.dma_start(w2pre_sc.ap()[63 * 128:, :], wdb[1][:]).then_inc(S["q3"], 16)
                    s.wait_ge(S["q2"], bq[2] + 32 * 16)
                    s.wait_ge(S["q3"], bq[3] + 32 * 16)
                    a = w2pre_sc.ap()
                    b = w2t_sc.ap()
                    for j in range(256):
                        r, odh = divmod(j, 2)
                        src = bass.AP(a.tensor, a.offset + r * 2048 + odh * 1024,
                                      [[128 * 2048, 64], [1, 1024]])
                        if j >= 2:
                            s.wait_ge(S["pe2"], b_pe2 + (j - 1))
                        s.dma_start(wcb[j % 2][:], src).then_inc(S["q%d" % (4 + j % 2)], 16)
                        if j >= 1:
                            ro, odho = divmod(j - 1, 2)
                            s.wait_ge(S["ac2"], b_ac2 + j)
                            dst = bass.AP(b.tensor, b.offset + ro * 2048 + odho * 1024,
                                          [[128 * 2048, 64], [1, 1024]])
                            s.dma_start(dst, wob[(j - 1) % 2][:]).then_inc(S["q%d" % (6 + (j - 1) % 2)], 16)
                    s.wait_ge(S["ac2"], b_ac2 + 256)
                    dst = bass.AP(b.tensor, b.offset + 127 * 2048 + 1024,
                                  [[128 * 2048, 64], [1, 1024]])
                    s.dma_start(dst, wob[1][:]).then_inc(S["q7"], 16)
                    s.wait_ge(S["q6"], bq[6] + 128 * 16)
                    s.wait_ge(S["q7"], bq[7] + 128 * 16)
                    phase_end(s)
                cnt.set("ld", b_ld + 16 * 11)
                for _i, _v in ((0, 32), (1, 32), (2, 32), (3, 32), (4, 128), (5, 128), (6, 128), (7, 128)):
                    cnt.set("q%d" % _i, bq[_i] + _v * 16)

                @blk.vector
                def _(v):
                    phase_wait(v)
                    v.wait_ge(S["ld"], b_ld + 176)
                    v.tensor_tensor(scup[:], t_ug[:], t_dp[:], ALU.mult).then_inc(S["ve1"], 1)
                    v.wait_ge(S["ve1"], b_ve1 + 1)
                    v.tensor_scalar(scup[:], scup[:], C1, None, ALU.mult)
                    v.tensor_scalar(scgt[:], scgt[:], C1, None, ALU.mult)
                    v.tensor_scalar(sdp_row[:], sdp_row[:], CDP, None, ALU.mult
                                    ).then_inc(S["ve1"], 1)
                    for i in range(64):
                        v.wait_ge(S["q%d" % (i % 2)], bq[i % 2] + 16 * (i // 2 + 1))
                        if i == 0:
                            v.wait_ge(S["ac1"], b_ac1 + 4)
                        if i >= 2:
                            v.wait_ge(S["q%d" % (2 + i % 2)], bq[2 + i % 2] + 16 * (i // 2))
                        v.tensor_tensor(wdb[i % 2][:], wdf[i % 2][:], bcdp[:], ALU.mult)
                        v.memset(flt[:], 0.0).then_inc(S["ve2"], 1)
                    phase_end(v)
                cnt.set("ve1", b_ve1 + 2)
                cnt.set("ve2", b_ve2 + 64)

                @blk.tensor
                def _(t):
                    phase_wait(t)
                    t.wait_ge(S["ve1"], b_ve1 + 2)
                    for kk in range(4):
                        if kk >= 1:
                            t.wait_ge(S["ac1"], b_ac1 + kk)
                        t.matmul(ps_bc[:], onesf[:], sdp_row[:, kk * 512:(kk + 1) * 512],
                                 start=True, stop=True).then_inc(S["pe1"], 1)
                    for j in range(256):
                        t.wait_ge(S["q%d" % (4 + j % 2)], bq[4 + j % 2] + 16 * (j // 2 + 1))
                        if j >= 1:
                            t.wait_ge(S["ac2"], b_ac2 + j)
                        for nn in range(2):
                            mm = t.matmul(psh[j % 2][:, nn * 512:(nn + 1) * 512], h64[:],
                                          wcb[j % 2][:, nn * 512:(nn + 1) * 512],
                                          start=True, stop=True)
                            if nn == 1:
                                mm.then_inc(S["pe2"], 1)
                    phase_end(t)
                cnt.set("pe1", b_pe1 + 4)
                cnt.set("pe2", b_pe2 + 256)

                @blk.scalar
                def _(sc):
                    phase_wait(sc)
                    for kk in range(4):
                        sc.wait_ge(S["pe1"], b_pe1 + kk + 1)
                        sc.copy(bcdp[:, kk * 512:(kk + 1) * 512], ps_bc[:]).then_inc(S["ac1"], 1)
                    for j in range(256):
                        sc.wait_ge(S["pe2"], b_pe2 + j + 1)
                        if j >= 2:
                            sc.wait_ge(S["q%d" % (6 + j % 2)], bq[6 + j % 2] + 16 * (j // 2))
                        sc.copy(wob[j % 2][:], psh[j % 2][:]).then_inc(S["ac2"], 1)
                    phase_end(sc)
                cnt.set("ac1", b_ac1 + 4)
                cnt.set("ac2", b_ac2 + 256)

                @blk.gpsimd
                def _(g):
                    phase_wait(g)
                    phase_end(g)
                _blkcm.__exit__(None, None, None)
                phase_A()
            close_phase()
            close_phase()

        def phase_A():
            (b_xl, b_st, b_gw, b_gst, b_wl, b_pe1, b_pe2,
             b_ve1, b_ve2, b_ve3, b_ac1, b_ac2) = bases(
                "xl", "st", "gw", "gst", "wl", "pe1", "pe2",
                "ve1", "ve2", "ve3", "ac1", "ac2")
            bq = [cnt.get("q%d" % _i) for _i in range(22)]
            with contextlib.ExitStack() as _es:
                xt0 = _es.enter_context(nc.sbuf_tensor("xt0", [128, D], F32))
                xt1 = _es.enter_context(nc.sbuf_tensor("xt1", [128, D], F32))
                sqt = _es.enter_context(nc.sbuf_tensor("sqt", [128, D], F32))
                xn0 = _es.enter_context(nc.sbuf_tensor("xn0", [128, D], BF16))
                xn1 = _es.enter_context(nc.sbuf_tensor("xn1", [128, D], BF16))
                xnfm = _es.enter_context(nc.sbuf_tensor("xnfm", [128, 16 * 2048], BF16))
                wqb0 = _es.enter_context(nc.sbuf_tensor("wqb0", [128, 2048], BF16))
                wqb1 = _es.enter_context(nc.sbuf_tensor("wqb1", [128, 2048], BF16))
                qkt0 = _es.enter_context(nc.sbuf_tensor("qkt0", [128, 512], BF16))
                qkt1 = _es.enter_context(nc.sbuf_tensor("qkt1", [128, 512], BF16))
                qkt2 = _es.enter_context(nc.sbuf_tensor("qkt2", [128, 512], BF16))
                qkt3 = _es.enter_context(nc.sbuf_tensor("qkt3", [128, 512], BF16))
                st1a = _es.enter_context(nc.sbuf_tensor("st1a", [128, 1], F32))
                st1b = _es.enter_context(nc.sbuf_tensor("st1b", [128, 1], F32))
                st2a = _es.enter_context(nc.sbuf_tensor("st2a", [128, 1], F32))
                st2b = _es.enter_context(nc.sbuf_tensor("st2b", [128, 1], F32))
                mva = _es.enter_context(nc.sbuf_tensor("mva", [128, 4], F32))
                mvb = _es.enter_context(nc.sbuf_tensor("mvb", [128, 4], F32))
                wpc0 = _es.enter_context(nc.sbuf_tensor("wpc0", [128, 2048], BF16))
                wpc1 = _es.enter_context(nc.sbuf_tensor("wpc1", [128, 2048], BF16))
                pq0 = _es.enter_context(nc.psum_tensor("pq0", [128, 512], F32))
                pq1 = _es.enter_context(nc.psum_tensor("pq1", [128, 512], F32))
                pt0 = _es.enter_context(nc.psum_tensor("pt0", [128, 128], BF16))
                pt1 = _es.enter_context(nc.psum_tensor("pt1", [128, 128], BF16))
                blk = _es.enter_context(nc.Block("phA"))
                xt = [xt0, xt1]
                xn = [xn0, xn1]
                wqb = [wqb0, wqb1]
                qkt = [qkt0, qkt1, qkt2, qkt3]
                stat1 = [st1a, st1b]
                stat2 = [st2a, st2b]
                mv = [mva, mvb]
                pq = [pq0, pq1]
                pt = [pt0, pt1]
                wpc = [wpc0, wpc1]

                NT = 16
                CT = (48, 32)
                E0 = (0, 192)
                NE = 320

                def qdst(ctg, tc):
                    tb = 0 if ctg < 48 else 1
                    ct = ctg if ctg < 48 else ctg - 48
                    if ct < 16:
                        return q_sc.ap()[ct * 128:(ct + 1) * 128,
                                         tb * 2048 + tc * 512: tb * 2048 + (tc + 1) * 512]
                    if ct < 32:
                        return k_sc.ap()[(ct - 16) * 128:(ct - 15) * 128,
                                         tb * 2048 + tc * 512: tb * 2048 + (tc + 1) * 512]
                    return v_sc.ap()[(ct - 32) * 128:(ct - 31) * 128,
                                     tc * 512:(tc + 1) * 512]

                @blk.sync
                def _(s):
                    phase_wait(s)
                    for tb in range(2):
                        for tt in range(NT):
                            u = tb * NT + tt
                            if u >= 2:
                                s.wait_ge(S["ve1"], b_ve1 + 8 * (u - 1))
                            s.dma_start(xt[u % 2][:], x_in.ap()[u * 128:(u + 1) * 128, :]
                                        ).then_inc(S["q%d" % (8 + u % 2)], 16)
                        for ei in range(CT[tb] * 4):
                            e = E0[tb] + ei
                            s.wait_ge(S["ve2"], b_ve2 + e + 1)
                            ctg, tc = divmod(e, 4)
                            s.dma_start(qdst(ctg, tc), qkt[e % 4][:]).then_inc(S["q%d" % (12 + e % 4)], 16)
                    for _k in range(4):
                        s.wait_ge(S["q%d" % (12 + _k)], bq[12 + _k] + 80 * 16)
                    phase_end(s)
                cnt.set("q8", bq[8] + 16 * 16)
                cnt.set("q9", bq[9] + 16 * 16)
                for _k in range(4):
                    cnt.set("q%d" % (12 + _k), bq[12 + _k] + 80 * 16)

                @blk.gpsimd
                def _(g):
                    phase_wait(g)
                    pj = 0
                    for ctg in range(80):
                        if ctg >= 2:
                            g.wait_ge(S["pe2"], b_pe2 + 4 * (ctg - 1))
                        g.dma_start(wqb[ctg % 2][:],
                                    wqkv_in.ap()[ctg if ctg < 48 else ctg - 48, :, :, :]
                                    .rearrange("p a b -> p (a b)")).then_inc(S["q%d" % (10 + ctg % 2)], 16)
                        for _ in range(2):
                            if pj < 128:
                                win = wup_in if pj < 64 else wgate_in
                                wsc = wup_sc if pj < 64 else wgate_sc
                                c = pj % 64
                                if pj >= 2:
                                    g.wait_ge(S["gst"], b_gst + 16 * (pj - 1))
                                g.dma_start(wpc[pj % 2][:],
                                            win.ap()[c, :, :, :].rearrange("p a b -> p (a b)")
                                            ).then_inc(S["q%d" % (16 + pj % 2)], 16)
                                if pj >= 1:
                                    g.wait_ge(S["q%d" % (16 + (pj - 1) % 2)],
                                              bq[16 + (pj - 1) % 2] + 16 * ((pj - 1) // 2 + 1))
                                    wino = wup_sc if pj - 1 < 64 else wgate_sc
                                    g.dma_start(wino.ap()[(pj - 1) % 64, :, :],
                                                wpc[(pj - 1) % 2][:]).then_inc(S["gst"], 16)
                                pj += 1
                    g.wait_ge(S["q%d" % (16 + 127 % 2)], bq[16 + 127 % 2] + 16 * (127 // 2 + 1))
                    g.dma_start(wgate_sc.ap()[63, :, :], wpc[127 % 2][:]).then_inc(S["gst"], 16)
                    g.wait_ge(S["gst"], b_gst + 128 * 16)
                    phase_end(g)
                cnt.set("q10", bq[10] + 40 * 16)
                cnt.set("q11", bq[11] + 40 * 16)
                cnt.set("q16", bq[16] + 64 * 16)
                cnt.set("q17", bq[17] + 64 * 16)
                cnt.set("gst", b_gst + 128 * 16)

                def ln_chain(v, u):
                    par = u % 2
                    v.wait_ge(S["q%d" % (8 + u % 2)], bq[8 + u % 2] + 16 * (u // 2 + 1))
                    v.reduce_sum(stat1[par][:], xt[par][:], axis=AX.X).then_inc(S["ve1"], 1)
                    v.wait_ge(S["ac1"], b_ac1 + 2 * u + 1)
                    v.wait_ge(S["ve1"], b_ve1 + 8 * u + 1)
                    m = mv[par][:, 0:1]
                    var = mv[par][:, 1:2]
                    rstd = mv[par][:, 2:3]
                    negmr = mv[par][:, 3:4]
                    v.tensor_scalar(m, stat1[par][:], 1.0 / D, None, ALU.mult).then_inc(S["ve1"], 1)
                    v.tensor_scalar(var, stat2[par][:], 1.0 / D, None, ALU.mult).then_inc(S["ve1"], 1)
                    v.wait_ge(S["ve1"], b_ve1 + 8 * u + 3)
                    v.tensor_tensor(rstd, m, m, ALU.mult).then_inc(S["ve1"], 1)
                    v.wait_ge(S["ve1"], b_ve1 + 8 * u + 4)
                    v.tensor_tensor(var, var, rstd, ALU.subtract).then_inc(S["ve1"], 1)
                    v.wait_ge(S["ac1"], b_ac1 + 2 * u + 2)
                    v.reciprocal(rstd, rstd).then_inc(S["ve1"], 1)
                    v.wait_ge(S["ve1"], b_ve1 + 8 * u + 6)
                    v.scalar_tensor_tensor(negmr, m, -1.0, rstd, ALU.mult, ALU.mult
                                           ).then_inc(S["ve1"], 1)
                    v.wait_ge(S["ve1"], b_ve1 + 8 * u + 7)
                    if u >= 2:
                        v.wait_ge(S["pe1"], b_pe1 + 16 * (u - 1))
                    v.tensor_scalar(xn[par][:], xt[par][:], m, rstd,
                                    ALU.subtract, ALU.mult).then_inc(S["ve1"], 1)

                @blk.vector
                def _(v):
                    phase_wait(v)
                    for tb in range(2):
                        ln_chain(v, tb * NT)
                        for tt in range(NT):
                            u = tb * NT + tt
                            if tt + 1 < NT:
                                ln_chain(v, u + 1)
                            for k in range(16):
                                g_idx = 16 * u + k
                                v.wait_ge(S["pe1"], b_pe1 + g_idx + 1)
                                v.tensor_copy(
                                    xnfm[:, k * 2048 + tt * 128: k * 2048 + (tt + 1) * 128],
                                    pt[k % 2][:])
                                v.memset(flt[:], 0.0).then_inc(S["ve3"], 1)
                        for ei in range(CT[tb] * 4):
                            e = E0[tb] + ei
                            v.wait_ge(S["pe2"], b_pe2 + e + 1)
                            if e >= 4:
                                v.wait_ge(S["q%d" % (12 + e % 4)], bq[12 + e % 4] + 16 * (e // 4))
                            v.tensor_copy(qkt[e % 4][:], pq[e % 2][:])
                            v.memset(flt[:], 0.0).then_inc(S["ve2"], 1)
                    phase_end(v)
                cnt.set("ve1", b_ve1 + 8 * 32)
                cnt.set("ve2", b_ve2 + NE)
                cnt.set("ve3", b_ve3 + 512)

                @blk.scalar
                def _(sc):
                    phase_wait(sc)
                    sc.wait_ge(S["ld"], cnt.get("ld"))
                    for tb in range(2):
                        for tt in range(NT):
                            u = tb * NT + tt
                            par = u % 2
                            sc.wait_ge(S["q%d" % (8 + u % 2)], bq[8 + u % 2] + 16 * (u // 2 + 1))
                            sc.activation(sqt[:], xt[par][:], AF.Square,
                                          accum_out=stat2[par][:]).then_inc(S["ac1"], 1)
                            sc.wait_ge(S["ve1"], b_ve1 + 8 * u + 5)
                            sc.activation(mv[par][:, 2:3], mv[par][:, 1:2], AF.Sqrt,
                                          bias=cst[:, CI_EPS:CI_EPS + 1]).then_inc(S["ac1"], 1)
                    phase_end(sc)
                cnt.set("ac1", b_ac1 + 64)

                @blk.tensor
                def _(t):
                    phase_wait(t)
                    t.wait_ge(S["ld"], cnt.get("ld"))
                    for tb in range(2):
                        for tt in range(NT):
                            u = tb * NT + tt
                            t.wait_ge(S["ve1"], b_ve1 + 8 * u + 8)
                            for k in range(16):
                                g_idx = 16 * u + k
                                if g_idx >= 2:
                                    t.wait_ge(S["ve3"], b_ve3 + g_idx - 1)
                                t.transpose(pt[k % 2][:], xn[u % 2][:, k * 128:(k + 1) * 128],
                                            idn[:]).then_inc(S["pe1"], 1)
                        t.wait_ge(S["ve3"], b_ve3 + 256 * (tb + 1))
                        for ei in range(CT[tb] * 4):
                            e = E0[tb] + ei
                            ctg, tc = divmod(e, 4)
                            t.wait_ge(S["q%d" % (10 + ctg % 2)], bq[10 + ctg % 2] + 16 * (ctg // 2 + 1))
                            if e >= 2:
                                t.wait_ge(S["ve2"], b_ve2 + e - 1)
                            for fc in range(16):
                                t.matmul(pq[e % 2][:],
                                         wqb[ctg % 2][:, fc * 128:(fc + 1) * 128],
                                         xnfm[:, fc * 2048 + tc * 512: fc * 2048 + (tc + 1) * 512],
                                         start=(fc == 0), stop=(fc == 15)
                                         ).then_inc(S["pe2"], 1) if fc == 15 else \
                                    t.matmul(pq[e % 2][:],
                                             wqb[ctg % 2][:, fc * 128:(fc + 1) * 128],
                                             xnfm[:, fc * 2048 + tc * 512: fc * 2048 + (tc + 1) * 512],
                                             start=(fc == 0), stop=False)
                    phase_end(t)
                cnt.set("pe1", b_pe1 + 512)
                cnt.set("pe2", b_pe2 + NE)

        def phase_BC():
            (b_xl, b_gw, b_pe1, b_pe2, b_pe3,
             b_ve1, b_ve2, b_ac1, b_ac2) = bases(
                "xl", "gw", "pe1", "pe2", "pe3", "ve1", "ve2", "ac1", "ac2")
            with contextlib.ExitStack() as _es:
                vals = _es.enter_context(nc.sbuf_tensor("vals", [128, NH * 2048], BF16))
                wo_sb = _es.enter_context(nc.sbuf_tensor("wo_sb", [128, 16 * 2048], BF16))
                csr = _es.enter_context(nc.sbuf_tensor("csr", [1, TOK], F32))
                phase_B_inner(vals, wo_sb, csr,
                              b_xl, b_gw, b_pe1, b_pe2, b_pe3,
                              b_ve1, b_ve2, b_ac1, b_ac2)
                close_phase()
                phase_C_inner(vals, wo_sb)
                close_phase()

        def phase_B_inner(vals, wo_sb, csr,
                          b_xl, b_gw, b_pe1, b_pe2, b_pe3,
                          b_ve1, b_ve2, b_ac1, b_ac2):
            bq = [cnt.get("q%d" % _i) for _i in range(22)]
            with contextlib.ExitStack() as _es:
                qh0 = _es.enter_context(nc.sbuf_tensor("qh0", [128, TFULL], BF16))
                qh1 = _es.enter_context(nc.sbuf_tensor("qh1", [128, TFULL], BF16))
                kh0 = _es.enter_context(nc.sbuf_tensor("kh0", [128, TFULL], BF16))
                kh1 = _es.enter_context(nc.sbuf_tensor("kh1", [128, TFULL], BF16))
                vh0 = _es.enter_context(nc.sbuf_tensor("vh0", [128, TOK], BF16))
                vh1 = _es.enter_context(nc.sbuf_tensor("vh1", [128, TOK], BF16))
                un0 = _es.enter_context(nc.sbuf_tensor("un0", [128, TOK], BF16))
                un1 = _es.enter_context(nc.sbuf_tensor("un1", [128, TOK], BF16))
                utmp = _es.enter_context(nc.sbuf_tensor("utmp", [128, TOK], BF16))
                lo = _es.enter_context(nc.sbuf_tensor("lo", [128, 8], F32))
                lw = _es.enter_context(nc.sbuf_tensor("lw", [128, 8], F32))
                lwb = _es.enter_context(nc.sbuf_tensor("lwb", [128, 2], BF16))
                psA0 = _es.enter_context(nc.psum_tensor("psA0", [128, 1024], F32))
                psA1 = _es.enter_context(nc.psum_tensor("psA1", [128, 1024], F32))
                csps = _es.enter_context(nc.psum_tensor("csps", [1, 2048], F32))
                blk = _es.enter_context(nc.Block("phB"))
                qh = [qh0, qh1]
                kh = [kh0, kh1]
                vh = [vh0, vh1]
                un = [un0, un1]
                psA = [psA0, psA1]

                @blk.sync
                def _(s):
                    phase_wait(s)
                    for h in range(NH):
                        if h >= 2:
                            s.wait_ge(S["pe2"], b_pe2 + 128 * (h - 1))
                            s.wait_ge(S["ve2"], b_ve2 + 2 * (h - 1))
                        s.dma_start(qh[h % 2][:], q_sc.ap()[h * 128:(h + 1) * 128, :]
                                    ).then_inc(S["q%d" % (h % 2)], 16)
                        s.dma_start(kh[h % 2][:], k_sc.ap()[h * 128:(h + 1) * 128, :]
                                    ).then_inc(S["q%d" % (h % 2)], 16)
                        s.dma_start(vh[h % 2][:], v_sc.ap()[h * 128:(h + 1) * 128, :]
                                    ).then_inc(S["q%d" % (h % 2)], 16)
                    phase_end(s)
                cnt.set("q0", bq[0] + 8 * 48)
                cnt.set("q1", bq[1] + 8 * 48)

                @blk.gpsimd
                def _(g):
                    phase_wait(g)
                    for hc in range(16):
                        g.dma_start(wo_sb[:, hc * 2048:(hc + 1) * 2048],
                                    wo_in.ap()[hc * 128:(hc + 1) * 128, :]).then_inc(S["gw"], 16)
                    phase_end(g)
                cnt.set("gw", b_gw + 256)

                @blk.tensor
                def _(t):
                    phase_wait(t)
                    for h in range(NH):
                        for qt in range(32):
                            j = 32 * h + qt
                            if qt == 0:
                                t.wait_ge(S["q%d" % (h % 2)], bq[h % 2] + 48 * (h // 2 + 1))
                                if h >= 1:
                                    t.wait_ge(S["ve2"], b_ve2 + 2 * h)
                                    t.wait_ge(S["ac2"], b_ac2 + h)
                            if j >= 1:
                                t.wait_ge(S["ac1"], b_ac1 + 4 * (j - 1) + 3)
                            for n in range(2):
                                t.matmul(psA0[:, n * 512:(n + 1) * 512],
                                         qh[h % 2][:, qt * 128:(qt + 1) * 128],
                                         kh[h % 2][:, n * 512:(n + 1) * 512],
                                         start=True, stop=True
                                         ).then_inc(S["pe2"], 1) if n == 1 else \
                                    t.matmul(psA0[:, n * 512:(n + 1) * 512],
                                             qh[h % 2][:, qt * 128:(qt + 1) * 128],
                                             kh[h % 2][:, n * 512:(n + 1) * 512],
                                             start=True, stop=True)
                            if j >= 1:
                                t.wait_ge(S["ac1"], b_ac1 + 4 * (j - 1) + 4)
                            for n in range(2):
                                t.matmul(psA1[:, n * 512:(n + 1) * 512],
                                         qh[h % 2][:, qt * 128:(qt + 1) * 128],
                                         kh[h % 2][:, 1024 + n * 512: 1024 + (n + 1) * 512],
                                         start=True, stop=True
                                         ).then_inc(S["pe2"], 1) if n == 1 else \
                                    t.matmul(psA1[:, n * 512:(n + 1) * 512],
                                             qh[h % 2][:, qt * 128:(qt + 1) * 128],
                                             kh[h % 2][:, 1024 + n * 512: 1024 + (n + 1) * 512],
                                             start=True, stop=True)
                            if qt >= 1:
                                jj = j - 1
                                t.wait_ge(S["ve1"], b_ve1 + jj + 1)
                                for kc in range(4):
                                    t.matmul(csps[:, kc * 512:(kc + 1) * 512],
                                             lwb[:, jj % 2: jj % 2 + 1],
                                             un[jj % 2][:, kc * 512:(kc + 1) * 512],
                                             start=(jj % 32 == 0), stop=(jj % 32 == 31)
                                             ).then_inc(S["pe3"], 1) if kc == 3 else \
                                        t.matmul(csps[:, kc * 512:(kc + 1) * 512],
                                                 lwb[:, jj % 2: jj % 2 + 1],
                                                 un[jj % 2][:, kc * 512:(kc + 1) * 512],
                                                 start=(jj % 32 == 0), stop=(jj % 32 == 31))
                            t.wait_ge(S["ac1"], b_ac1 + 4 * j + 1)
                            for n in range(2):
                                t.matmul(psA0[:, n * 512:(n + 1) * 512],
                                         qh[h % 2][:, qt * 128:(qt + 1) * 128],
                                         kh[h % 2][:, 2048 + n * 512: 2048 + (n + 1) * 512],
                                         start=True, stop=True
                                         ).then_inc(S["pe2"], 1) if n == 1 else \
                                    t.matmul(psA0[:, n * 512:(n + 1) * 512],
                                             qh[h % 2][:, qt * 128:(qt + 1) * 128],
                                             kh[h % 2][:, 2048 + n * 512: 2048 + (n + 1) * 512],
                                             start=True, stop=True)
                            t.wait_ge(S["ac1"], b_ac1 + 4 * j + 2)
                            for n in range(2):
                                t.matmul(psA1[:, n * 512:(n + 1) * 512],
                                         qh[h % 2][:, qt * 128:(qt + 1) * 128],
                                         kh[h % 2][:, 3072 + n * 512: 3072 + (n + 1) * 512],
                                         start=True, stop=True
                                         ).then_inc(S["pe2"], 1) if n == 1 else \
                                    t.matmul(psA1[:, n * 512:(n + 1) * 512],
                                             qh[h % 2][:, qt * 128:(qt + 1) * 128],
                                             kh[h % 2][:, 3072 + n * 512: 3072 + (n + 1) * 512],
                                             start=True, stop=True)
                        jj = 32 * h + 31
                        t.wait_ge(S["ve1"], b_ve1 + jj + 1)
                        for kc in range(4):
                            t.matmul(csps[:, kc * 512:(kc + 1) * 512],
                                     lwb[:, jj % 2: jj % 2 + 1],
                                     un[jj % 2][:, kc * 512:(kc + 1) * 512],
                                     start=(jj % 32 == 0), stop=(jj % 32 == 31)
                                     ).then_inc(S["pe3"], 1) if kc == 3 else \
                                t.matmul(csps[:, kc * 512:(kc + 1) * 512],
                                         lwb[:, jj % 2: jj % 2 + 1],
                                         un[jj % 2][:, kc * 512:(kc + 1) * 512],
                                         start=(jj % 32 == 0), stop=(jj % 32 == 31))
                        t.wait_ge(S["ac2"], b_ac2 + h + 1)
                        for kk in range(4):
                            dstp = psA[kk // 2][:, (kk % 2) * 512:(kk % 2 + 1) * 512]
                            t.matmul(dstp, onesf[:], csr[:, kk * 512:(kk + 1) * 512],
                                     start=True, stop=True).then_inc(S["pe1"], 1)
                    phase_end(t)
                cnt.set("pe1", b_pe1 + NH * 4)
                cnt.set("pe2", b_pe2 + 512 * 4)
                cnt.set("pe3", b_pe3 + 512)

                @blk.scalar
                def _(sc):
                    phase_wait(sc)
                    sc_s = cst[:, CI_S128:CI_S128 + 1]
                    for h in range(NH):
                        for qt in range(32):
                            j = 32 * h + qt
                            par = j % 2
                            sc.wait_ge(S["pe2"], b_pe2 + 4 * j + 1)
                            if j >= 2:
                                sc.wait_ge(S["pe3"], b_pe3 + j - 1)
                            sc.activation(un[par][:, 0:1024], psA0[:], AF.Exp, scale=sc_s,
                                          accum_out=lo[:, 4 * par + 0: 4 * par + 1]
                                          ).then_inc(S["ac1"], 1)
                            sc.wait_ge(S["pe2"], b_pe2 + 4 * j + 2)
                            sc.activation(un[par][:, 1024:2048], psA1[:], AF.Exp, scale=sc_s,
                                          accum_out=lo[:, 4 * par + 1: 4 * par + 2]
                                          ).then_inc(S["ac1"], 1)
                            sc.wait_ge(S["pe2"], b_pe2 + 4 * j + 3)
                            sc.activation(utmp[:, 0:1024], psA0[:], AF.Exp, scale=sc_s,
                                          accum_out=lo[:, 4 * par + 2: 4 * par + 3]
                                          ).then_inc(S["ac1"], 1)
                            sc.wait_ge(S["pe2"], b_pe2 + 4 * j + 4)
                            sc.activation(utmp[:, 1024:2048], psA1[:], AF.Exp, scale=sc_s,
                                          accum_out=lo[:, 4 * par + 3: 4 * par + 4]
                                          ).then_inc(S["ac1"], 1)
                        sc.wait_ge(S["pe3"], b_pe3 + 32 * (h + 1))
                        if h >= 1:
                            sc.wait_ge(S["pe1"], b_pe1 + 4 * (h - 1) + 4)
                        sc.copy(csr[:], csps[:])
                        sc.copy(flt[:], flt[:]).then_inc(S["ac2"], 1)
                    phase_end(sc)
                cnt.set("ac1", b_ac1 + 512 * 4)
                cnt.set("ac2", b_ac2 + NH)

                @blk.vector
                def _(v):
                    phase_wait(v)
                    vs = cnt.get("ve3")
                    for h in range(NH):
                        for qt in range(32):
                            j = 32 * h + qt
                            par = j % 2
                            v.wait_ge(S["ac1"], b_ac1 + 4 * j + 4)
                            a = lw[:, 4 * par + 0:4 * par + 1]
                            bb2 = lw[:, 4 * par + 1:4 * par + 2]
                            l = lw[:, 4 * par + 2:4 * par + 3]
                            il = lw[:, 4 * par + 3:4 * par + 4]
                            v.tensor_tensor(a, lo[:, 4 * par:4 * par + 1],
                                            lo[:, 4 * par + 1:4 * par + 2], ALU.add)
                            v.tensor_tensor(bb2, lo[:, 4 * par + 2:4 * par + 3],
                                            lo[:, 4 * par + 3:4 * par + 4], ALU.add
                                            ).then_inc(S["ve3"], 1)
                            vs += 1
                            v.wait_ge(S["ve3"], vs)
                            v.tensor_tensor(l, a, bb2, ALU.add).then_inc(S["ve3"], 1)
                            vs += 1
                            v.wait_ge(S["ve3"], vs)
                            v.reciprocal(il, l).then_inc(S["ve3"], 1)
                            vs += 1
                            v.wait_ge(S["ve3"], vs)
                            v.tensor_copy(lwb[:, par:par + 1], il)
                            v.memset(flt[:], 0.0).then_inc(S["ve1"], 1)
                        v.wait_ge(S["q%d" % (h % 2)], bq[h % 2] + 48 * (h // 2 + 1))
                        v.wait_ge(S["pe1"], b_pe1 + 4 * h + 2)
                        v.tensor_tensor(vals[:, h * 2048:h * 2048 + 1024],
                                        vh[h % 2][:, 0:1024], psA0[:], ALU.mult)
                        v.memset(flt[:], 0.0).then_inc(S["ve2"], 1)
                        v.wait_ge(S["pe1"], b_pe1 + 4 * h + 4)
                        v.tensor_tensor(vals[:, h * 2048 + 1024:(h + 1) * 2048],
                                        vh[h % 2][:, 1024:2048], psA1[:], ALU.mult)
                        v.memset(flt[:], 0.0).then_inc(S["ve2"], 1)
                    phase_end(v)
                cnt.set("ve1", b_ve1 + 512)
                cnt.set("ve2", b_ve2 + 2 * NH)
                cnt.set("ve3", cnt.get("ve3") + 512 * 3)

        def phase_C_inner(vals, wo_sb):
            (c_xl, c_st, c_gw, c_pe2, c_ve5) = bases("xl", "st", "gw", "pe2", "ve5")
            bq = [cnt.get("q%d" % _i) for _i in range(22)]
            with contextlib.ExitStack() as _es:
                xtc0 = _es.enter_context(nc.sbuf_tensor("xtc0", [128, D], F32))
                xtc1 = _es.enter_context(nc.sbuf_tensor("xtc1", [128, D], F32))
                x1sb0 = _es.enter_context(nc.sbuf_tensor("x1sb0", [128, D], F32))
                x1sb1 = _es.enter_context(nc.sbuf_tensor("x1sb1", [128, D], F32))
                po0 = _es.enter_context(nc.psum_tensor("po0", [128, 512], F32))
                po1 = _es.enter_context(nc.psum_tensor("po1", [128, 512], F32))
                po2 = _es.enter_context(nc.psum_tensor("po2", [128, 512], F32))
                po3 = _es.enter_context(nc.psum_tensor("po3", [128, 512], F32))
                blk = _es.enter_context(nc.Block("phC"))
                xtc = [xtc0, xtc1]
                x1sb = [x1sb0, x1sb1]
                po = [po0, po1, po2, po3]

                @blk.sync
                def _(s):
                    phase_wait(s)
                    for tt in range(16):
                        if tt >= 2:
                            s.wait_ge(S["ve5"], c_ve5 + 4 * (tt - 1))
                        s.dma_start(xtc[tt % 2][:], x_in.ap()[tt * 128:(tt + 1) * 128, :]
                                    ).then_inc(S["q%d" % (tt % 2)], 16)
                        if tt >= 1:
                            s.wait_ge(S["ve5"], c_ve5 + 4 * tt)
                            s.dma_start(x1_sc.ap()[(tt - 1) * 128:tt * 128, :],
                                        x1sb[(tt - 1) % 2][:]).then_inc(S["q%d" % (2 + (tt - 1) % 2)], 16)
                    s.wait_ge(S["ve5"], c_ve5 + 64)
                    s.dma_start(x1_sc.ap()[15 * 128:, :], x1sb[15 % 2][:]).then_inc(S["q3"], 16)
                    s.wait_ge(S["q2"], bq[2] + 8 * 16)
                    s.wait_ge(S["q3"], bq[3] + 8 * 16)
                    phase_end(s)
                cnt.set("q0", bq[0] + 8 * 16)
                cnt.set("q1", bq[1] + 8 * 16)
                cnt.set("q2", bq[2] + 8 * 16)
                cnt.set("q3", bq[3] + 8 * 16)

                @blk.tensor
                def _(t):
                    phase_wait(t)
                    t.wait_ge(S["gw"], c_gw)
                    for tt in range(16):
                        for od in range(4):
                            if tt >= 1:
                                t.wait_ge(S["ve5"], c_ve5 + 4 * (tt - 1) + od + 1)
                            for hc in range(16):
                                mm = t.matmul(po[od][:],
                                              vals[:, hc * 2048 + tt * 128: hc * 2048 + (tt + 1) * 128],
                                              wo_sb[:, hc * 2048 + od * 512: hc * 2048 + (od + 1) * 512],
                                              start=(hc == 0), stop=(hc == 15))
                                if hc == 15:
                                    mm.then_inc(S["pe2"], 1)
                    phase_end(t)
                cnt.set("pe2", c_pe2 + 64)

                @blk.vector
                def _(v):
                    phase_wait(v)
                    for tt in range(16):
                        v.wait_ge(S["q%d" % (tt % 2)], bq[tt % 2] + 16 * (tt // 2 + 1))
                        if tt >= 2:
                            v.wait_ge(S["q%d" % (2 + tt % 2)], bq[2 + tt % 2] + 16 * (tt // 2))
                        for od in range(4):
                            v.wait_ge(S["pe2"], c_pe2 + 4 * tt + od + 1)
                            v.tensor_add(x1sb[tt % 2][:, od * 512:(od + 1) * 512],
                                         po[od][:],
                                         xtc[tt % 2][:, od * 512:(od + 1) * 512])
                            v.memset(flt[:], 0.0).then_inc(S["ve5"], 1)
                    phase_end(v)
                cnt.set("ve5", c_ve5 + 64)

                @blk.scalar
                def _(sc):
                    phase_wait(sc)
                    phase_end(sc)

                @blk.gpsimd
                def _(g):
                    phase_wait(g)
                    phase_end(g)

        def phase_D():
            (d_xl, d_st, d_wl, d_pe1, d_pe2, d_pe3, d_pe4,
             d_ve1, d_ve2, d_ve3, d_ve4, d_ve5, d_ac1, d_ac2, d_ac3) = bases(
                "xl", "st", "wl", "pe1", "pe2", "pe3", "pe4",
                "ve1", "ve2", "ve3", "ve4", "ve5", "ac1", "ac2", "ac3")
            bq = [cnt.get("q%d" % _i) for _i in range(22)]
            with contextlib.ExitStack() as _es:
                x1t0 = _es.enter_context(nc.sbuf_tensor("x1t0", [128, D], F32))
                x1t1 = _es.enter_context(nc.sbuf_tensor("x1t1", [128, D], F32))
                sqt2 = _es.enter_context(nc.sbuf_tensor("sqt2", [128, D], F32))
                xn20 = _es.enter_context(nc.sbuf_tensor("xn20", [128, D], F32))
                xn21 = _es.enter_context(nc.sbuf_tensor("xn21", [128, D], F32))
                yfm = _es.enter_context(nc.sbuf_tensor("yfm", [128, 16 * 512], BF16))
                tfm = _es.enter_context(nc.sbuf_tensor("tfm", [128, 16 * 512], BF16))
                g2 = _es.enter_context(nc.sbuf_tensor("g2", [128, 64 * 512], BF16))
                hb0 = _es.enter_context(nc.sbuf_tensor("hb0", [128, 2048], BF16))
                hb1 = _es.enter_context(nc.sbuf_tensor("hb1", [128, 2048], BF16))
                wub0 = _es.enter_context(nc.sbuf_tensor("wub0", [128, 2048], BF16))
                wub1 = _es.enter_context(nc.sbuf_tensor("wub1", [128, 2048], BF16))
                wub2 = _es.enter_context(nc.sbuf_tensor("wub2", [128, 2048], BF16))
                wub3 = _es.enter_context(nc.sbuf_tensor("wub3", [128, 2048], BF16))
                wgb0 = _es.enter_context(nc.sbuf_tensor("wgb0", [128, 2048], BF16))
                wgb1 = _es.enter_context(nc.sbuf_tensor("wgb1", [128, 2048], BF16))
                wgb2 = _es.enter_context(nc.sbuf_tensor("wgb2", [128, 2048], BF16))
                wgb3 = _es.enter_context(nc.sbuf_tensor("wgb3", [128, 2048], BF16))
                w2b0 = _es.enter_context(nc.sbuf_tensor("w2b0", [128, 512], BF16))
                w2b1 = _es.enter_context(nc.sbuf_tensor("w2b1", [128, 512], BF16))
                w2b2 = _es.enter_context(nc.sbuf_tensor("w2b2", [128, 512], BF16))
                w2b3 = _es.enter_context(nc.sbuf_tensor("w2b3", [128, 512], BF16))
                xup0 = _es.enter_context(nc.sbuf_tensor("xup0", [128, 512], F32))
                xup1 = _es.enter_context(nc.sbuf_tensor("xup1", [128, 512], F32))
                sg0 = _es.enter_context(nc.sbuf_tensor("sg0", [128, 512], F32))
                sg1 = _es.enter_context(nc.sbuf_tensor("sg1", [128, 512], F32))
                gc0 = _es.enter_context(nc.sbuf_tensor("gc0", [128, 512], BF16))
                gc1 = _es.enter_context(nc.sbuf_tensor("gc1", [128, 512], BF16))
                xob0 = _es.enter_context(nc.sbuf_tensor("xob0", [128, 512], F32))
                xob1 = _es.enter_context(nc.sbuf_tensor("xob1", [128, 512], F32))
                osb0 = _es.enter_context(nc.sbuf_tensor("osb0", [128, 512], F32))
                osb1 = _es.enter_context(nc.sbuf_tensor("osb1", [128, 512], F32))
                st1c = _es.enter_context(nc.sbuf_tensor("st1c", [128, 1], F32))
                st1d = _es.enter_context(nc.sbuf_tensor("st1d", [128, 1], F32))
                st2c = _es.enter_context(nc.sbuf_tensor("st2c", [128, 1], F32))
                st2d = _es.enter_context(nc.sbuf_tensor("st2d", [128, 1], F32))
                mvc = _es.enter_context(nc.sbuf_tensor("mvc", [128, 4], F32))
                mvd = _es.enter_context(nc.sbuf_tensor("mvd", [128, 4], F32))
                pf0 = _es.enter_context(nc.psum_tensor("pf0", [128, 512], F32))
                pf1 = _es.enter_context(nc.psum_tensor("pf1", [128, 512], F32))
                ph = _es.enter_context(nc.psum_tensor("ph", [128, 512], F32))
                pd0 = _es.enter_context(nc.psum_tensor("pd0", [128, 512], F32))
                pd1 = _es.enter_context(nc.psum_tensor("pd1", [128, 512], F32))
                pd2 = _es.enter_context(nc.psum_tensor("pd2", [128, 512], F32))
                pd3 = _es.enter_context(nc.psum_tensor("pd3", [128, 512], F32))
                blk = _es.enter_context(nc.Block("phD"))
                x1t = [x1t0, x1t1]
                xn2 = [xn20, xn21]
                hb = [hb0, hb1]
                HBQ = [4, 5]
                wub = [wub0, wub1, wub2, wub3]
                wgb = [wgb0, wgb1, wgb2, wgb3]
                w2b = [w2b0, w2b1, w2b2, w2b3]
                xup = [xup0, xup1]
                sg = [sg0, sg1]
                gc = [gc0, gc1]
                xob = [xob0, xob1]
                XOQ = [2, 3]
                osb = [osb0, osb1]
                OSQ = [10, 11]
                stat1 = [st1c, st1d]
                stat2 = [st2c, st2d]
                mv = [mvc, mvd]
                pf = [pf0, pf1]
                pd = [pd0, pd1, pd2, pd3]

                WL_CH = 16 * 16 + 64 * 32 + 256 * 16

                def wl_h(ch, fo):
                    return d_wl + ch * WL_CH + 16 * (fo + 1)

                def wl_ug(ch, c):
                    return d_wl + ch * WL_CH + 256 + 32 * (c + 1)

                def wl_w2(ch, od, c):
                    return d_wl + ch * WL_CH + 256 + 2048 + 16 * (od * 64 + c + 1)

                XL_CH = 20 * 16

                def xl_ln(ch, tt):
                    return d_xl + ch * XL_CH + 16 * (tt + 1)

                def xl_part(ch, k):
                    return d_xl + ch * XL_CH + 64 + 16 * (k + 1)

                PE3_UP, PE3_GT, PE3_H = {}, {}, {}
                _n = 0
                for _ch in range(4):
                    for _c in range(64):
                        _cc = _ch * 64 + _c
                        _n += 1
                        PE3_UP[_cc] = _n
                        _n += 1
                        PE3_GT[_cc] = _n
                        if _c >= 1:
                            _n += 1
                            PE3_H[_cc - 1] = _n
                    _n += 1
                    PE3_H[_ch * 64 + 63] = _n

                @blk.sync
                def _(s):
                    phase_wait(s)
                    s.wait_ge(S["gst"], cnt.get("gst"))
                    for ch in range(4):
                        for tt in range(4):
                            u = ch * 4 + tt
                            if u >= 2:
                                s.wait_ge(S["ve1"], d_ve1 + 8 * (u - 1))
                            s.dma_start(x1t[u % 2][:],
                                        x1_sc.ap()[u * 128:(u + 1) * 128, :]).then_inc(S["q%d" % (u % 2)], 16)
                        for fo in range(16):
                            ff = ch * 16 + fo
                            if ff >= 2:
                                s.wait_ge(S["pe2"], d_pe2 + ff - 1)
                            s.dma_start(hb[ff % 2][:],
                                        h2048_in.ap()[fo, :, :, :].rearrange("p a b -> p (a b)")
                                        ).then_inc(S["q%d" % HBQ[ff % 2]], 16)
                        for c in range(64):
                            cc = ch * 64 + c
                            if cc >= 4:
                                s.wait_ge(S["pe3"], d_pe3 + PE3_GT[cc - 4])
                            s.dma_start(wub[cc % 4][:], wup_sc.ap()[c, :, :]).then_inc(S["q%d" % (6 + cc % 4)], 16)
                            s.dma_start(wgb[cc % 4][:], wgate_sc.ap()[c, :, :]).then_inc(S["q%d" % (6 + cc % 4)], 16)
                        for od in range(4):
                            for c in range(64):
                                dcc = (ch * 4 + od) * 64 + c
                                if dcc >= 4:
                                    s.wait_ge(S["pe4"], d_pe4 + dcc - 3)
                                s.dma_start(w2b[dcc % 4][:],
                                            w2t_sc.ap()[c * 128:(c + 1) * 128,
                                                        od * 512:(od + 1) * 512]
                                            ).then_inc(S["q%d" % (12 + dcc % 4)], 16)
                            for tt in range(4):
                                k = od * 4 + tt
                                idx = ch * 16 + k
                                if idx >= 2:
                                    s.wait_ge(S["ve5"], d_ve5 + idx - 1)
                                s.dma_start(xob[idx % 2][:],
                                            x1_sc.ap()[(ch * 4 + tt) * 128:(ch * 4 + tt + 1) * 128,
                                                       od * 512:(od + 1) * 512]
                                            ).then_inc(S["q%d" % XOQ[idx % 2]], 16)
                    phase_end(s)
                for _i, _v in ((0, 8), (1, 8), (2, 32), (3, 32), (4, 32), (5, 32),
                               (6, 64 * 2), (7, 64 * 2), (8, 64 * 2), (9, 64 * 2),
                               (12, 256), (13, 256), (14, 256), (15, 256)):
                    cnt.set("q%d" % _i, bq[_i] + _v * 16)

                def ln_chain2(v, u):
                    par = u % 2
                    v.wait_ge(S["q%d" % (u % 2)], bq[u % 2] + 16 * (u // 2 + 1))
                    v.reduce_sum(stat1[par][:], x1t[par][:], axis=AX.X).then_inc(S["ve1"], 1)
                    v.wait_ge(S["ac1"], d_ac1 + 2 * u + 1)
                    v.wait_ge(S["ve1"], d_ve1 + 8 * u + 1)
                    m = mv[par][:, 0:1]
                    var = mv[par][:, 1:2]
                    rstd = mv[par][:, 2:3]
                    negmr = mv[par][:, 3:4]
                    v.tensor_scalar(m, stat1[par][:], 1.0 / D, None, ALU.mult).then_inc(S["ve1"], 1)
                    v.tensor_scalar(var, stat2[par][:], 1.0 / D, None, ALU.mult).then_inc(S["ve1"], 1)
                    v.wait_ge(S["ve1"], d_ve1 + 8 * u + 3)
                    v.tensor_tensor(rstd, m, m, ALU.mult).then_inc(S["ve1"], 1)
                    v.wait_ge(S["ve1"], d_ve1 + 8 * u + 4)
                    v.tensor_tensor(var, var, rstd, ALU.subtract).then_inc(S["ve1"], 1)
                    v.wait_ge(S["ac1"], d_ac1 + 2 * u + 2)
                    v.reciprocal(rstd, rstd).then_inc(S["ve1"], 1)
                    v.wait_ge(S["ve1"], d_ve1 + 8 * u + 6)
                    v.scalar_tensor_tensor(negmr, m, -1.0, rstd, ALU.mult, ALU.mult
                                           ).then_inc(S["ve1"], 1)
                    v.wait_ge(S["ve1"], d_ve1 + 8 * u + 7)
                    if u >= 2:
                        v.wait_ge(S["pe1"], d_pe1 + 16 * (u - 1))
                    v.tensor_scalar(xn2[par][:], x1t[par][:], m, rstd,
                                    ALU.subtract, ALU.mult).then_inc(S["ve1"], 1)

                @blk.vector
                def _(v):
                    phase_wait(v)
                    for ch in range(4):
                        ln_chain2(v, ch * 4)
                        for tt in range(4):
                            u = ch * 4 + tt
                            if tt + 1 < 4:
                                ln_chain2(v, u + 1)
                            for k in range(16):
                                g_idx = 16 * u + k
                                v.wait_ge(S["pe1"], d_pe1 + g_idx + 1)
                                v.tensor_scalar(
                                    yfm[:, k * 512 + tt * 128: k * 512 + (tt + 1) * 128],
                                    pf[k % 2][:, 0:128], su_sb[:, k:k + 1], None, ALU.mult)
                                v.memset(flt[:], 0.0).then_inc(S["ve3"], 1)
                        for fo in range(16):
                            ff = ch * 16 + fo
                            v.wait_ge(S["pe2"], d_pe2 + ff + 1)
                            v.tensor_copy(tfm[:, fo * 512:(fo + 1) * 512], pf[ff % 2][:])
                            v.memset(flt[:], 0.0).then_inc(S["ve4"], 1)
                        for c in range(64):
                            cc = ch * 64 + c
                            v.wait_ge(S["ac3"], d_ac3 + 2 * cc + 2)
                            v.tensor_tensor(gc[cc % 2][:], xup[cc % 2][:], sg[cc % 2][:],
                                            ALU.mult)
                            v.memset(flt[:], 0.0).then_inc(S["ve2"], 1)
                            v.wait_ge(S["pe3"], d_pe3 + PE3_H[cc])
                            v.tensor_copy(g2[:, c * 512:(c + 1) * 512], ph[:])
                            v.memset(flt[:], 0.0).then_inc(S["ve2"], 1)
                        for od in range(4):
                            for tt in range(4):
                                k = od * 4 + tt
                                idx = ch * 16 + k
                                v.wait_ge(S["pe4"], d_pe4 + (ch * 4 + od) * 64 + 64)
                                v.wait_ge(S["q%d" % XOQ[idx % 2]], bq[XOQ[idx % 2]] + 16 * (idx // 2 + 1))
                                if idx >= 2:
                                    v.wait_ge(S["q%d" % OSQ[idx % 2]], bq[OSQ[idx % 2]] + 16 * (idx // 2))
                                v.tensor_add(osb[idx % 2][:], pd[tt][:], xob[idx % 2][:])
                                v.memset(flt[:], 0.0).then_inc(S["ve5"], 1)
                    phase_end(v)
                cnt.set("ve1", d_ve1 + 8 * 16)
                cnt.set("ve2", d_ve2 + 2 * 256)
                cnt.set("ve3", d_ve3 + 256)
                cnt.set("ve4", d_ve4 + 64)
                cnt.set("ve5", d_ve5 + 64)

                @blk.scalar
                def _(sc):
                    phase_wait(sc)
                    for ch in range(4):
                        for tt in range(4):
                            u = ch * 4 + tt
                            par = u % 2
                            sc.wait_ge(S["q%d" % ((ch * 4 + tt) % 2)], bq[(ch * 4 + tt) % 2] + 16 * ((ch * 4 + tt) // 2 + 1))
                            sc.activation(sqt2[:], x1t[par][:], AF.Square,
                                          accum_out=stat2[par][:]).then_inc(S["ac1"], 1)
                            sc.wait_ge(S["ve1"], d_ve1 + 8 * u + 5)
                            sc.activation(mv[par][:, 2:3], mv[par][:, 1:2], AF.Sqrt,
                                          bias=cst[:, CI_EPS:CI_EPS + 1]).then_inc(S["ac1"], 1)
                        for c in range(64):
                            cc = ch * 64 + c
                            sc.wait_ge(S["pe3"], d_pe3 + PE3_UP[cc])
                            if cc >= 2:
                                sc.wait_ge(S["ve2"], d_ve2 + 2 * cc - 3)
                            sc.activation(xup[cc % 2][:], pf0[:], AF.Copy,
                                          scale=scup[:, c:c + 1]).then_inc(S["ac3"], 1)
                            sc.wait_ge(S["pe3"], d_pe3 + PE3_GT[cc])
                            sc.activation(sg[cc % 2][:], pf1[:], AF.Silu,
                                          scale=scgt[:, c:c + 1]).then_inc(S["ac3"], 1)
                    phase_end(sc)
                cnt.set("ac1", d_ac1 + 32)
                cnt.set("ac3", d_ac3 + 2 * 256)

                @blk.tensor
                def _(t):
                    phase_wait(t)
                    for ch in range(4):
                        if ch >= 1:
                            t.wait_ge(S["ac3"], d_ac3 + 128 * ch)
                        for tt in range(4):
                            u = ch * 4 + tt
                            t.wait_ge(S["ve1"], d_ve1 + 8 * u + 8)
                            for k in range(16):
                                g_idx = 16 * u + k
                                if g_idx >= 2:
                                    t.wait_ge(S["ve3"], d_ve3 + g_idx - 1)
                                t.transpose(pf[k % 2][:, 0:128], xn2[u % 2][:, k * 128:(k + 1) * 128],
                                            idnf[:]).then_inc(S["pe1"], 1)
                        t.wait_ge(S["ve3"], d_ve3 + 64 * (ch + 1))
                        for fo in range(16):
                            ff = ch * 16 + fo
                            t.wait_ge(S["q%d" % HBQ[ff % 2]], bq[HBQ[ff % 2]] + 16 * (ff // 2 + 1))
                            if ff >= 2:
                                t.wait_ge(S["ve4"], d_ve4 + ff - 1)
                            for fi in range(16):
                                mm = t.matmul(pf[ff % 2][:],
                                              hb[ff % 2][:, fi * 128:(fi + 1) * 128],
                                              yfm[:, fi * 512:(fi + 1) * 512],
                                              start=(fi == 0), stop=(fi == 15))
                                if fi == 15:
                                    mm.then_inc(S["pe2"], 1)
                        t.wait_ge(S["ve4"], d_ve4 + 16 * (ch + 1))
                        for c in range(64):
                            cc = ch * 64 + c
                            t.wait_ge(S["q%d" % (6 + cc % 4)], bq[6 + cc % 4] + 32 * (cc // 4 + 1))
                            if cc >= 1:
                                t.wait_ge(S["ac3"], d_ac3 + 2 * (cc - 1) + 1)
                            for fc in range(16):
                                mm = t.matmul(pf0[:], wub[cc % 4][:, fc * 128:(fc + 1) * 128],
                                              tfm[:, fc * 512:(fc + 1) * 512],
                                              start=(fc == 0), stop=(fc == 15))
                                if fc == 15:
                                    mm.then_inc(S["pe3"], 1)
                            if cc >= 1:
                                t.wait_ge(S["ac3"], d_ac3 + 2 * (cc - 1) + 2)
                            for fc in range(16):
                                mm = t.matmul(pf1[:], wgb[cc % 4][:, fc * 128:(fc + 1) * 128],
                                              tfm[:, fc * 512:(fc + 1) * 512],
                                              start=(fc == 0), stop=(fc == 15))
                                if fc == 15:
                                    mm.then_inc(S["pe3"], 1)
                            if c >= 1:
                                t.wait_ge(S["ve2"], d_ve2 + 2 * (cc - 1) + 1)
                                if cc >= 2:
                                    t.wait_ge(S["ve2"], d_ve2 + 2 * (cc - 2) + 2)
                                t.matmul(ph[:], h128[:], gc[(cc - 1) % 2][:],
                                         start=True, stop=True).then_inc(S["pe3"], 1)
                        cc = ch * 64 + 63
                        t.wait_ge(S["ve2"], d_ve2 + 2 * cc + 1)
                        if cc >= 1:
                            t.wait_ge(S["ve2"], d_ve2 + 2 * (cc - 1) + 2)
                        t.matmul(ph[:], h128[:], gc[cc % 2][:],
                                 start=True, stop=True).then_inc(S["pe3"], 1)
                        for od in range(4):
                            for c in range(64):
                                dcc = (ch * 4 + od) * 64 + c
                                t.wait_ge(S["q%d" % (12 + dcc % 4)], bq[12 + dcc % 4] + 16 * (dcc // 4 + 1))
                                t.wait_ge(S["ve2"], d_ve2 + 2 * (ch * 64 + c) + 2)
                                if c == 0 and (ch * 4 + od) >= 1:
                                    t.wait_ge(S["ve5"], d_ve5 + (ch * 4 + od - 1) * 4 + 4)
                                for tt in range(4):
                                    mm = t.matmul(pd[tt][:],
                                                  g2[:, c * 512 + tt * 128: c * 512 + (tt + 1) * 128],
                                                  w2b[dcc % 4][:],
                                                  start=(c == 0), stop=(c == 63))
                                    if tt == 3:
                                        mm.then_inc(S["pe4"], 1)
                    phase_end(t)
                cnt.set("pe1", d_pe1 + 256)
                cnt.set("pe2", d_pe2 + 64)
                cnt.set("pe3", d_pe3 + 3 * 256)
                cnt.set("pe4", d_pe4 + 1024)

                @blk.gpsimd
                def _(g):
                    phase_wait(g)
                    for ch in range(4):
                        for od in range(4):
                            for tt in range(4):
                                k = od * 4 + tt
                                idx = ch * 16 + k
                                g.wait_ge(S["ve5"], d_ve5 + idx + 1)
                                g.dma_start(out.ap()[(ch * 4 + tt) * 128:(ch * 4 + tt + 1) * 128,
                                                     od * 512:(od + 1) * 512],
                                            osb[idx % 2][:]).then_inc(S["q%d" % OSQ[idx % 2]], 16)
                    for _k in range(2):
                        g.wait_ge(S["q%d" % OSQ[_k]], bq[OSQ[_k]] + 16 * 32)
                    phase_end(g)
                for _k in range(2):
                    cnt.set("q%d" % OSQ[_k], bq[OSQ[_k]] + 16 * 32)
            close_phase()

        phase_setup()
        phase_BC()
        phase_D()
    return nc



_H2048 = None
_H128 = None
_H64 = None


def _consts():
    global _H2048, _H128, _H64
    if _H2048 is None:
        _H2048 = _hadamard(2048)
        _H128 = _hadamard(128)
        _H64 = _hadamard(64)
    bf = ml_dtypes.bfloat16
    h2048 = np.ascontiguousarray(
        _H2048.reshape(16, 128, 16, 128).transpose(2, 1, 0, 3)).astype(bf)
    h128 = _H128.astype(bf)
    h64 = _H64.astype(bf)
    ident = np.eye(128, dtype=bf)
    identf = np.eye(128, dtype=np.float32)
    ones = np.ones((1, 128), np.float32)
    consts = np.zeros((1, NC_CONST), np.float32)
    consts[0, CI_S128] = S128
    consts[0, CI_EPS] = 1e-5
    return h2048, h128, h64, ident, identf, ones, consts


def kernel(L0_attn_x, L0_attn_w_qkv, L0_attn_w_o,
           L0_ffn_SU_ug, L0_ffn_SU_dp, L0_ffn_Wscale_ug, L0_ffn_Wscale_dp,
           L0_ffn_W_up, L0_ffn_W_gate, L0_ffn_W_down):
    x = np.asarray(L0_attn_x, np.float32)
    wqkv = np.asarray(L0_attn_w_qkv, np.float32)
    wo = np.ascontiguousarray(np.asarray(L0_attn_w_o, np.float32))
    su_ug = np.asarray(L0_ffn_SU_ug, np.float32).reshape(1, D)
    su_dp = np.asarray(L0_ffn_SU_dp, np.float32).reshape(1, INTER)
    wsc_ug = np.asarray(L0_ffn_Wscale_ug, np.float32).reshape(1, 2 * INTER)
    wsc_dp = np.asarray(L0_ffn_Wscale_dp, np.float32).reshape(1, D)
    wup = np.asarray(L0_ffn_W_up, np.float32)
    wgate = np.asarray(L0_ffn_W_gate, np.float32)
    wdown = np.asarray(L0_ffn_W_down, np.float32)

    wqkv_in = np.ascontiguousarray(
        wqkv.reshape(16, 128, 48, 128).transpose(2, 1, 0, 3))
    wup_in = np.ascontiguousarray(
        wup.reshape(64, 128, 16, 128).transpose(0, 3, 2, 1))
    wgate_in = np.ascontiguousarray(
        wgate.reshape(64, 128, 16, 128).transpose(0, 3, 2, 1))
    wdownT_in = np.ascontiguousarray(wdown.T)

    h2048, h128, h64, ident, identf, ones, consts = _consts()

    nc = build_nc()
    in_maps = []
    for c in range(8):
        b, p = divmod(c, 2)
        own = x[b, p * TOK:(p + 1) * TOK, :]
        other = x[b, (1 - p) * TOK:(2 - p) * TOK, :]
        x_core = np.ascontiguousarray(np.concatenate([own, other], axis=0))
        in_maps.append(dict(
            x_in=x_core, wqkv_in=wqkv_in, wo_in=wo,
            wup_in=wup_in, wgate_in=wgate_in, wdownT_in=wdownT_in,
            su_ug_in=su_ug, su_dp_in=su_dp, wsc_ug_in=wsc_ug, wsc_dp_in=wsc_dp,
            h2048_in=h2048, h128_in=h128, h64_in=h64,
            ident_in=ident, identf_in=identf, ones_in=ones, consts_in=consts,
        ))
    res = run_bass_kernel_spmd(nc, in_maps, core_ids=list(range(8)))
    outp = np.empty((4, TFULL, D), np.float32)
    for c in range(8):
        b, p = divmod(c, 2)
        outp[b, p * TOK:(p + 1) * TOK, :] = res.results[c]["out"]
    return outp



# version 10
# speedup vs baseline: 1.2704x; 1.0396x over previous
import math
import contextlib
import numpy as np
import ml_dtypes
import concourse.bass as bass
import concourse.mybir as mybir
from concourse.bass_utils import run_bass_kernel_spmd

F32 = mybir.dt.float32
BF16 = mybir.dt.bfloat16
AF = mybir.ActivationFunctionType
ALU = mybir.AluOpType
AX = mybir.AxisListType

D = 2048
TOK = 2048
TFULL = 4096
NH = 16
INTER = 8192
S128 = 1.0 / math.sqrt(128.0)
C1 = 1.0 / math.sqrt(2048.0)
CDP = 1.0 / math.sqrt(8192.0)

CI_S128, CI_EPS = 0, 1
NC_CONST = 4

SEMNAMES = ["bar", "ld", "st", "gw", "gst", "xl", "wl",
            "pe1", "pe2", "pe3", "pe4",
            "ve1", "ve2", "ve3", "ve4", "ve5",
            "ac1", "ac2", "ac3"] + ["q%d" % i for i in range(22)]


def _hadamard(n):
    h = np.array([[1.0]])
    while h.shape[0] < n:
        h = np.block([[h, h], [h, -h]])
    return h


class Ctr:
    def __init__(self):
        self.c = {}

    def inc(self, instr, sem, name, k=1):
        instr.then_inc(sem, k)
        self.c[name] = self.c.get(name, 0) + k
        return self.c[name]

    def get(self, name):
        return self.c.get(name, 0)

    def set(self, name, v):
        self.c[name] = v


def build_nc():
    nc = bass.Bass("TRN2", target_bir_lowering=False)

    x_in = nc.dram_tensor("x_in", [TFULL, D], F32, kind="ExternalInput")
    wqkv_in = nc.dram_tensor("wqkv_in", [48, 128, 16, 128], F32, kind="ExternalInput")
    wo_in = nc.dram_tensor("wo_in", [D, D], F32, kind="ExternalInput")
    wup_in = nc.dram_tensor("wup_in", [64, 128, 16, 128], F32, kind="ExternalInput")
    wgate_in = nc.dram_tensor("wgate_in", [64, 128, 16, 128], F32, kind="ExternalInput")
    wdownT_in = nc.dram_tensor("wdownT_in", [INTER, D], F32, kind="ExternalInput")
    su_ug_in = nc.dram_tensor("su_ug_in", [1, D], F32, kind="ExternalInput")
    su_dp_in = nc.dram_tensor("su_dp_in", [1, INTER], F32, kind="ExternalInput")
    wsc_ug_in = nc.dram_tensor("wsc_ug_in", [1, 2 * INTER], F32, kind="ExternalInput")
    wsc_dp_in = nc.dram_tensor("wsc_dp_in", [1, D], F32, kind="ExternalInput")
    h2048_in = nc.dram_tensor("h2048_in", [16, 128, 16, 128], BF16, kind="ExternalInput")
    h128_in = nc.dram_tensor("h128_in", [128, 128], BF16, kind="ExternalInput")
    h64_in = nc.dram_tensor("h64_in", [64, 64], BF16, kind="ExternalInput")
    ident_in = nc.dram_tensor("ident_in", [128, 128], BF16, kind="ExternalInput")
    identf_in = nc.dram_tensor("identf_in", [128, 128], F32, kind="ExternalInput")
    ones_in = nc.dram_tensor("ones_in", [1, 128], F32, kind="ExternalInput")
    consts_in = nc.dram_tensor("consts_in", [1, NC_CONST], F32, kind="ExternalInput")

    out = nc.dram_tensor("out", [TOK, D], F32, kind="ExternalOutput")

    q_sc = nc.dram_tensor("q_sc", [D, TFULL], BF16, kind="Internal")
    k_sc = nc.dram_tensor("k_sc", [D, TFULL], BF16, kind="Internal")
    v_sc = nc.dram_tensor("v_sc", [D, TOK], BF16, kind="Internal")
    x1_sc = nc.dram_tensor("x1_sc", [TOK, D], F32, kind="Internal")
    wup_sc = nc.dram_tensor("wup_sc", [64, 128, 2048], BF16, kind="Internal")
    wgate_sc = nc.dram_tensor("wgate_sc", [64, 128, 2048], BF16, kind="Internal")
    w2pre_sc = nc.dram_tensor("w2pre_sc", [INTER, D], BF16, kind="Internal")
    w2t_sc = nc.dram_tensor("w2t_sc", [INTER, D], BF16, kind="Internal")

    def bcast_row(dram_t, n, col0=0):
        a = dram_t.ap()
        return bass.AP(a.tensor, a.offset + col0, [[0, 128], [1, n]])

    def strided_col(dram_t, p, cols, col0=0):
        a = dram_t.ap()
        return bass.AP(a.tensor, a.offset + col0, [[1, p], [p, cols]])

    cnt = Ctr()

    with contextlib.ExitStack() as stack:
        sems = {sn: stack.enter_context(nc.semaphore(sn)) for sn in SEMNAMES}
        S = sems

        cst = stack.enter_context(nc.sbuf_tensor("cst", [128, NC_CONST], F32))
        onesf = stack.enter_context(nc.sbuf_tensor("onesf", [1, 128], F32))
        idn = stack.enter_context(nc.sbuf_tensor("idn", [128, 128], BF16))
        idnf = stack.enter_context(nc.sbuf_tensor("idnf", [128, 128], F32))
        flt = stack.enter_context(nc.sbuf_tensor("flt", [128, 1], F32))
        h128 = stack.enter_context(nc.sbuf_tensor("h128", [128, 128], BF16))
        h64 = stack.enter_context(nc.sbuf_tensor("h64", [64, 64], BF16))
        su_sb = stack.enter_context(nc.sbuf_tensor("su_sb", [128, 16], F32))
        scup = stack.enter_context(nc.sbuf_tensor("scup", [128, 64], F32))
        scgt = stack.enter_context(nc.sbuf_tensor("scgt", [128, 64], F32))

        phase_no = [0]

        def phase_wait(eng):
            if phase_no[0] > 0:
                eng.wait_ge(S["bar"], 5 * phase_no[0])

        def phase_end(eng):
            eng.sem_inc(S["bar"], 1)

        def close_phase():
            phase_no[0] += 1

        def bases(*names):
            return tuple(cnt.get(n) for n in names)

        def phase_setup():
            b_ld, b_st, b_pe1, b_pe2, b_ve1, b_ve2, b_ac1, b_ac2 = bases(
                "ld", "st", "pe1", "pe2", "ve1", "ve2", "ac1", "ac2")
            bq = [cnt.get("q%d" % _i) for _i in range(22)]
            with contextlib.ExitStack() as _es:
                t_ug = _es.enter_context(nc.sbuf_tensor("t_ug", [128, 64], F32))
                t_dp = _es.enter_context(nc.sbuf_tensor("t_dp", [128, 64], F32))
                bcdp = _es.enter_context(nc.sbuf_tensor("bcdp", [128, D], F32))
                sdp_row = _es.enter_context(nc.sbuf_tensor("sdp_row", [1, D], F32))
                wdf0 = _es.enter_context(nc.sbuf_tensor("wdf0", [128, D], F32))
                wdf1 = _es.enter_context(nc.sbuf_tensor("wdf1", [128, D], F32))
                wdb0 = _es.enter_context(nc.sbuf_tensor("wdb0", [128, D], BF16))
                wdb1 = _es.enter_context(nc.sbuf_tensor("wdb1", [128, D], BF16))
                wcb0 = _es.enter_context(nc.sbuf_tensor("wcb0", [64, 1024], BF16))
                wcb1 = _es.enter_context(nc.sbuf_tensor("wcb1", [64, 1024], BF16))
                wob0 = _es.enter_context(nc.sbuf_tensor("wob0", [64, 1024], BF16))
                wob1 = _es.enter_context(nc.sbuf_tensor("wob1", [64, 1024], BF16))
                ps_bc = _es.enter_context(nc.psum_tensor("ps_bc", [128, 512], F32))
                ps_h0 = _es.enter_context(nc.psum_tensor("ps_h0", [64, 1024], F32))
                _blkcm = nc.Block("setup")
                blk = _blkcm.__enter__()
                wdf = [wdf0, wdf1]
                wdb = [wdb0, wdb1]
                wcb = [wcb0, wcb1]
                wob = [wob0, wob1]
                psh = [ps_h0, ps_h0]

                @blk.sync
                def _(s):
                    phase_wait(s)
                    for dst, src in (
                        (cst[:], bcast_row(consts_in, NC_CONST)),
                        (onesf[:], ones_in.ap()),
                        (idn[:], ident_in.ap()),
                        (idnf[:], identf_in.ap()),
                        (h128[:], h128_in.ap()),
                        (h64[:], h64_in.ap()),
                    ):
                        s.dma_start(dst, src).then_inc(S["ld"], 16)
                    s.dma_start(su_sb[:], strided_col(su_ug_in, 128, 16),
                                allow_slow_non_contiguous=True).then_inc(S["ld"], 16)
                    s.dma_start(t_ug[:], strided_col(wsc_ug_in, 128, 64),
                                allow_slow_non_contiguous=True).then_inc(S["ld"], 16)
                    s.dma_start(t_dp[:], strided_col(su_dp_in, 128, 64),
                                allow_slow_non_contiguous=True).then_inc(S["ld"], 16)
                    s.dma_start(sdp_row[:], wsc_dp_in.ap()).then_inc(S["ld"], 16)
                    s.dma_start(scgt[:], strided_col(wsc_ug_in, 128, 64, col0=INTER),
                                allow_slow_non_contiguous=True).then_inc(S["ld"], 16)
                    for i in range(64):
                        if i >= 2:
                            s.wait_ge(S["ve2"], b_ve2 + (i - 1))
                        s.dma_start(wdf[i % 2][:], wdownT_in.ap()[i * 128:(i + 1) * 128, :]
                                    ).then_inc(S["q%d" % (i % 2)], 16)
                        if i >= 1:
                            s.wait_ge(S["ve2"], b_ve2 + i)
                            s.dma_start(w2pre_sc.ap()[(i - 1) * 128:i * 128, :],
                                        wdb[(i - 1) % 2][:]).then_inc(S["q%d" % (2 + (i - 1) % 2)], 16)
                    s.wait_ge(S["ve2"], b_ve2 + 64)
                    s.dma_start(w2pre_sc.ap()[63 * 128:, :], wdb[1][:]).then_inc(S["q3"], 16)
                    s.wait_ge(S["q2"], bq[2] + 32 * 16)
                    s.wait_ge(S["q3"], bq[3] + 32 * 16)
                    a = w2pre_sc.ap()
                    b = w2t_sc.ap()
                    for j in range(256):
                        r, odh = divmod(j, 2)
                        src = bass.AP(a.tensor, a.offset + r * 2048 + odh * 1024,
                                      [[128 * 2048, 64], [1, 1024]])
                        if j >= 2:
                            s.wait_ge(S["pe2"], b_pe2 + (j - 1))
                        s.dma_start(wcb[j % 2][:], src).then_inc(S["q%d" % (4 + j % 2)], 16)
                        if j >= 1:
                            ro, odho = divmod(j - 1, 2)
                            s.wait_ge(S["ac2"], b_ac2 + j)
                            dst = bass.AP(b.tensor, b.offset + ro * 2048 + odho * 1024,
                                          [[128 * 2048, 64], [1, 1024]])
                            s.dma_start(dst, wob[(j - 1) % 2][:]).then_inc(S["q%d" % (6 + (j - 1) % 2)], 16)
                    s.wait_ge(S["ac2"], b_ac2 + 256)
                    dst = bass.AP(b.tensor, b.offset + 127 * 2048 + 1024,
                                  [[128 * 2048, 64], [1, 1024]])
                    s.dma_start(dst, wob[1][:]).then_inc(S["q7"], 16)
                    s.wait_ge(S["q6"], bq[6] + 128 * 16)
                    s.wait_ge(S["q7"], bq[7] + 128 * 16)
                    phase_end(s)
                cnt.set("ld", b_ld + 16 * 11)
                for _i, _v in ((0, 32), (1, 32), (2, 32), (3, 32), (4, 128), (5, 128), (6, 128), (7, 128)):
                    cnt.set("q%d" % _i, bq[_i] + _v * 16)

                @blk.vector
                def _(v):
                    phase_wait(v)
                    v.wait_ge(S["ld"], b_ld + 176)
                    v.tensor_tensor(scup[:], t_ug[:], t_dp[:], ALU.mult).then_inc(S["ve1"], 1)
                    v.wait_ge(S["ve1"], b_ve1 + 1)
                    v.tensor_scalar(scup[:], scup[:], C1, None, ALU.mult)
                    v.tensor_scalar(scgt[:], scgt[:], C1, None, ALU.mult)
                    v.tensor_scalar(sdp_row[:], sdp_row[:], CDP, None, ALU.mult
                                    ).then_inc(S["ve1"], 1)
                    for i in range(64):
                        v.wait_ge(S["q%d" % (i % 2)], bq[i % 2] + 16 * (i // 2 + 1))
                        if i == 0:
                            v.wait_ge(S["ac1"], b_ac1 + 4)
                        if i >= 2:
                            v.wait_ge(S["q%d" % (2 + i % 2)], bq[2 + i % 2] + 16 * (i // 2))
                        v.tensor_tensor(wdb[i % 2][:], wdf[i % 2][:], bcdp[:], ALU.mult)
                        v.memset(flt[:], 0.0).then_inc(S["ve2"], 1)
                    phase_end(v)
                cnt.set("ve1", b_ve1 + 2)
                cnt.set("ve2", b_ve2 + 64)

                @blk.tensor
                def _(t):
                    phase_wait(t)
                    t.wait_ge(S["ve1"], b_ve1 + 2)
                    for kk in range(4):
                        if kk >= 1:
                            t.wait_ge(S["ac1"], b_ac1 + kk)
                        t.matmul(ps_bc[:], onesf[:], sdp_row[:, kk * 512:(kk + 1) * 512],
                                 start=True, stop=True).then_inc(S["pe1"], 1)
                    for j in range(256):
                        t.wait_ge(S["q%d" % (4 + j % 2)], bq[4 + j % 2] + 16 * (j // 2 + 1))
                        if j >= 1:
                            t.wait_ge(S["ac2"], b_ac2 + j)
                        for nn in range(2):
                            mm = t.matmul(psh[j % 2][:, nn * 512:(nn + 1) * 512], h64[:],
                                          wcb[j % 2][:, nn * 512:(nn + 1) * 512],
                                          start=True, stop=True)
                            if nn == 1:
                                mm.then_inc(S["pe2"], 1)
                    phase_end(t)
                cnt.set("pe1", b_pe1 + 4)
                cnt.set("pe2", b_pe2 + 256)

                @blk.scalar
                def _(sc):
                    phase_wait(sc)
                    for kk in range(4):
                        sc.wait_ge(S["pe1"], b_pe1 + kk + 1)
                        sc.copy(bcdp[:, kk * 512:(kk + 1) * 512], ps_bc[:]).then_inc(S["ac1"], 1)
                    for j in range(256):
                        sc.wait_ge(S["pe2"], b_pe2 + j + 1)
                        if j >= 2:
                            sc.wait_ge(S["q%d" % (6 + j % 2)], bq[6 + j % 2] + 16 * (j // 2))
                        sc.copy(wob[j % 2][:], psh[j % 2][:]).then_inc(S["ac2"], 1)
                    phase_end(sc)
                cnt.set("ac1", b_ac1 + 4)
                cnt.set("ac2", b_ac2 + 256)

                @blk.gpsimd
                def _(g):
                    phase_wait(g)
                    phase_end(g)
                _blkcm.__exit__(None, None, None)
                phase_A()
            close_phase()
            close_phase()

        def phase_A():
            (b_xl, b_st, b_gw, b_gst, b_wl, b_pe1, b_pe2,
             b_ve1, b_ve2, b_ve3, b_ac1, b_ac2) = bases(
                "xl", "st", "gw", "gst", "wl", "pe1", "pe2",
                "ve1", "ve2", "ve3", "ac1", "ac2")
            bq = [cnt.get("q%d" % _i) for _i in range(22)]
            with contextlib.ExitStack() as _es:
                xt0 = _es.enter_context(nc.sbuf_tensor("xt0", [128, D], F32))
                xt1 = _es.enter_context(nc.sbuf_tensor("xt1", [128, D], F32))
                sqt = _es.enter_context(nc.sbuf_tensor("sqt", [128, D], F32))
                xn0 = _es.enter_context(nc.sbuf_tensor("xn0", [128, D], BF16))
                xn1 = _es.enter_context(nc.sbuf_tensor("xn1", [128, D], BF16))
                xnfm = _es.enter_context(nc.sbuf_tensor("xnfm", [128, 16 * 2048], BF16))
                wqb0 = _es.enter_context(nc.sbuf_tensor("wqb0", [128, 2048], BF16))
                wqb1 = _es.enter_context(nc.sbuf_tensor("wqb1", [128, 2048], BF16))
                qkt0 = _es.enter_context(nc.sbuf_tensor("qkt0", [128, 512], BF16))
                qkt1 = _es.enter_context(nc.sbuf_tensor("qkt1", [128, 512], BF16))
                qkt2 = _es.enter_context(nc.sbuf_tensor("qkt2", [128, 512], BF16))
                qkt3 = _es.enter_context(nc.sbuf_tensor("qkt3", [128, 512], BF16))
                st1a = _es.enter_context(nc.sbuf_tensor("st1a", [128, 1], F32))
                st1b = _es.enter_context(nc.sbuf_tensor("st1b", [128, 1], F32))
                st2a = _es.enter_context(nc.sbuf_tensor("st2a", [128, 1], F32))
                st2b = _es.enter_context(nc.sbuf_tensor("st2b", [128, 1], F32))
                mva = _es.enter_context(nc.sbuf_tensor("mva", [128, 4], F32))
                mvb = _es.enter_context(nc.sbuf_tensor("mvb", [128, 4], F32))
                pq0 = _es.enter_context(nc.psum_tensor("pq0", [128, 512], F32))
                pq1 = _es.enter_context(nc.psum_tensor("pq1", [128, 512], F32))
                pt0 = _es.enter_context(nc.psum_tensor("pt0", [128, 128], BF16))
                pt1 = _es.enter_context(nc.psum_tensor("pt1", [128, 128], BF16))
                blk = _es.enter_context(nc.Block("phA"))
                xt = [xt0, xt1]
                xn = [xn0, xn1]
                wqb = [wqb0, wqb1]
                qkt = [qkt0, qkt1, qkt2, qkt3]
                stat1 = [st1a, st1b]
                stat2 = [st2a, st2b]
                mv = [mva, mvb]
                pq = [pq0, pq1]
                pt = [pt0, pt1]

                NT = 16
                CT = (48, 32)
                E0 = (0, 192)
                NE = 320

                def qdst(ctg, tc):
                    tb = 0 if ctg < 48 else 1
                    ct = ctg if ctg < 48 else ctg - 48
                    if ct < 16:
                        return q_sc.ap()[ct * 128:(ct + 1) * 128,
                                         tb * 2048 + tc * 512: tb * 2048 + (tc + 1) * 512]
                    if ct < 32:
                        return k_sc.ap()[(ct - 16) * 128:(ct - 15) * 128,
                                         tb * 2048 + tc * 512: tb * 2048 + (tc + 1) * 512]
                    return v_sc.ap()[(ct - 32) * 128:(ct - 31) * 128,
                                     tc * 512:(tc + 1) * 512]

                @blk.sync
                def _(s):
                    phase_wait(s)
                    for tb in range(2):
                        for tt in range(NT):
                            u = tb * NT + tt
                            if u >= 2:
                                s.wait_ge(S["ve1"], b_ve1 + 8 * (u - 1))
                            s.dma_start(xt[u % 2][:], x_in.ap()[u * 128:(u + 1) * 128, :]
                                        ).then_inc(S["q%d" % (8 + u % 2)], 16)
                        for ei in range(CT[tb] * 4):
                            e = E0[tb] + ei
                            s.wait_ge(S["ve2"], b_ve2 + e + 1)
                            ctg, tc = divmod(e, 4)
                            s.dma_start(qdst(ctg, tc), qkt[e % 4][:]).then_inc(S["q%d" % (12 + e % 4)], 16)
                    for _k in range(4):
                        s.wait_ge(S["q%d" % (12 + _k)], bq[12 + _k] + 80 * 16)
                    phase_end(s)
                cnt.set("q8", bq[8] + 16 * 16)
                cnt.set("q9", bq[9] + 16 * 16)
                for _k in range(4):
                    cnt.set("q%d" % (12 + _k), bq[12 + _k] + 80 * 16)

                @blk.gpsimd
                def _(g):
                    phase_wait(g)
                    for ctg in range(80):
                        if ctg >= 2:
                            g.wait_ge(S["pe2"], b_pe2 + 4 * (ctg - 1))
                        g.dma_start(wqb[ctg % 2][:],
                                    wqkv_in.ap()[ctg if ctg < 48 else ctg - 48, :, :, :]
                                    .rearrange("p a b -> p (a b)")).then_inc(S["q%d" % (10 + ctg % 2)], 16)
                    phase_end(g)
                cnt.set("q10", bq[10] + 40 * 16)
                cnt.set("q11", bq[11] + 40 * 16)

                def ln_chain(v, u):
                    par = u % 2
                    v.wait_ge(S["q%d" % (8 + u % 2)], bq[8 + u % 2] + 16 * (u // 2 + 1))
                    v.reduce_sum(stat1[par][:], xt[par][:], axis=AX.X).then_inc(S["ve1"], 1)
                    v.wait_ge(S["ac1"], b_ac1 + 2 * u + 1)
                    v.wait_ge(S["ve1"], b_ve1 + 8 * u + 1)
                    m = mv[par][:, 0:1]
                    var = mv[par][:, 1:2]
                    rstd = mv[par][:, 2:3]
                    negmr = mv[par][:, 3:4]
                    v.tensor_scalar(m, stat1[par][:], 1.0 / D, None, ALU.mult).then_inc(S["ve1"], 1)
                    v.tensor_scalar(var, stat2[par][:], 1.0 / D, None, ALU.mult).then_inc(S["ve1"], 1)
                    v.wait_ge(S["ve1"], b_ve1 + 8 * u + 3)
                    v.tensor_tensor(rstd, m, m, ALU.mult).then_inc(S["ve1"], 1)
                    v.wait_ge(S["ve1"], b_ve1 + 8 * u + 4)
                    v.tensor_tensor(var, var, rstd, ALU.subtract).then_inc(S["ve1"], 1)
                    v.wait_ge(S["ac1"], b_ac1 + 2 * u + 2)
                    v.reciprocal(rstd, rstd).then_inc(S["ve1"], 1)
                    v.wait_ge(S["ve1"], b_ve1 + 8 * u + 6)
                    v.scalar_tensor_tensor(negmr, m, -1.0, rstd, ALU.mult, ALU.mult
                                           ).then_inc(S["ve1"], 1)
                    v.wait_ge(S["ve1"], b_ve1 + 8 * u + 7)
                    if u >= 2:
                        v.wait_ge(S["pe1"], b_pe1 + 16 * (u - 1))
                    v.tensor_scalar(xn[par][:], xt[par][:], m, rstd,
                                    ALU.subtract, ALU.mult).then_inc(S["ve1"], 1)

                @blk.vector
                def _(v):
                    phase_wait(v)
                    for tb in range(2):
                        ln_chain(v, tb * NT)
                        for tt in range(NT):
                            u = tb * NT + tt
                            if tt + 1 < NT:
                                ln_chain(v, u + 1)
                            for k in range(16):
                                g_idx = 16 * u + k
                                v.wait_ge(S["pe1"], b_pe1 + g_idx + 1)
                                v.tensor_copy(
                                    xnfm[:, k * 2048 + tt * 128: k * 2048 + (tt + 1) * 128],
                                    pt[k % 2][:])
                                v.memset(flt[:], 0.0).then_inc(S["ve3"], 1)
                        for ei in range(CT[tb] * 4):
                            e = E0[tb] + ei
                            v.wait_ge(S["pe2"], b_pe2 + e + 1)
                            if e >= 4:
                                v.wait_ge(S["q%d" % (12 + e % 4)], bq[12 + e % 4] + 16 * (e // 4))
                            v.tensor_copy(qkt[e % 4][:], pq[e % 2][:])
                            v.memset(flt[:], 0.0).then_inc(S["ve2"], 1)
                    phase_end(v)
                cnt.set("ve1", b_ve1 + 8 * 32)
                cnt.set("ve2", b_ve2 + NE)
                cnt.set("ve3", b_ve3 + 512)

                @blk.scalar
                def _(sc):
                    phase_wait(sc)
                    sc.wait_ge(S["ld"], cnt.get("ld"))
                    for tb in range(2):
                        for tt in range(NT):
                            u = tb * NT + tt
                            par = u % 2
                            sc.wait_ge(S["q%d" % (8 + u % 2)], bq[8 + u % 2] + 16 * (u // 2 + 1))
                            sc.activation(sqt[:], xt[par][:], AF.Square,
                                          accum_out=stat2[par][:]).then_inc(S["ac1"], 1)
                            sc.wait_ge(S["ve1"], b_ve1 + 8 * u + 5)
                            sc.activation(mv[par][:, 2:3], mv[par][:, 1:2], AF.Sqrt,
                                          bias=cst[:, CI_EPS:CI_EPS + 1]).then_inc(S["ac1"], 1)
                    phase_end(sc)
                cnt.set("ac1", b_ac1 + 64)

                @blk.tensor
                def _(t):
                    phase_wait(t)
                    t.wait_ge(S["ld"], cnt.get("ld"))
                    for tb in range(2):
                        for tt in range(NT):
                            u = tb * NT + tt
                            t.wait_ge(S["ve1"], b_ve1 + 8 * u + 8)
                            for k in range(16):
                                g_idx = 16 * u + k
                                if g_idx >= 2:
                                    t.wait_ge(S["ve3"], b_ve3 + g_idx - 1)
                                t.transpose(pt[k % 2][:], xn[u % 2][:, k * 128:(k + 1) * 128],
                                            idn[:]).then_inc(S["pe1"], 1)
                        t.wait_ge(S["ve3"], b_ve3 + 256 * (tb + 1))
                        for ei in range(CT[tb] * 4):
                            e = E0[tb] + ei
                            ctg, tc = divmod(e, 4)
                            t.wait_ge(S["q%d" % (10 + ctg % 2)], bq[10 + ctg % 2] + 16 * (ctg // 2 + 1))
                            if e >= 2:
                                t.wait_ge(S["ve2"], b_ve2 + e - 1)
                            for fc in range(16):
                                t.matmul(pq[e % 2][:],
                                         wqb[ctg % 2][:, fc * 128:(fc + 1) * 128],
                                         xnfm[:, fc * 2048 + tc * 512: fc * 2048 + (tc + 1) * 512],
                                         start=(fc == 0), stop=(fc == 15)
                                         ).then_inc(S["pe2"], 1) if fc == 15 else \
                                    t.matmul(pq[e % 2][:],
                                             wqb[ctg % 2][:, fc * 128:(fc + 1) * 128],
                                             xnfm[:, fc * 2048 + tc * 512: fc * 2048 + (tc + 1) * 512],
                                             start=(fc == 0), stop=False)
                    phase_end(t)
                cnt.set("pe1", b_pe1 + 512)
                cnt.set("pe2", b_pe2 + NE)

        def phase_BC():
            (b_xl, b_gw, b_pe1, b_pe2, b_pe3,
             b_ve1, b_ve2, b_ac1, b_ac2) = bases(
                "xl", "gw", "pe1", "pe2", "pe3", "ve1", "ve2", "ac1", "ac2")
            with contextlib.ExitStack() as _es:
                vals = _es.enter_context(nc.sbuf_tensor("vals", [128, NH * 2048], BF16))
                wo_sb = _es.enter_context(nc.sbuf_tensor("wo_sb", [128, 16 * 2048], BF16))
                csr = _es.enter_context(nc.sbuf_tensor("csr", [1, TOK], F32))
                phase_B_inner(vals, wo_sb, csr,
                              b_xl, b_gw, b_pe1, b_pe2, b_pe3,
                              b_ve1, b_ve2, b_ac1, b_ac2)
                close_phase()
                phase_C_inner(vals, wo_sb)
                close_phase()

        def phase_B_inner(vals, wo_sb, csr,
                          b_xl, b_gw, b_pe1, b_pe2, b_pe3,
                          b_ve1, b_ve2, b_ac1, b_ac2):
            bq = [cnt.get("q%d" % _i) for _i in range(22)]
            b_gst = cnt.get("gst")
            with contextlib.ExitStack() as _es:
                wpc0 = _es.enter_context(nc.sbuf_tensor("wpc0", [128, 2048], BF16))
                wpc1 = _es.enter_context(nc.sbuf_tensor("wpc1", [128, 2048], BF16))
                qh0 = _es.enter_context(nc.sbuf_tensor("qh0", [128, TFULL], BF16))
                qh1 = _es.enter_context(nc.sbuf_tensor("qh1", [128, TFULL], BF16))
                kh0 = _es.enter_context(nc.sbuf_tensor("kh0", [128, TFULL], BF16))
                kh1 = _es.enter_context(nc.sbuf_tensor("kh1", [128, TFULL], BF16))
                vh0 = _es.enter_context(nc.sbuf_tensor("vh0", [128, TOK], BF16))
                vh1 = _es.enter_context(nc.sbuf_tensor("vh1", [128, TOK], BF16))
                un0 = _es.enter_context(nc.sbuf_tensor("un0", [128, TOK], BF16))
                un1 = _es.enter_context(nc.sbuf_tensor("un1", [128, TOK], BF16))
                utmp = _es.enter_context(nc.sbuf_tensor("utmp", [128, TOK], BF16))
                lo = _es.enter_context(nc.sbuf_tensor("lo", [128, 8], F32))
                lw = _es.enter_context(nc.sbuf_tensor("lw", [128, 8], F32))
                lwb = _es.enter_context(nc.sbuf_tensor("lwb", [128, 2], BF16))
                psA0 = _es.enter_context(nc.psum_tensor("psA0", [128, 1024], F32))
                psA1 = _es.enter_context(nc.psum_tensor("psA1", [128, 1024], F32))
                csps = _es.enter_context(nc.psum_tensor("csps", [1, 2048], F32))
                blk = _es.enter_context(nc.Block("phB"))
                qh = [qh0, qh1]
                kh = [kh0, kh1]
                vh = [vh0, vh1]
                un = [un0, un1]
                psA = [psA0, psA1]
                wpc = [wpc0, wpc1]

                @blk.sync
                def _(s):
                    phase_wait(s)
                    for h in range(NH):
                        if h >= 2:
                            s.wait_ge(S["pe2"], b_pe2 + 128 * (h - 1))
                            s.wait_ge(S["ve2"], b_ve2 + 2 * (h - 1))
                        s.dma_start(qh[h % 2][:], q_sc.ap()[h * 128:(h + 1) * 128, :]
                                    ).then_inc(S["q%d" % (h % 2)], 16)
                        s.dma_start(kh[h % 2][:], k_sc.ap()[h * 128:(h + 1) * 128, :]
                                    ).then_inc(S["q%d" % (h % 2)], 16)
                        s.dma_start(vh[h % 2][:], v_sc.ap()[h * 128:(h + 1) * 128, :]
                                    ).then_inc(S["q%d" % (h % 2)], 16)
                    phase_end(s)
                cnt.set("q0", bq[0] + 8 * 48)
                cnt.set("q1", bq[1] + 8 * 48)

                @blk.gpsimd
                def _(g):
                    phase_wait(g)
                    for hc in range(16):
                        g.dma_start(wo_sb[:, hc * 2048:(hc + 1) * 2048],
                                    wo_in.ap()[hc * 128:(hc + 1) * 128, :]).then_inc(S["gw"], 16)
                    for pj in range(128):
                        win = wup_in if pj < 64 else wgate_in
                        c = pj % 64
                        if pj >= 2:
                            g.wait_ge(S["gst"], b_gst + 16 * (pj - 1))
                        g.dma_start(wpc[pj % 2][:],
                                    win.ap()[c, :, :, :].rearrange("p a b -> p (a b)")
                                    ).then_inc(S["q%d" % (16 + pj % 2)], 16)
                        if pj >= 1:
                            g.wait_ge(S["q%d" % (16 + (pj - 1) % 2)],
                                      bq[16 + (pj - 1) % 2] + 16 * ((pj - 1) // 2 + 1))
                            wino = wup_sc if pj - 1 < 64 else wgate_sc
                            g.dma_start(wino.ap()[(pj - 1) % 64, :, :],
                                        wpc[(pj - 1) % 2][:]).then_inc(S["gst"], 16)
                    g.wait_ge(S["q17"], bq[17] + 16 * 64)
                    g.dma_start(wgate_sc.ap()[63, :, :], wpc[1][:]).then_inc(S["gst"], 16)
                    g.wait_ge(S["gst"], b_gst + 128 * 16)
                    phase_end(g)
                cnt.set("gw", b_gw + 256)
                cnt.set("gst", b_gst + 128 * 16)
                cnt.set("q16", bq[16] + 64 * 16)
                cnt.set("q17", bq[17] + 64 * 16)

                @blk.tensor
                def _(t):
                    phase_wait(t)
                    for h in range(NH):
                        for qt in range(32):
                            j = 32 * h + qt
                            if qt == 0:
                                t.wait_ge(S["q%d" % (h % 2)], bq[h % 2] + 48 * (h // 2 + 1))
                                if h >= 1:
                                    t.wait_ge(S["ve2"], b_ve2 + 2 * h)
                                    t.wait_ge(S["ac2"], b_ac2 + h)
                            if j >= 1:
                                t.wait_ge(S["ac1"], b_ac1 + 4 * (j - 1) + 3)
                            for n in range(2):
                                t.matmul(psA0[:, n * 512:(n + 1) * 512],
                                         qh[h % 2][:, qt * 128:(qt + 1) * 128],
                                         kh[h % 2][:, n * 512:(n + 1) * 512],
                                         start=True, stop=True
                                         ).then_inc(S["pe2"], 1) if n == 1 else \
                                    t.matmul(psA0[:, n * 512:(n + 1) * 512],
                                             qh[h % 2][:, qt * 128:(qt + 1) * 128],
                                             kh[h % 2][:, n * 512:(n + 1) * 512],
                                             start=True, stop=True)
                            if j >= 1:
                                t.wait_ge(S["ac1"], b_ac1 + 4 * (j - 1) + 4)
                            for n in range(2):
                                t.matmul(psA1[:, n * 512:(n + 1) * 512],
                                         qh[h % 2][:, qt * 128:(qt + 1) * 128],
                                         kh[h % 2][:, 1024 + n * 512: 1024 + (n + 1) * 512],
                                         start=True, stop=True
                                         ).then_inc(S["pe2"], 1) if n == 1 else \
                                    t.matmul(psA1[:, n * 512:(n + 1) * 512],
                                             qh[h % 2][:, qt * 128:(qt + 1) * 128],
                                             kh[h % 2][:, 1024 + n * 512: 1024 + (n + 1) * 512],
                                             start=True, stop=True)
                            if qt >= 1:
                                jj = j - 1
                                t.wait_ge(S["ve1"], b_ve1 + jj + 1)
                                for kc in range(4):
                                    t.matmul(csps[:, kc * 512:(kc + 1) * 512],
                                             lwb[:, jj % 2: jj % 2 + 1],
                                             un[jj % 2][:, kc * 512:(kc + 1) * 512],
                                             start=(jj % 32 == 0), stop=(jj % 32 == 31)
                                             ).then_inc(S["pe3"], 1) if kc == 3 else \
                                        t.matmul(csps[:, kc * 512:(kc + 1) * 512],
                                                 lwb[:, jj % 2: jj % 2 + 1],
                                                 un[jj % 2][:, kc * 512:(kc + 1) * 512],
                                                 start=(jj % 32 == 0), stop=(jj % 32 == 31))
                            t.wait_ge(S["ac1"], b_ac1 + 4 * j + 1)
                            for n in range(2):
                                t.matmul(psA0[:, n * 512:(n + 1) * 512],
                                         qh[h % 2][:, qt * 128:(qt + 1) * 128],
                                         kh[h % 2][:, 2048 + n * 512: 2048 + (n + 1) * 512],
                                         start=True, stop=True
                                         ).then_inc(S["pe2"], 1) if n == 1 else \
                                    t.matmul(psA0[:, n * 512:(n + 1) * 512],
                                             qh[h % 2][:, qt * 128:(qt + 1) * 128],
                                             kh[h % 2][:, 2048 + n * 512: 2048 + (n + 1) * 512],
                                             start=True, stop=True)
                            t.wait_ge(S["ac1"], b_ac1 + 4 * j + 2)
                            for n in range(2):
                                t.matmul(psA1[:, n * 512:(n + 1) * 512],
                                         qh[h % 2][:, qt * 128:(qt + 1) * 128],
                                         kh[h % 2][:, 3072 + n * 512: 3072 + (n + 1) * 512],
                                         start=True, stop=True
                                         ).then_inc(S["pe2"], 1) if n == 1 else \
                                    t.matmul(psA1[:, n * 512:(n + 1) * 512],
                                             qh[h % 2][:, qt * 128:(qt + 1) * 128],
                                             kh[h % 2][:, 3072 + n * 512: 3072 + (n + 1) * 512],
                                             start=True, stop=True)
                        jj = 32 * h + 31
                        t.wait_ge(S["ve1"], b_ve1 + jj + 1)
                        for kc in range(4):
                            t.matmul(csps[:, kc * 512:(kc + 1) * 512],
                                     lwb[:, jj % 2: jj % 2 + 1],
                                     un[jj % 2][:, kc * 512:(kc + 1) * 512],
                                     start=(jj % 32 == 0), stop=(jj % 32 == 31)
                                     ).then_inc(S["pe3"], 1) if kc == 3 else \
                                t.matmul(csps[:, kc * 512:(kc + 1) * 512],
                                         lwb[:, jj % 2: jj % 2 + 1],
                                         un[jj % 2][:, kc * 512:(kc + 1) * 512],
                                         start=(jj % 32 == 0), stop=(jj % 32 == 31))
                        t.wait_ge(S["ac2"], b_ac2 + h + 1)
                        for kk in range(4):
                            dstp = psA[kk // 2][:, (kk % 2) * 512:(kk % 2 + 1) * 512]
                            t.matmul(dstp, onesf[:], csr[:, kk * 512:(kk + 1) * 512],
                                     start=True, stop=True).then_inc(S["pe1"], 1)
                    phase_end(t)
                cnt.set("pe1", b_pe1 + NH * 4)
                cnt.set("pe2", b_pe2 + 512 * 4)
                cnt.set("pe3", b_pe3 + 512)

                @blk.scalar
                def _(sc):
                    phase_wait(sc)
                    sc_s = cst[:, CI_S128:CI_S128 + 1]
                    for h in range(NH):
                        for qt in range(32):
                            j = 32 * h + qt
                            par = j % 2
                            sc.wait_ge(S["pe2"], b_pe2 + 4 * j + 1)
                            if j >= 2:
                                sc.wait_ge(S["pe3"], b_pe3 + j - 1)
                            sc.activation(un[par][:, 0:1024], psA0[:], AF.Exp, scale=sc_s,
                                          accum_out=lo[:, 4 * par + 0: 4 * par + 1]
                                          ).then_inc(S["ac1"], 1)
                            sc.wait_ge(S["pe2"], b_pe2 + 4 * j + 2)
                            sc.activation(un[par][:, 1024:2048], psA1[:], AF.Exp, scale=sc_s,
                                          accum_out=lo[:, 4 * par + 1: 4 * par + 2]
                                          ).then_inc(S["ac1"], 1)
                            sc.wait_ge(S["pe2"], b_pe2 + 4 * j + 3)
                            sc.activation(utmp[:, 0:1024], psA0[:], AF.Exp, scale=sc_s,
                                          accum_out=lo[:, 4 * par + 2: 4 * par + 3]
                                          ).then_inc(S["ac1"], 1)
                            sc.wait_ge(S["pe2"], b_pe2 + 4 * j + 4)
                            sc.activation(utmp[:, 1024:2048], psA1[:], AF.Exp, scale=sc_s,
                                          accum_out=lo[:, 4 * par + 3: 4 * par + 4]
                                          ).then_inc(S["ac1"], 1)
                        sc.wait_ge(S["pe3"], b_pe3 + 32 * (h + 1))
                        if h >= 1:
                            sc.wait_ge(S["pe1"], b_pe1 + 4 * (h - 1) + 4)
                        sc.copy(csr[:], csps[:])
                        sc.copy(flt[:], flt[:]).then_inc(S["ac2"], 1)
                    phase_end(sc)
                cnt.set("ac1", b_ac1 + 512 * 4)
                cnt.set("ac2", b_ac2 + NH)

                @blk.vector
                def _(v):
                    phase_wait(v)
                    vs = cnt.get("ve3")
                    for h in range(NH):
                        for qt in range(32):
                            j = 32 * h + qt
                            par = j % 2
                            v.wait_ge(S["ac1"], b_ac1 + 4 * j + 4)
                            a = lw[:, 4 * par + 0:4 * par + 1]
                            bb2 = lw[:, 4 * par + 1:4 * par + 2]
                            l = lw[:, 4 * par + 2:4 * par + 3]
                            il = lw[:, 4 * par + 3:4 * par + 4]
                            v.tensor_tensor(a, lo[:, 4 * par:4 * par + 1],
                                            lo[:, 4 * par + 1:4 * par + 2], ALU.add)
                            v.tensor_tensor(bb2, lo[:, 4 * par + 2:4 * par + 3],
                                            lo[:, 4 * par + 3:4 * par + 4], ALU.add
                                            ).then_inc(S["ve3"], 1)
                            vs += 1
                            v.wait_ge(S["ve3"], vs)
                            v.tensor_tensor(l, a, bb2, ALU.add).then_inc(S["ve3"], 1)
                            vs += 1
                            v.wait_ge(S["ve3"], vs)
                            v.reciprocal(il, l).then_inc(S["ve3"], 1)
                            vs += 1
                            v.wait_ge(S["ve3"], vs)
                            v.tensor_copy(lwb[:, par:par + 1], il)
                            v.memset(flt[:], 0.0).then_inc(S["ve1"], 1)
                        v.wait_ge(S["q%d" % (h % 2)], bq[h % 2] + 48 * (h // 2 + 1))
                        v.wait_ge(S["pe1"], b_pe1 + 4 * h + 2)
                        v.tensor_tensor(vals[:, h * 2048:h * 2048 + 1024],
                                        vh[h % 2][:, 0:1024], psA0[:], ALU.mult)
                        v.memset(flt[:], 0.0).then_inc(S["ve2"], 1)
                        v.wait_ge(S["pe1"], b_pe1 + 4 * h + 4)
                        v.tensor_tensor(vals[:, h * 2048 + 1024:(h + 1) * 2048],
                                        vh[h % 2][:, 1024:2048], psA1[:], ALU.mult)
                        v.memset(flt[:], 0.0).then_inc(S["ve2"], 1)
                    phase_end(v)
                cnt.set("ve1", b_ve1 + 512)
                cnt.set("ve2", b_ve2 + 2 * NH)
                cnt.set("ve3", cnt.get("ve3") + 512 * 3)

        def phase_C_inner(vals, wo_sb):
            (c_xl, c_st, c_gw, c_pe2, c_ve5) = bases("xl", "st", "gw", "pe2", "ve5")
            bq = [cnt.get("q%d" % _i) for _i in range(22)]
            with contextlib.ExitStack() as _es:
                xtc0 = _es.enter_context(nc.sbuf_tensor("xtc0", [128, D], F32))
                xtc1 = _es.enter_context(nc.sbuf_tensor("xtc1", [128, D], F32))
                x1sb0 = _es.enter_context(nc.sbuf_tensor("x1sb0", [128, D], F32))
                x1sb1 = _es.enter_context(nc.sbuf_tensor("x1sb1", [128, D], F32))
                po0 = _es.enter_context(nc.psum_tensor("po0", [128, 512], F32))
                po1 = _es.enter_context(nc.psum_tensor("po1", [128, 512], F32))
                po2 = _es.enter_context(nc.psum_tensor("po2", [128, 512], F32))
                po3 = _es.enter_context(nc.psum_tensor("po3", [128, 512], F32))
                blk = _es.enter_context(nc.Block("phC"))
                xtc = [xtc0, xtc1]
                x1sb = [x1sb0, x1sb1]
                po = [po0, po1, po2, po3]

                @blk.sync
                def _(s):
                    phase_wait(s)
                    for tt in range(16):
                        if tt >= 2:
                            s.wait_ge(S["ve5"], c_ve5 + 4 * (tt - 1))
                        s.dma_start(xtc[tt % 2][:], x_in.ap()[tt * 128:(tt + 1) * 128, :]
                                    ).then_inc(S["q%d" % (tt % 2)], 16)
                        if tt >= 1:
                            s.wait_ge(S["ve5"], c_ve5 + 4 * tt)
                            s.dma_start(x1_sc.ap()[(tt - 1) * 128:tt * 128, :],
                                        x1sb[(tt - 1) % 2][:]).then_inc(S["q%d" % (2 + (tt - 1) % 2)], 16)
                    s.wait_ge(S["ve5"], c_ve5 + 64)
                    s.dma_start(x1_sc.ap()[15 * 128:, :], x1sb[15 % 2][:]).then_inc(S["q3"], 16)
                    s.wait_ge(S["q2"], bq[2] + 8 * 16)
                    s.wait_ge(S["q3"], bq[3] + 8 * 16)
                    phase_end(s)
                cnt.set("q0", bq[0] + 8 * 16)
                cnt.set("q1", bq[1] + 8 * 16)
                cnt.set("q2", bq[2] + 8 * 16)
                cnt.set("q3", bq[3] + 8 * 16)

                @blk.tensor
                def _(t):
                    phase_wait(t)
                    t.wait_ge(S["gw"], c_gw)
                    for tt in range(16):
                        for od in range(4):
                            if tt >= 1:
                                t.wait_ge(S["ve5"], c_ve5 + 4 * (tt - 1) + od + 1)
                            for hc in range(16):
                                mm = t.matmul(po[od][:],
                                              vals[:, hc * 2048 + tt * 128: hc * 2048 + (tt + 1) * 128],
                                              wo_sb[:, hc * 2048 + od * 512: hc * 2048 + (od + 1) * 512],
                                              start=(hc == 0), stop=(hc == 15))
                                if hc == 15:
                                    mm.then_inc(S["pe2"], 1)
                    phase_end(t)
                cnt.set("pe2", c_pe2 + 64)

                @blk.vector
                def _(v):
                    phase_wait(v)
                    for tt in range(16):
                        v.wait_ge(S["q%d" % (tt % 2)], bq[tt % 2] + 16 * (tt // 2 + 1))
                        if tt >= 2:
                            v.wait_ge(S["q%d" % (2 + tt % 2)], bq[2 + tt % 2] + 16 * (tt // 2))
                        for od in range(4):
                            v.wait_ge(S["pe2"], c_pe2 + 4 * tt + od + 1)
                            v.tensor_add(x1sb[tt % 2][:, od * 512:(od + 1) * 512],
                                         po[od][:],
                                         xtc[tt % 2][:, od * 512:(od + 1) * 512])
                            v.memset(flt[:], 0.0).then_inc(S["ve5"], 1)
                    phase_end(v)
                cnt.set("ve5", c_ve5 + 64)

                @blk.scalar
                def _(sc):
                    phase_wait(sc)
                    phase_end(sc)

                @blk.gpsimd
                def _(g):
                    phase_wait(g)
                    phase_end(g)

        def phase_D():
            (d_xl, d_st, d_wl, d_pe1, d_pe2, d_pe3, d_pe4,
             d_ve1, d_ve2, d_ve3, d_ve4, d_ve5, d_ac1, d_ac2, d_ac3) = bases(
                "xl", "st", "wl", "pe1", "pe2", "pe3", "pe4",
                "ve1", "ve2", "ve3", "ve4", "ve5", "ac1", "ac2", "ac3")
            bq = [cnt.get("q%d" % _i) for _i in range(22)]
            with contextlib.ExitStack() as _es:
                x1t0 = _es.enter_context(nc.sbuf_tensor("x1t0", [128, D], F32))
                x1t1 = _es.enter_context(nc.sbuf_tensor("x1t1", [128, D], F32))
                sqt2 = _es.enter_context(nc.sbuf_tensor("sqt2", [128, D], F32))
                xn20 = _es.enter_context(nc.sbuf_tensor("xn20", [128, D], F32))
                xn21 = _es.enter_context(nc.sbuf_tensor("xn21", [128, D], F32))
                yfm = _es.enter_context(nc.sbuf_tensor("yfm", [128, 16 * 512], BF16))
                tfm = _es.enter_context(nc.sbuf_tensor("tfm", [128, 16 * 512], BF16))
                g2 = _es.enter_context(nc.sbuf_tensor("g2", [128, 64 * 512], BF16))
                hb0 = _es.enter_context(nc.sbuf_tensor("hb0", [128, 2048], BF16))
                hb1 = _es.enter_context(nc.sbuf_tensor("hb1", [128, 2048], BF16))
                wub0 = _es.enter_context(nc.sbuf_tensor("wub0", [128, 2048], BF16))
                wub1 = _es.enter_context(nc.sbuf_tensor("wub1", [128, 2048], BF16))
                wub2 = _es.enter_context(nc.sbuf_tensor("wub2", [128, 2048], BF16))
                wub3 = _es.enter_context(nc.sbuf_tensor("wub3", [128, 2048], BF16))
                wgb0 = _es.enter_context(nc.sbuf_tensor("wgb0", [128, 2048], BF16))
                wgb1 = _es.enter_context(nc.sbuf_tensor("wgb1", [128, 2048], BF16))
                wgb2 = _es.enter_context(nc.sbuf_tensor("wgb2", [128, 2048], BF16))
                wgb3 = _es.enter_context(nc.sbuf_tensor("wgb3", [128, 2048], BF16))
                w2b0 = _es.enter_context(nc.sbuf_tensor("w2b0", [128, 512], BF16))
                w2b1 = _es.enter_context(nc.sbuf_tensor("w2b1", [128, 512], BF16))
                w2b2 = _es.enter_context(nc.sbuf_tensor("w2b2", [128, 512], BF16))
                w2b3 = _es.enter_context(nc.sbuf_tensor("w2b3", [128, 512], BF16))
                xup0 = _es.enter_context(nc.sbuf_tensor("xup0", [128, 512], F32))
                xup1 = _es.enter_context(nc.sbuf_tensor("xup1", [128, 512], F32))
                sg0 = _es.enter_context(nc.sbuf_tensor("sg0", [128, 512], F32))
                sg1 = _es.enter_context(nc.sbuf_tensor("sg1", [128, 512], F32))
                gc0 = _es.enter_context(nc.sbuf_tensor("gc0", [128, 512], BF16))
                gc1 = _es.enter_context(nc.sbuf_tensor("gc1", [128, 512], BF16))
                xob0 = _es.enter_context(nc.sbuf_tensor("xob0", [128, 512], F32))
                xob1 = _es.enter_context(nc.sbuf_tensor("xob1", [128, 512], F32))
                osb0 = _es.enter_context(nc.sbuf_tensor("osb0", [128, 512], F32))
                osb1 = _es.enter_context(nc.sbuf_tensor("osb1", [128, 512], F32))
                st1c = _es.enter_context(nc.sbuf_tensor("st1c", [128, 1], F32))
                st1d = _es.enter_context(nc.sbuf_tensor("st1d", [128, 1], F32))
                st2c = _es.enter_context(nc.sbuf_tensor("st2c", [128, 1], F32))
                st2d = _es.enter_context(nc.sbuf_tensor("st2d", [128, 1], F32))
                mvc = _es.enter_context(nc.sbuf_tensor("mvc", [128, 4], F32))
                mvd = _es.enter_context(nc.sbuf_tensor("mvd", [128, 4], F32))
                pf0 = _es.enter_context(nc.psum_tensor("pf0", [128, 512], F32))
                pf1 = _es.enter_context(nc.psum_tensor("pf1", [128, 512], F32))
                ph = _es.enter_context(nc.psum_tensor("ph", [128, 512], F32))
                pd0 = _es.enter_context(nc.psum_tensor("pd0", [128, 512], F32))
                pd1 = _es.enter_context(nc.psum_tensor("pd1", [128, 512], F32))
                pd2 = _es.enter_context(nc.psum_tensor("pd2", [128, 512], F32))
                pd3 = _es.enter_context(nc.psum_tensor("pd3", [128, 512], F32))
                blk = _es.enter_context(nc.Block("phD"))
                x1t = [x1t0, x1t1]
                xn2 = [xn20, xn21]
                hb = [hb0, hb1]
                HBQ = [4, 5]
                wub = [wub0, wub1, wub2, wub3]
                wgb = [wgb0, wgb1, wgb2, wgb3]
                w2b = [w2b0, w2b1, w2b2, w2b3]
                xup = [xup0, xup1]
                sg = [sg0, sg1]
                gc = [gc0, gc1]
                xob = [xob0, xob1]
                XOQ = [2, 3]
                osb = [osb0, osb1]
                OSQ = [10, 11]
                stat1 = [st1c, st1d]
                stat2 = [st2c, st2d]
                mv = [mvc, mvd]
                pf = [pf0, pf1]
                pd = [pd0, pd1, pd2, pd3]

                WL_CH = 16 * 16 + 64 * 32 + 256 * 16

                def wl_h(ch, fo):
                    return d_wl + ch * WL_CH + 16 * (fo + 1)

                def wl_ug(ch, c):
                    return d_wl + ch * WL_CH + 256 + 32 * (c + 1)

                def wl_w2(ch, od, c):
                    return d_wl + ch * WL_CH + 256 + 2048 + 16 * (od * 64 + c + 1)

                XL_CH = 20 * 16

                def xl_ln(ch, tt):
                    return d_xl + ch * XL_CH + 16 * (tt + 1)

                def xl_part(ch, k):
                    return d_xl + ch * XL_CH + 64 + 16 * (k + 1)

                PE3_UP, PE3_GT, PE3_H = {}, {}, {}
                _n = 0
                for _ch in range(4):
                    for _c in range(64):
                        _cc = _ch * 64 + _c
                        _n += 1
                        PE3_UP[_cc] = _n
                        _n += 1
                        PE3_GT[_cc] = _n
                        if _c >= 1:
                            _n += 1
                            PE3_H[_cc - 1] = _n
                    _n += 1
                    PE3_H[_ch * 64 + 63] = _n

                @blk.sync
                def _(s):
                    phase_wait(s)
                    s.wait_ge(S["gst"], cnt.get("gst"))
                    for ch in range(4):
                        for tt in range(4):
                            u = ch * 4 + tt
                            if u >= 2:
                                s.wait_ge(S["ve1"], d_ve1 + 8 * (u - 1))
                            s.dma_start(x1t[u % 2][:],
                                        x1_sc.ap()[u * 128:(u + 1) * 128, :]).then_inc(S["q%d" % (u % 2)], 16)
                        for fo in range(16):
                            ff = ch * 16 + fo
                            if ff >= 2:
                                s.wait_ge(S["pe2"], d_pe2 + ff - 1)
                            s.dma_start(hb[ff % 2][:],
                                        h2048_in.ap()[fo, :, :, :].rearrange("p a b -> p (a b)")
                                        ).then_inc(S["q%d" % HBQ[ff % 2]], 16)
                        for c in range(64):
                            cc = ch * 64 + c
                            if cc >= 4:
                                s.wait_ge(S["pe3"], d_pe3 + PE3_GT[cc - 4])
                            s.dma_start(wub[cc % 4][:], wup_sc.ap()[c, :, :]).then_inc(S["q%d" % (6 + cc % 4)], 16)
                            s.dma_start(wgb[cc % 4][:], wgate_sc.ap()[c, :, :]).then_inc(S["q%d" % (6 + cc % 4)], 16)
                        for od in range(4):
                            for c in range(64):
                                dcc = (ch * 4 + od) * 64 + c
                                if dcc >= 4:
                                    s.wait_ge(S["pe4"], d_pe4 + dcc - 3)
                                s.dma_start(w2b[dcc % 4][:],
                                            w2t_sc.ap()[c * 128:(c + 1) * 128,
                                                        od * 512:(od + 1) * 512]
                                            ).then_inc(S["q%d" % (12 + dcc % 4)], 16)
                            for tt in range(4):
                                k = od * 4 + tt
                                idx = ch * 16 + k
                                if idx >= 2:
                                    s.wait_ge(S["ve5"], d_ve5 + idx - 1)
                                s.dma_start(xob[idx % 2][:],
                                            x1_sc.ap()[(ch * 4 + tt) * 128:(ch * 4 + tt + 1) * 128,
                                                       od * 512:(od + 1) * 512]
                                            ).then_inc(S["q%d" % XOQ[idx % 2]], 16)
                    phase_end(s)
                for _i, _v in ((0, 8), (1, 8), (2, 32), (3, 32), (4, 32), (5, 32),
                               (6, 64 * 2), (7, 64 * 2), (8, 64 * 2), (9, 64 * 2),
                               (12, 256), (13, 256), (14, 256), (15, 256)):
                    cnt.set("q%d" % _i, bq[_i] + _v * 16)

                def ln_chain2(v, u):
                    par = u % 2
                    v.wait_ge(S["q%d" % (u % 2)], bq[u % 2] + 16 * (u // 2 + 1))
                    v.reduce_sum(stat1[par][:], x1t[par][:], axis=AX.X).then_inc(S["ve1"], 1)
                    v.wait_ge(S["ac1"], d_ac1 + 2 * u + 1)
                    v.wait_ge(S["ve1"], d_ve1 + 8 * u + 1)
                    m = mv[par][:, 0:1]
                    var = mv[par][:, 1:2]
                    rstd = mv[par][:, 2:3]
                    negmr = mv[par][:, 3:4]
                    v.tensor_scalar(m, stat1[par][:], 1.0 / D, None, ALU.mult).then_inc(S["ve1"], 1)
                    v.tensor_scalar(var, stat2[par][:], 1.0 / D, None, ALU.mult).then_inc(S["ve1"], 1)
                    v.wait_ge(S["ve1"], d_ve1 + 8 * u + 3)
                    v.tensor_tensor(rstd, m, m, ALU.mult).then_inc(S["ve1"], 1)
                    v.wait_ge(S["ve1"], d_ve1 + 8 * u + 4)
                    v.tensor_tensor(var, var, rstd, ALU.subtract).then_inc(S["ve1"], 1)
                    v.wait_ge(S["ac1"], d_ac1 + 2 * u + 2)
                    v.reciprocal(rstd, rstd).then_inc(S["ve1"], 1)
                    v.wait_ge(S["ve1"], d_ve1 + 8 * u + 6)
                    v.scalar_tensor_tensor(negmr, m, -1.0, rstd, ALU.mult, ALU.mult
                                           ).then_inc(S["ve1"], 1)
                    v.wait_ge(S["ve1"], d_ve1 + 8 * u + 7)
                    if u >= 2:
                        v.wait_ge(S["pe1"], d_pe1 + 16 * (u - 1))
                    v.tensor_scalar(xn2[par][:], x1t[par][:], m, rstd,
                                    ALU.subtract, ALU.mult).then_inc(S["ve1"], 1)

                @blk.vector
                def _(v):
                    phase_wait(v)
                    for ch in range(4):
                        ln_chain2(v, ch * 4)
                        for tt in range(4):
                            u = ch * 4 + tt
                            if tt + 1 < 4:
                                ln_chain2(v, u + 1)
                            for k in range(16):
                                g_idx = 16 * u + k
                                v.wait_ge(S["pe1"], d_pe1 + g_idx + 1)
                                v.tensor_scalar(
                                    yfm[:, k * 512 + tt * 128: k * 512 + (tt + 1) * 128],
                                    pf[k % 2][:, 0:128], su_sb[:, k:k + 1], None, ALU.mult)
                                v.memset(flt[:], 0.0).then_inc(S["ve3"], 1)
                        for fo in range(16):
                            ff = ch * 16 + fo
                            v.wait_ge(S["pe2"], d_pe2 + ff + 1)
                            v.tensor_copy(tfm[:, fo * 512:(fo + 1) * 512], pf[ff % 2][:])
                            v.memset(flt[:], 0.0).then_inc(S["ve4"], 1)
                        for c in range(64):
                            cc = ch * 64 + c
                            v.wait_ge(S["ac3"], d_ac3 + 2 * cc + 2)
                            v.tensor_tensor(gc[cc % 2][:], xup[cc % 2][:], sg[cc % 2][:],
                                            ALU.mult)
                            v.memset(flt[:], 0.0).then_inc(S["ve2"], 1)
                            v.wait_ge(S["pe3"], d_pe3 + PE3_H[cc])
                            v.tensor_copy(g2[:, c * 512:(c + 1) * 512], ph[:])
                            v.memset(flt[:], 0.0).then_inc(S["ve2"], 1)
                        for od in range(4):
                            for tt in range(4):
                                k = od * 4 + tt
                                idx = ch * 16 + k
                                v.wait_ge(S["pe4"], d_pe4 + (ch * 4 + od) * 64 + 64)
                                v.wait_ge(S["q%d" % XOQ[idx % 2]], bq[XOQ[idx % 2]] + 16 * (idx // 2 + 1))
                                if idx >= 2:
                                    v.wait_ge(S["q%d" % OSQ[idx % 2]], bq[OSQ[idx % 2]] + 16 * (idx // 2))
                                v.tensor_add(osb[idx % 2][:], pd[tt][:], xob[idx % 2][:])
                                v.memset(flt[:], 0.0).then_inc(S["ve5"], 1)
                    phase_end(v)
                cnt.set("ve1", d_ve1 + 8 * 16)
                cnt.set("ve2", d_ve2 + 2 * 256)
                cnt.set("ve3", d_ve3 + 256)
                cnt.set("ve4", d_ve4 + 64)
                cnt.set("ve5", d_ve5 + 64)

                @blk.scalar
                def _(sc):
                    phase_wait(sc)
                    for ch in range(4):
                        for tt in range(4):
                            u = ch * 4 + tt
                            par = u % 2
                            sc.wait_ge(S["q%d" % ((ch * 4 + tt) % 2)], bq[(ch * 4 + tt) % 2] + 16 * ((ch * 4 + tt) // 2 + 1))
                            sc.activation(sqt2[:], x1t[par][:], AF.Square,
                                          accum_out=stat2[par][:]).then_inc(S["ac1"], 1)
                            sc.wait_ge(S["ve1"], d_ve1 + 8 * u + 5)
                            sc.activation(mv[par][:, 2:3], mv[par][:, 1:2], AF.Sqrt,
                                          bias=cst[:, CI_EPS:CI_EPS + 1]).then_inc(S["ac1"], 1)
                        for c in range(64):
                            cc = ch * 64 + c
                            sc.wait_ge(S["pe3"], d_pe3 + PE3_UP[cc])
                            if cc >= 2:
                                sc.wait_ge(S["ve2"], d_ve2 + 2 * cc - 3)
                            sc.activation(xup[cc % 2][:], pf0[:], AF.Copy,
                                          scale=scup[:, c:c + 1]).then_inc(S["ac3"], 1)
                            sc.wait_ge(S["pe3"], d_pe3 + PE3_GT[cc])
                            sc.activation(sg[cc % 2][:], pf1[:], AF.Silu,
                                          scale=scgt[:, c:c + 1]).then_inc(S["ac3"], 1)
                    phase_end(sc)
                cnt.set("ac1", d_ac1 + 32)
                cnt.set("ac3", d_ac3 + 2 * 256)

                @blk.tensor
                def _(t):
                    phase_wait(t)
                    for ch in range(4):
                        if ch >= 1:
                            t.wait_ge(S["ac3"], d_ac3 + 128 * ch)
                        for tt in range(4):
                            u = ch * 4 + tt
                            t.wait_ge(S["ve1"], d_ve1 + 8 * u + 8)
                            for k in range(16):
                                g_idx = 16 * u + k
                                if g_idx >= 2:
                                    t.wait_ge(S["ve3"], d_ve3 + g_idx - 1)
                                t.transpose(pf[k % 2][:, 0:128], xn2[u % 2][:, k * 128:(k + 1) * 128],
                                            idnf[:]).then_inc(S["pe1"], 1)
                        t.wait_ge(S["ve3"], d_ve3 + 64 * (ch + 1))
                        for fo in range(16):
                            ff = ch * 16 + fo
                            t.wait_ge(S["q%d" % HBQ[ff % 2]], bq[HBQ[ff % 2]] + 16 * (ff // 2 + 1))
                            if ff >= 2:
                                t.wait_ge(S["ve4"], d_ve4 + ff - 1)
                            for fi in range(16):
                                mm = t.matmul(pf[ff % 2][:],
                                              hb[ff % 2][:, fi * 128:(fi + 1) * 128],
                                              yfm[:, fi * 512:(fi + 1) * 512],
                                              start=(fi == 0), stop=(fi == 15))
                                if fi == 15:
                                    mm.then_inc(S["pe2"], 1)
                        t.wait_ge(S["ve4"], d_ve4 + 16 * (ch + 1))
                        for c in range(64):
                            cc = ch * 64 + c
                            t.wait_ge(S["q%d" % (6 + cc % 4)], bq[6 + cc % 4] + 32 * (cc // 4 + 1))
                            if cc >= 1:
                                t.wait_ge(S["ac3"], d_ac3 + 2 * (cc - 1) + 1)
                            for fc in range(16):
                                mm = t.matmul(pf0[:], wub[cc % 4][:, fc * 128:(fc + 1) * 128],
                                              tfm[:, fc * 512:(fc + 1) * 512],
                                              start=(fc == 0), stop=(fc == 15))
                                if fc == 15:
                                    mm.then_inc(S["pe3"], 1)
                            if cc >= 1:
                                t.wait_ge(S["ac3"], d_ac3 + 2 * (cc - 1) + 2)
                            for fc in range(16):
                                mm = t.matmul(pf1[:], wgb[cc % 4][:, fc * 128:(fc + 1) * 128],
                                              tfm[:, fc * 512:(fc + 1) * 512],
                                              start=(fc == 0), stop=(fc == 15))
                                if fc == 15:
                                    mm.then_inc(S["pe3"], 1)
                            if c >= 1:
                                t.wait_ge(S["ve2"], d_ve2 + 2 * (cc - 1) + 1)
                                if cc >= 2:
                                    t.wait_ge(S["ve2"], d_ve2 + 2 * (cc - 2) + 2)
                                t.matmul(ph[:], h128[:], gc[(cc - 1) % 2][:],
                                         start=True, stop=True).then_inc(S["pe3"], 1)
                        cc = ch * 64 + 63
                        t.wait_ge(S["ve2"], d_ve2 + 2 * cc + 1)
                        if cc >= 1:
                            t.wait_ge(S["ve2"], d_ve2 + 2 * (cc - 1) + 2)
                        t.matmul(ph[:], h128[:], gc[cc % 2][:],
                                 start=True, stop=True).then_inc(S["pe3"], 1)
                        for od in range(4):
                            for c in range(64):
                                dcc = (ch * 4 + od) * 64 + c
                                t.wait_ge(S["q%d" % (12 + dcc % 4)], bq[12 + dcc % 4] + 16 * (dcc // 4 + 1))
                                t.wait_ge(S["ve2"], d_ve2 + 2 * (ch * 64 + c) + 2)
                                if c == 0 and (ch * 4 + od) >= 1:
                                    t.wait_ge(S["ve5"], d_ve5 + (ch * 4 + od - 1) * 4 + 4)
                                for tt in range(4):
                                    mm = t.matmul(pd[tt][:],
                                                  g2[:, c * 512 + tt * 128: c * 512 + (tt + 1) * 128],
                                                  w2b[dcc % 4][:],
                                                  start=(c == 0), stop=(c == 63))
                                    if tt == 3:
                                        mm.then_inc(S["pe4"], 1)
                    phase_end(t)
                cnt.set("pe1", d_pe1 + 256)
                cnt.set("pe2", d_pe2 + 64)
                cnt.set("pe3", d_pe3 + 3 * 256)
                cnt.set("pe4", d_pe4 + 1024)

                @blk.gpsimd
                def _(g):
                    phase_wait(g)
                    for ch in range(4):
                        for od in range(4):
                            for tt in range(4):
                                k = od * 4 + tt
                                idx = ch * 16 + k
                                g.wait_ge(S["ve5"], d_ve5 + idx + 1)
                                g.dma_start(out.ap()[(ch * 4 + tt) * 128:(ch * 4 + tt + 1) * 128,
                                                     od * 512:(od + 1) * 512],
                                            osb[idx % 2][:]).then_inc(S["q%d" % OSQ[idx % 2]], 16)
                    for _k in range(2):
                        g.wait_ge(S["q%d" % OSQ[_k]], bq[OSQ[_k]] + 16 * 32)
                    phase_end(g)
                for _k in range(2):
                    cnt.set("q%d" % OSQ[_k], bq[OSQ[_k]] + 16 * 32)
            close_phase()

        phase_setup()
        phase_BC()
        phase_D()
    return nc



_H2048 = None
_H128 = None
_H64 = None


def _consts():
    global _H2048, _H128, _H64
    if _H2048 is None:
        _H2048 = _hadamard(2048)
        _H128 = _hadamard(128)
        _H64 = _hadamard(64)
    bf = ml_dtypes.bfloat16
    h2048 = np.ascontiguousarray(
        _H2048.reshape(16, 128, 16, 128).transpose(2, 1, 0, 3)).astype(bf)
    h128 = _H128.astype(bf)
    h64 = _H64.astype(bf)
    ident = np.eye(128, dtype=bf)
    identf = np.eye(128, dtype=np.float32)
    ones = np.ones((1, 128), np.float32)
    consts = np.zeros((1, NC_CONST), np.float32)
    consts[0, CI_S128] = S128
    consts[0, CI_EPS] = 1e-5
    return h2048, h128, h64, ident, identf, ones, consts


def kernel(L0_attn_x, L0_attn_w_qkv, L0_attn_w_o,
           L0_ffn_SU_ug, L0_ffn_SU_dp, L0_ffn_Wscale_ug, L0_ffn_Wscale_dp,
           L0_ffn_W_up, L0_ffn_W_gate, L0_ffn_W_down):
    x = np.asarray(L0_attn_x, np.float32)
    wqkv = np.asarray(L0_attn_w_qkv, np.float32)
    wo = np.ascontiguousarray(np.asarray(L0_attn_w_o, np.float32))
    su_ug = np.asarray(L0_ffn_SU_ug, np.float32).reshape(1, D)
    su_dp = np.asarray(L0_ffn_SU_dp, np.float32).reshape(1, INTER)
    wsc_ug = np.asarray(L0_ffn_Wscale_ug, np.float32).reshape(1, 2 * INTER)
    wsc_dp = np.asarray(L0_ffn_Wscale_dp, np.float32).reshape(1, D)
    wup = np.asarray(L0_ffn_W_up, np.float32)
    wgate = np.asarray(L0_ffn_W_gate, np.float32)
    wdown = np.asarray(L0_ffn_W_down, np.float32)

    wqkv_in = np.ascontiguousarray(
        wqkv.reshape(16, 128, 48, 128).transpose(2, 1, 0, 3))
    wup_in = np.ascontiguousarray(
        wup.reshape(64, 128, 16, 128).transpose(0, 3, 2, 1))
    wgate_in = np.ascontiguousarray(
        wgate.reshape(64, 128, 16, 128).transpose(0, 3, 2, 1))
    wdownT_in = np.ascontiguousarray(wdown.T)

    h2048, h128, h64, ident, identf, ones, consts = _consts()

    nc = build_nc()
    in_maps = []
    for c in range(8):
        b, p = divmod(c, 2)
        own = x[b, p * TOK:(p + 1) * TOK, :]
        other = x[b, (1 - p) * TOK:(2 - p) * TOK, :]
        x_core = np.ascontiguousarray(np.concatenate([own, other], axis=0))
        in_maps.append(dict(
            x_in=x_core, wqkv_in=wqkv_in, wo_in=wo,
            wup_in=wup_in, wgate_in=wgate_in, wdownT_in=wdownT_in,
            su_ug_in=su_ug, su_dp_in=su_dp, wsc_ug_in=wsc_ug, wsc_dp_in=wsc_dp,
            h2048_in=h2048, h128_in=h128, h64_in=h64,
            ident_in=ident, identf_in=identf, ones_in=ones, consts_in=consts,
        ))
    res = run_bass_kernel_spmd(nc, in_maps, core_ids=list(range(8)))
    outp = np.empty((4, TFULL, D), np.float32)
    for c in range(8):
        b, p = divmod(c, 2)
        outp[b, p * TOK:(p + 1) * TOK, :] = res.results[c]["out"]
    return outp

